# version 20
# baseline (speedup 1.0000x reference)
import contextlib

import numpy as np
import ml_dtypes
import concourse.bass as bass
import concourse.mybir as mybir
from concourse.bass import IndirectOffsetOnAxis
from concourse.bass_utils import run_bass_kernel_spmd

F32 = mybir.dt.float32
BF16 = mybir.dt.bfloat16
I32 = mybir.dt.int32
A = mybir.AluOpType
AF = mybir.ActivationFunctionType
AX = mybir.AxisListType

NCORE = 8
T = 8192
TPC = 1024
NT = 8
H = 2048
HC = 16
II = 1024
IC = 8
E = 8
CAP = 1024
D1 = 4096
BIGF = 100000.0
LADDER = [128.0, 64.0, 32.0, 16.0, 8.0, 4.0, 2.0, 1.0]

ENGS = ("sync", "scalar", "vector", "tensor", "gpsimd")


class Buf:
    __slots__ = ("t", "wtok", "rtoks", "name")

    def __init__(self, t, name=""):
        self.t = t
        self.name = name
        self.wtok = []
        self.rtoks = []


class Plan:
    def __init__(self):
        self.streams = {e: [] for e in ENGS}
        self.counts = {e: 0 for e in ENGS}
        self.marks = {e: {} for e in ENGS}
        self.sem_names = set(ENGS)

    def op(self, eng, fn, reads=(), writes=(), deps=(), dma=False, sem=None, tag=""):
        want = []
        for b in reads:
            want += b.wtok
        for b in writes:
            want += b.wtok
            want += b.rtoks
        want += list(deps)
        need = {}
        for (s, v) in want:
            if eng == "tensor" and s == "tensor":
                continue
            if self.marks[eng].get(s, 0) >= v:
                continue
            if need.get(s, 0) < v:
                need[s] = v
        waits = sorted(need.items())
        for s, v in waits:
            self.marks[eng][s] = v
        amt = 16 if dma else 1
        cnt_key = sem if sem is not None else eng
        if cnt_key not in self.counts:
            self.counts[cnt_key] = 0
            self.sem_names.add(cnt_key)
        self.counts[cnt_key] += amt
        tok = (cnt_key, self.counts[cnt_key])
        self.streams[eng].append((waits, fn, amt, cnt_key, tag))
        for b in writes:
            b.wtok = [tok]
            b.rtoks = []
        for b in reads:
            b.rtoks.append(tok)
        return tok

    def all_toks(self):
        return [(k, v) for k, v in self.counts.items() if v > 0]

    def emit(self, block, sems, pre_emit=None):
        def mk(eng_name):
            def body(eng):
                if pre_emit and eng_name in pre_emit:
                    pre_emit[eng_name](eng)
                for waits, fn, amt, cnt_key, tag in self.streams[eng_name]:
                    for s, v in waits:
                        eng.wait_ge(sems[s], v)
                    ins = fn(eng)
                    ins.then_inc(sems[cnt_key], amt)
            return body
        for e in ENGS:
            getattr(block, e)(mk(e))


def build_program(upto="all", taps=()):
    nc = bass.Bass("TRN2", target_bir_lowering=False)
    P = Plan()

    def din(name, shape, dt=F32):
        return nc.dram_tensor(name, list(shape), dt, kind="ExternalInput").ap()

    x_own = din("x_own", (TPC, H))
    x_full = din("x_full", (T, H))
    W1T = din("W1T", (H, D1))
    W2T = din("W2T", (D1, D1))
    W3T = din("W3T", (D1, H))
    b1m = din("b1m", (32, 128))
    b2m = din("b2m", (32, 128))
    b3r = din("b3r", (1, H))
    sWg_d = din("sWg", (H, 2 * II))
    sWd_d = din("sWd", (II, H))
    Wg_d = din("Wg_e", (H, 2 * II))
    Wd_d = din("Wd_e", (II, H))
    sNg_r = din("sNg_r", (1, H))
    sNd_r = din("sNd_r", (1, II))
    Ng_r = din("Ng_r", (1, H))
    Nd_r = din("Nd_r", (1, II))
    gnw_r = din("gnw_r", (1, H))
    Wgate = din("Wgate", (H, E))
    Wsg = din("Wsg", (H, 1))
    ident_bf = din("ident_bf", (128, 128), BF16)
    ident_f = din("ident_f", (128, 128), F32)
    ones_col = din("ones_col", (128, 1))
    ones_row = din("ones_row", (1, 128))
    selall_d = din("selall", (64, 8))
    selgt_own = din("selgt_own", (64, 8))
    selallE_d = din("selallE", (64, 8))
    selgtnegE_d = din("selgtnegE", (64, 8))
    selidE_d = din("selidE", (64, 8))
    eoff_d = din("eoff_rep", (128, 8))
    iota_d = din("iota_rm", (128, 64), I32)

    out_own = nc.dram_tensor("out_own", [TPC, H], F32, kind="ExternalOutput").ap()

    tap_aps = {}
    for nm, shape, dt in taps:
        tap_aps[nm] = nc.dram_tensor("tap_" + nm, list(shape), dt, kind="ExternalOutput").ap()

    def want_tap(nm):
        return nm in tap_aps

    def dbuf(name, shape, dt):
        return Buf(nc.dram_tensor(name, list(shape), dt).ap(), name)

    t_sWg = dbuf("t_sWg", (H, 2 * II), BF16)
    t_sWd = dbuf("t_sWd", (II, H), BF16)
    t_Wg = dbuf("t_Wg", (H, 2 * II), BF16)
    t_Wd = dbuf("t_Wd", (II, H), BF16)
    wm_bounce = dbuf("wm_bounce", (8, 1024), F32)
    wm_all = dbuf("wm_all", (64, 1024), F32)
    eo_bounce = dbuf("eo_bounce", (TPC, H), F32)
    eo_all = dbuf("eo_all", (T, H), F32)
    x1_dram = dbuf("x1_dram", (TPC, H), F32)
    disp_dram = dbuf("disp_dram", (CAP, 1), I32)

    stack = contextlib.ExitStack()
    moe_stack = contextlib.ExitStack()

    def sb(name, shape, dt=F32):
        return Buf(nc.alloc_sbuf_tensor(name, list(shape), dt), name)

    def msb(name, shape, dt=F32):
        return Buf(moe_stack.enter_context(nc.sbuf_tensor(name, list(shape), dt)), name)

    def psum(name, shape, dt=F32):
        return Buf(stack.enter_context(nc.psum_tensor(name, list(shape), dt)), name)

    sems = {}

    def dma(eng, sem, dst_ap, src_ap, reads=(), writes=(), deps=(), tag=""):
        return P.op(eng, lambda g, d=dst_ap, s=src_ap: g.dma_start(out=d, in_=s),
                    reads=reads, writes=writes, deps=deps, dma=True, sem=sem, tag=tag)

    def op(eng, fn, reads=(), writes=(), deps=(), tag=""):
        return P.op(eng, fn, reads=reads, writes=writes, deps=deps, tag=tag)

    REGS = {}

    def _pre_gpsimd(eng):
        rc = eng.alloc_register("bc_cap")
        rt = eng.alloc_register("bc_t")
        eng.reg_mov(rc, CAP - 1)
        eng.reg_mov(rt, T - 1)
        REGS["cap"] = rc
        REGS["t"] = rt

    def finish():
        op("sync", lambda g: g.nop(), deps=P.all_toks(), tag="fin")
        make_sems()
        P.emit(stack.enter_context(nc.Block()), sems, pre_emit={"gpsimd": _pre_gpsimd})
        return nc

    def make_sems():
        for nm in sorted(P.sem_names):
            if nm not in sems:
                sems[nm] = stack.enter_context(nc.semaphore("sem_" + nm))

    with stack:
        pass_placeholder = None

        qfm_a = sb("qfm_a", (128, HC * 1024), BF16)
        c_idbf = sb("c_idbf", (128, 128), BF16)
        c_idf = sb("c_idf", (128, 128), F32)
        c_ones = sb("c_ones", (128, 1), F32)
        c_onesr = sb("c_onesr", (1, 128), F32)
        c_eoff = sb("c_eoff", (128, 8), F32)
        c_iota = sb("c_iota", (128, 64), I32)
        c_selaU = sb("c_selaU", (64, 8), F32)
        c_selgt = sb("c_selgt", (64, 8), F32)
        c_selaE = sb("c_selaE", (64, 8), F32)
        c_selgE = sb("c_selgE", (64, 8), F32)
        c_selidE = sb("c_selidE", (64, 8), F32)
        wgp = sb("wgp", (128, HC * 8), F32)
        wsg_sb = sb("wsg_sb", (128, HC), F32)
        gnw_c = sb("gnw_c", (128, HC), F32)
        b1t = sb("b1t", (128, 32), F32)
        b2t = sb("b2t", (128, 32), F32)
        row_scr = sb("row_scr", (1, 128), F32)
        st = {nm: sb("st_" + nm, (128, 8), F32) for nm in
              ("mu", "rstd", "var", "rmsq", "rstdrms", "rstdr", "alph", "gate",
               "m1", "m2", "w1", "w2", "idx1f", "idx2f", "osc", "t18",
               "s1x", "s2x", "nmr8", "rmsE", "rrE", "tb1", "tb2")}
        ste = {nm: sb("ste_" + nm, (128, 8), F32) for nm in ("alph", "osc")}
        stf1 = [sb(f"stf1_{j}", (128, 8), F32) for j in range(NT)]
        stf2 = [sb(f"stf2_{j}", (128, 8), F32) for j in range(NT)]
        idx1_i = sb("idx1_i", (128, 8), I32)
        idx2_i = sb("idx2_i", (128, 8), I32)
        z8 = sb("z8", (128, 8), F32)
        trep = {nm: sb("trep_" + nm, (128, 2), F32) for nm in ("sWg", "sWd", "Wg", "Wd")}
        tmc = {nm: sb("tmc_" + nm, (128, 1), F32) for nm in ("sWg", "sWd", "Wg", "Wd")}
        rcol = {nm: sb("rc_" + nm, (128, 1), F32) for nm in
                ("s2", "atmp", "m", "mm", "rc", "s127", "csc", "nmax", "ssum",
                 "rs", "n1", "big1", "i1s")}

        pb = [psum(f"pb{i}", (128, 512), F32) for i in range(7)]
        pQ = psum("pQ", (128, 1024), BF16)

        const_bufs = (c_idbf, c_idf, c_ones, c_onesr, c_eoff, c_iota, c_selaU,
                      c_selgt, c_selaE, c_selgE, c_selidE)
        for cbuf, src in zip(const_bufs, (ident_bf, ident_f, ones_col, ones_row,
                                          eoff_d, iota_d, selall_d, selgt_own,
                                          selallE_d, selgtnegE_d, selidE_d)):
            dma("sync", "c", cbuf.t[:, :], src, writes=[cbuf])
        for k in range(HC):
            ctk = dma("sync", "c", wsg_sb.t[:, k:k + 1], Wsg[128 * k:128 * k + 128, :],
                      writes=[wsg_sb])
        for cbuf in const_bufs + (wsg_sb,):
            cbuf.wtok = [ctk]
        op("vector", lambda g: g.memset(z8.t[:, :], 0.0), writes=[z8])
        op("vector", lambda g: g.memset(st["rmsE"].t[:, :], 1.0),
           writes=[st["rmsE"]], tag="ms_rmsE")
        epsb = {}
        for enm, ev in (("e5", 1e-5), ("e6", 1e-6), ("e8", 1e-8),
                        ("invII", 1.0 / 1024.0)):
            epsb[enm] = sb("eps_" + enm, (128, 1), F32)
            op("vector", lambda g, enm=enm, ev=ev: g.memset(
                    epsb[enm].t[:, :], ev), writes=[epsb[enm]], tag=f"ms_{enm}")

        qfm_b = msb("qfm_b", (128, IC * 1024), BF16)
        xtile = [msb(f"xtile{i}", (128, H), F32) for i in range(2)]
        pool4 = [msb(f"pool4_{i}", (128, H), F32) for i in range(4)]
        gain2048 = msb("gain2048", (128, H), F32)
        gain1024 = msb("gain1024", (128, II), F32)
        zq = msb("zq", (128, 2048), F32)
        c0q = msb("c0q", (128, 2048), F32)
        ddq = msb("ddq", (128, 2048), F32)
        qtm = msb("qtm", (128, 2048), BF16)
        lnt = msb("lnt", (128, H), F32)
        wtile = [msb(f"wtile{i}", (128, 512), BF16) for i in range(8)]
        twt = [msb(f"twt{i}", (128, 2048), BF16) for i in range(2)]
        tern_rs = msb("tern_rs", (128, 32), F32)
        lgf = msb("lgf", (8, 1024), F32)
        sgr = msb("sgr", (1, 1024), F32)
        ltm = msb("ltm", (128, 64), F32)
        sgv = msb("sgv", (128, 8), F32)
        r8 = {nm: msb("r8_" + nm, (128, 8), F32) for nm in
              ("p", "eq", "cs", "c1", "p2", "eq2", "cs2", "c2", "tmp", "wm",
               "va", "po", "ia", "v1", "t1")}
        lf = msb("lf", (128, 512), F32)
        wa = msb("wa", (64, 1024), F32)
        mk_sb = msb("mk_sb", (64, 1024), F32)
        ci_sb = msb("ci_sb", (64, 1024), F32)
        inclo = msb("inclo", (8, 1024), F32)
        incle = msb("incle", (8, 1024), F32)
        mke = msb("mke", (8, 1024), F32)
        inclo_tm = msb("inclo_tm", (128, 64), F32)
        scat_tm = msb("scat_tm", (128, 64), F32)
        scat_i = msb("scat_i", (128, 64), I32)
        dsp_sb = msb("dsp_sb", (128, 8), I32)
        dfill = msb("dfill", (128, 8), I32)
        ysc = msb("ysc", (128, 512), F32)
        ysc2 = msb("ysc2", (128, 512), F32)
        scatf = mke
        wm_sb = lgf
        tern_p, tern_n = c0q, ddq

        for hh in range(2):
            dma("sync", "sgr", sgr.t[0:1, :], gnw_r[0:1, 1024 * hh:1024 * hh + 1024],
                writes=[sgr], tag=f"gnw_ld{hh}")
            for kk in range(8):
                k = 8 * hh + kk
                op("tensor", lambda g, k=k, kk=kk: g.transpose(
                        pb[6].t[:, k:k + 1], sgr.t[0:1, 128 * kk:128 * kk + 128],
                        c_idf.t[0:1, 0:1]),
                   reads=[sgr, c_idf], writes=[pb[6]],
                    tag=f"gnwT{k}")
        op("scalar", lambda g: g.activation(gnw_c.t[:, :], pb[6].t[:, 0:HC], AF.Copy),
           reads=[pb[6]], writes=[gnw_c], tag="gnw_ev")
        for k in range(HC):
            dma("sync", "lf", lf.t[:, 8 * k % 512:8 * k % 512 + 8],
                Wgate[128 * k:128 * k + 128, :], writes=[lf], tag=f"wg_ld{k}")
            op("vector", lambda g, k=k: g.tensor_scalar(
                    wgp.t[:, 8 * k:8 * k + 8], lf.t[:, 8 * k % 512:8 * k % 512 + 8],
                    gnw_c.t[:, k:k + 1], None, A.mult),
               reads=[lf, gnw_c], writes=[wgp], tag=f"wgp{k}")

        dma("sync", "lf", lf.t[0:32, 0:128], b1m, reads=[lf], writes=[lf], tag="b1_ld")
        op("tensor", lambda g: g.transpose(pb[6].t[:, 32:64], lf.t[0:32, 0:128],
                                           c_idf.t[0:32, 0:32]),
           reads=[lf, c_idf], writes=[pb[6]], tag="b1T")
        op("scalar", lambda g: g.activation(b1t.t[:, :], pb[6].t[:, 32:64], AF.Copy),
           reads=[pb[6]], writes=[b1t], tag="b1ev")
        dma("sync", "lf", lf.t[0:32, 0:128], b2m, reads=[lf], writes=[lf], tag="b2_ld")
        op("tensor", lambda g: g.transpose(pb[6].t[:, 64:96], lf.t[0:32, 0:128],
                                           c_idf.t[0:32, 0:32]),
           reads=[lf, c_idf], writes=[pb[6]], tag="b2T")
        op("scalar", lambda g: g.activation(b2t.t[:, :], pb[6].t[:, 64:96], AF.Copy),
           reads=[pb[6]], writes=[b2t], tag="b2ev")

        def replicate_row(dst, src_ap, width, tag):
            for hh in range((width + 1023) // 1024):
                w1 = min(1024, width - 1024 * hh)
                dma("sync", "sgr", sgr.t[0:1, 0:w1], src_ap[0:1, 1024 * hh:1024 * hh + w1],
                    reads=[sgr], writes=[sgr], tag=f"rr_{tag}{hh}")
                for qq in range((w1 + 511) // 512):
                    w2 = min(512, w1 - 512 * qq)
                    op("tensor", lambda g, qq=qq, w2=w2: g.matmul(
                            pb[5].t[:, 0:w2], c_onesr.t[0:1, :],
                            sgr.t[0:1, 512 * qq:512 * qq + w2], start=True, stop=True),
                       reads=[c_onesr, sgr], writes=[pb[5]], tag=f"rm_{tag}{hh}_{qq}")
                    op("scalar", lambda g, hh=hh, qq=qq, w2=w2: g.activation(
                            dst.t[:, 1024 * hh + 512 * qq:1024 * hh + 512 * qq + w2],
                            pb[5].t[:, 0:w2], AF.Copy),
                       reads=[pb[5]], writes=[dst], tag=f"re_{tag}{hh}_{qq}")

        replicate_row(gain2048, sNg_r, H, "sng")
        replicate_row(gain1024, sNd_r, II, "snd")

        qbf = qfm_b.t[:, :].bitcast(F32)
        wfb = [Buf(qbf[:, 0:2048], "wfb0"), Buf(qbf[:, 2048:4096], "wfb1")]

        def tern_stats_stream(src_ap, kchunks, name, slots, cbase=0, k0=0,
                              k1=None):
            for k in range(k0, kchunks if k1 is None else k1):
                wf = slots[k % 2]
                dma("gpsimd", f"swf_{wf.name}", wf.t[:, :],
                    src_ap[128 * k:128 * k + 128, :],
                    reads=[wf], writes=[wf], tag=f"ts1_{name}{k}")
                op("scalar", lambda g, wf=wf, k=k, cbase=cbase: g.activation(
                        pool4[0].t[:, :], wf.t[:, :], AF.Abs,
                        accum_out=tern_rs.t[:, cbase + k:cbase + k + 1]),
                   reads=[wf], writes=[pool4[0], tern_rs], tag=f"ta_{name}{k}")

        def tern_stats_reduce(kchunks, name, bank, cbase=0):
            ncc = kchunks
            op("tensor", lambda g, ncc=ncc, bank=bank, cbase=cbase: g.matmul(
                    bank.t[0:1, 0:ncc], c_ones.t[:, :],
                    tern_rs.t[:, cbase:cbase + ncc],
                    start=True, stop=True),
               reads=[c_ones, tern_rs], writes=[bank], tag=f"tsu_{name}")
            op("vector", lambda g, ncc=ncc, bank=bank: g.tensor_copy(
                    row_scr.t[0:1, 32:32 + ncc], bank.t[0:1, 0:ncc]),
               reads=[bank], writes=[row_scr], tag=f"tc_{name}")
            op("vector", lambda g, ncc=ncc: g.reduce_sum(
                    row_scr.t[0:1, 127:128], row_scr.t[0:1, 32:32 + ncc], axis=AX.X),
               reads=[row_scr], writes=[row_scr], tag=f"tr_{name}")
            inv_n = 1.0 / (kchunks * 128 * 2048)
            op("vector", lambda g, inv_n=inv_n: g.tensor_scalar(
                    row_scr.t[0:1, 126:127], row_scr.t[0:1, 127:128], inv_n, 1e-5,
                    A.mult, A.max),
               reads=[row_scr], writes=[row_scr], tag=f"tm_{name}")
            op("vector", lambda g: g.reciprocal(row_scr.t[0:1, 125:126],
                                                row_scr.t[0:1, 126:127]),
               reads=[row_scr], writes=[row_scr], tag=f"tre_{name}")
            op("vector", lambda g: g.tensor_scalar(row_scr.t[0:1, 124:125],
                                                   row_scr.t[0:1, 125:126], -1.0, None,
                                                   A.mult),
               reads=[row_scr], writes=[row_scr], tag=f"tn_{name}")
            op("tensor", lambda g, bank=bank: g.matmul(
                    bank.t[:, 16:19], c_onesr.t[0:1, :],
                    row_scr.t[0:1, 124:127], start=True, stop=True),
               reads=[c_onesr, row_scr], writes=[bank], tag=f"trp_{name}")
            op("scalar", lambda g, bank=bank: g.activation(
                    trep[name].t[:, 0:1], bank.t[:, 17:18], AF.Copy),
               reads=[bank], writes=[trep[name]], tag=f"tv1_{name}")
            op("scalar", lambda g, bank=bank: g.activation(
                    trep[name].t[:, 1:2], bank.t[:, 16:17], AF.Copy),
               reads=[bank], writes=[trep[name]], tag=f"tv2_{name}")
            op("scalar", lambda g, bank=bank: g.activation(
                    tmc[name].t[:, :], bank.t[:, 18:19], AF.Copy),
               reads=[bank], writes=[tmc[name]], tag=f"tv3_{name}")

        def tern_write(src_ap, dst_dram, kchunks, name, slots):
            ts_last = {}
            for k in range(kchunks):
                wf = slots[k % 2]
                tw = twt[k % 2]
                dma("scalar", f"wf_{wf.name}", wf.t[:, :],
                    src_ap[128 * k:128 * k + 128, :],
                    reads=[wf], writes=[wf], tag=f"tw_{name}{k}")
                op("vector", lambda g, wf=wf: g.tensor_scalar(
                        tern_p.t[:, :], wf.t[:, :], trep[name].t[:, 0:1],
                        0.5, A.mult, A.is_gt),
                   reads=[wf, trep[name]], writes=[tern_p], tag=f"tp_{name}{k}")
                op("vector", lambda g, wf=wf: g.tensor_scalar(
                        tern_n.t[:, :], wf.t[:, :], trep[name].t[:, 1:2],
                        0.5, A.mult, A.is_gt),
                   reads=[wf, trep[name]], writes=[tern_n], tag=f"tq_{name}{k}")
                op("vector", lambda g, tw=tw: g.tensor_tensor(
                        tw.t[:, :], tern_p.t[:, :], tern_n.t[:, :], A.subtract),
                   reads=[tern_p, tern_n], writes=[tw], tag=f"tt_{name}{k}")
                tk = dma("sync", f"ts{k % 2}_" + name,
                         dst_dram.t[128 * k:128 * k + 128, :], tw.t[:, :],
                         reads=[tw], deps=list(dst_dram.rtoks), tag=f"tso_{name}{k}")
                ts_last[k % 2] = tk
                dst_dram.wtok = [t for t in ts_last.values()]

        def quant_ladder(zh, c0h, ch, dh, qh, csc_buf, eng, width, tag):
            op(eng, lambda g, width=width: g.tensor_scalar(
                    c0h.t[:, 0:width], zh.t[:, 0:width], csc_buf.t[:, :], 128.5,
                    A.mult, A.add),
               reads=[zh, csc_buf], writes=[c0h], tag=tag + "_c0")
            cur = c0h
            for si, tk in enumerate(LADDER):
                op(eng, lambda g, tk=tk, cur=cur, width=width: g.tensor_scalar(
                        dh.t[:, 0:width], cur.t[:, 0:width], tk, -tk, A.is_ge, A.mult),
                   reads=[cur], writes=[dh], tag=tag + f"_d{si}")
                op(eng, lambda g, cur=cur, width=width: g.tensor_tensor(
                        ch.t[:, 0:width], cur.t[:, 0:width], dh.t[:, 0:width], A.add),
                   reads=[cur, dh], writes=[ch], tag=tag + f"_a{si}")
                cur = ch
            if eng == "vector":
                op(eng, lambda g, width=width: g.scalar_tensor_tensor(
                        qh.t[:, 0:width], c0h.t[:, 0:width], -128.0, ch.t[:, 0:width],
                        A.add, A.subtract),
                   reads=[c0h, ch], writes=[qh], tag=tag + "_q")
            else:
                op(eng, lambda g, width=width: g.tensor_scalar(
                        dh.t[:, 0:width], c0h.t[:, 0:width], -128.0, None, A.add),
                   reads=[c0h], writes=[dh], tag=tag + "_q1")
                op(eng, lambda g, width=width: g.tensor_tensor(
                        qh.t[:, 0:width], dh.t[:, 0:width], ch.t[:, 0:width],
                        A.subtract),
                   reads=[dh, ch], writes=[qh], tag=tag + "_q2")

        def quant_tile(hsrc, off0, off1, gain_buf, rrms_buf, rrms_ap, mc_name,
                       alph_st, jj, tag):
            width = off1 - off0
            op("vector", lambda g, off0=off0, off1=off1, width=width: g.tensor_tensor(
                    zq.t[:, 0:width], hsrc.t[:, off0:off1],
                    gain_buf.t[:, 0:width], A.mult),
               reads=[hsrc, gain_buf], writes=[zq], tag=f"{tag}_z")
            op("vector", lambda g, width=width: g.reduce_max(
                    rcol["m"].t[:, :], zq.t[:, 0:width], axis=AX.X,
                    apply_absolute_value=True),
               reads=[zq], writes=[rcol["m"]], tag=f"{tag}_m")
            op("vector", lambda g, rrms_ap=rrms_ap: g.tensor_scalar(
                    rcol["mm"].t[:, :], rcol["m"].t[:, :], rrms_ap, 1e-5, A.mult, A.max),
               reads=[rcol["m"], rrms_buf], writes=[rcol["mm"]], tag=f"{tag}_mm")
            op("vector", lambda g: g.reciprocal(rcol["rc"].t[:, :], rcol["mm"].t[:, :]),
               reads=[rcol["mm"]], writes=[rcol["rc"]], tag=f"{tag}_rc")
            op("vector", lambda g: g.tensor_scalar(rcol["s127"].t[:, :], rcol["rc"].t[:, :],
                                                   127.0, None, A.mult),
               reads=[rcol["rc"]], writes=[rcol["s127"]], tag=f"{tag}_s")
            op("vector", lambda g, rrms_ap=rrms_ap: g.tensor_scalar(
                    rcol["csc"].t[:, :], rcol["s127"].t[:, :], rrms_ap, None, A.mult),
               reads=[rcol["s127"], rrms_buf], writes=[rcol["csc"]], tag=f"{tag}_csc")
            op("vector", lambda g: g.tensor_scalar(rcol["atmp"].t[:, :], rcol["mm"].t[:, :],
                                                   1.0 / 127.0, None, A.mult),
               reads=[rcol["mm"]], writes=[rcol["atmp"]], tag=f"{tag}_i")
            op("vector", lambda g, jj=jj, alph_st=alph_st: g.tensor_tensor(
                    alph_st.t[:, jj:jj + 1], rcol["atmp"].t[:, :], tmc[mc_name].t[:, :],
                    A.mult),
               reads=[rcol["atmp"], tmc[mc_name]], writes=[alph_st], tag=f"{tag}_al")
            quant_ladder(zq, c0q, zq, ddq, qtm, rcol["csc"], "vector", width, tag + "_L")

        def transpose_q_to_fm(dst_fm, jj, kchunks, tag):
            for k in range(kchunks):
                sl8 = 128 * (k % 8)
                op("tensor", lambda g, k=k, sl8=sl8: g.transpose(
                        pQ.t[:, sl8:sl8 + 128], qtm.t[:, 128 * k:128 * k + 128],
                        c_idbf.t[:, :]),
                   reads=[qtm, c_idbf], writes=[pQ],
                   tag=f"{tag}_T{k}")
                if (k % 8) == 7 or k == kchunks - 1:
                    k0 = (k // 8) * 8
                    nblk = k - k0 + 1
                    dst = dst_fm.t[:, :].rearrange("p (a b) -> p a b", b=1024)[
                        :, k0:k0 + nblk, 128 * jj:128 * jj + 128]
                    op("scalar", lambda g, dst=dst, nblk=nblk: g.activation(
                            dst, pQ.t[:, 0:128 * nblk], AF.Copy),
                       reads=[pQ], writes=[dst_fm], tag=f"{tag}_ev{k0}")


        def batch_rsqrt(src_buf, eps_buf, dst_buf, tag, scale_buf=None):
            def fn(g):
                kw = {}
                if scale_buf is not None:
                    kw["scale"] = scale_buf.t[:, :]
                return g.activation(st["tb1"].t[:, :], src_buf.t[:, :], AF.Sqrt,
                                    bias=eps_buf.t[:, :], **kw)
            rd = [src_buf, eps_buf] + ([scale_buf] if scale_buf is not None else [])
            op("scalar", fn, reads=rd, writes=[st["tb1"]], tag=tag + "_sq")
            op("vector", lambda g: g.reciprocal(dst_buf.t[:, :], st["tb1"].t[:, :]),
               reads=[st["tb1"]], writes=[dst_buf], tag=tag + "_rc")

        def stats_tile(xa, jj, s1_st, s2_st, tag):
            op("scalar", lambda g, xa=xa, jj=jj, s1_st=s1_st: g.activation(
                    twt[0].t[:, :], xa.t[:, :], AF.Copy,
                    accum_out=s1_st.t[:, jj:jj + 1]),
               reads=[xa], writes=[twt[0], s1_st], tag=f"{tag}_s1")
            op("scalar", lambda g, xa=xa, jj=jj, s2_st=s2_st: g.activation(
                    twt[1].t[:, :], xa.t[:, :], AF.Square,
                    accum_out=s2_st.t[:, jj:jj + 1]),
               reads=[xa], writes=[twt[1], s2_st], tag=f"{tag}_s2")

        def batch_ln_math(s1_st, s2_st, tag, want_q=True):
            op("vector", lambda g, s1_st=s1_st: g.tensor_scalar(
                    st["mu"].t[:, :], s1_st.t[:, :], 1.0 / H, None, A.mult),
               reads=[s1_st], writes=[st["mu"]], tag=tag + "_mu")
            op("vector", lambda g, s2_st=s2_st: g.tensor_scalar(
                    st["t18"].t[:, :], s2_st.t[:, :], 1.0 / H, None, A.mult),
               reads=[s2_st], writes=[st["t18"]], tag=tag + "_t1")
            op("vector", lambda g: g.tensor_tensor(
                    st["tb1"].t[:, :], st["mu"].t[:, :], st["mu"].t[:, :], A.mult),
               reads=[st["mu"]], writes=[st["tb1"]], tag=tag + "_m2")
            op("vector", lambda g: g.tensor_tensor(
                    st["var"].t[:, :], st["t18"].t[:, :], st["tb1"].t[:, :],
                    A.subtract),
               reads=[st["t18"], st["tb1"]], writes=[st["var"]], tag=tag + "_var")
            batch_rsqrt(st["var"], epsb["e5"], st["rstd"], tag + "_rsq")
            if want_q:
                op("vector", lambda g: g.tensor_tensor(
                        st["tb1"].t[:, :], st["var"].t[:, :], st["rstd"].t[:, :],
                        A.mult),
                   reads=[st["var"], st["rstd"]], writes=[st["tb1"]], tag=tag + "_q1")
                op("vector", lambda g: g.tensor_tensor(
                        st["rmsq"].t[:, :], st["tb1"].t[:, :], st["rstd"].t[:, :],
                        A.mult),
                   reads=[st["tb1"], st["rstd"]], writes=[st["rmsq"]], tag=tag + "_q2")
                if want_q is True:
                    batch_rsqrt(st["rmsq"], epsb["e6"], st["rstdr"], tag + "_rsr")
                batch_rsqrt(st["rmsq"], epsb["e8"], st["rstdrms"], tag + "_rsm")
            op("vector", lambda g: g.tensor_tensor(
                    st["tb1"].t[:, :], st["mu"].t[:, :], st["rstd"].t[:, :], A.mult),
               reads=[st["mu"], st["rstd"]], writes=[st["tb1"]], tag=tag + "_n1")
            op("vector", lambda g: g.tensor_scalar(
                    st["nmr8"].t[:, :], st["tb1"].t[:, :], -1.0, None, A.mult),
               reads=[st["tb1"]], writes=[st["nmr8"]], tag=tag + "_n2")

        def mlp_post_a(jj, ji, tag, pbase=0):
            yb = pool4[pbase + ji]
            op("scalar", lambda g, yb=yb: g.activation(lnt.t[:, 0:II], yb.t[:, 0:II],
                                                       AF.Silu),
               reads=[yb], writes=[lnt], tag=f"{tag}_silu")
            op("vector", lambda g, yb=yb: g.tensor_tensor(
                    yb.t[:, 0:II], lnt.t[:, 0:II], yb.t[:, II:2 * II], A.mult),
               reads=[lnt, yb], writes=[yb], tag=f"{tag}_h")
            op("scalar", lambda g, yb=yb, jj=jj: g.activation(
                    ddq.t[:, 0:II], yb.t[:, 0:II], AF.Square,
                    accum_out=st["rmsE"].t[:, jj:jj + 1]),
               reads=[yb], writes=[ddq, st["rmsE"]], tag=f"{tag}_h2")

        def mlp_post_b(jj, ji, osc_st, gain_i, mc_d, qdst, tag, pbase=0):
            quant_tile(pool4[pbase + ji], 0, II, gain_i, st["rrE"],
                       st["rrE"].t[:, jj:jj + 1], mc_d, osc_st, jj, tag + "_q2")
            transpose_q_to_fm(qdst, jj, IC, f"{tag}_qT")

        def mlp_post_group(jjg, osc_st, gain_i, mc_d, qdst, tag, pbase=0):
            for ji, jj in enumerate(jjg):
                mlp_post_a(jj, ji, f"{tag}a{jj}", pbase=pbase)
            batch_rsqrt(st["rmsE"], epsb["e8"], st["rrE"], f"{tag}_rr",
                        scale_buf=epsb["invII"])
            for ji, jj in enumerate(jjg):
                mlp_post_b(jj, ji, osc_st, gain_i, mc_d, qdst, f"{tag}b{jj}",
                           pbase=pbase)




        def mlp_mm(qfm, kchunks, wt_dram, n_out, jjgrp, evac_cb, tag,
                   banks=(0, 1, 2, 3), defer_qfm=False):
            toks = []
            nblocks = n_out // 512
            for nb in range(nblocks):
                for k in range(kchunks):
                    wt = wtile[(nb * kchunks + k) % 8]
                    dma("sync", f"wt{(nb * kchunks + k) % 8}", wt.t[:, :],
                        wt_dram.t[128 * k:128 * k + 128, 512 * nb:512 * nb + 512],
                        reads=[wt_dram, wt], writes=[wt], tag=f"{tag}_w{nb}_{k}")
                    for ji, jj in enumerate(jjgrp):
                        bank = pb[banks[ji]]
                        fn = lambda g, k=k, jj=jj, wt=wt, bank=bank, qfm=qfm, \
                            kchunks=kchunks: g.matmul(
                                bank.t[:, :],
                                qfm.t[:, 1024 * k + 128 * jj:1024 * k + 128 * jj + 128],
                                wt.t[:, :],
                                start=(k == 0), stop=(k == kchunks - 1))
                        if defer_qfm:
                            t_ = op("tensor", fn, reads=[wt], writes=[bank],
                                    deps=list(qfm.wtok), tag=f"{tag}_mm{nb}_{k}_{jj}")
                            toks.append(t_)
                        else:
                            op("tensor", fn, reads=[qfm, wt], writes=[bank],
                               tag=f"{tag}_mm{nb}_{k}_{jj}")
                for ji, jj in enumerate(jjgrp):
                    evac_cb(jj, ji, 512 * nb, pb[banks[ji]])
            return toks

        def sh_mm1_evac(jj, ji, n0, pbuf):
            op("scalar", lambda g, jj=jj, ji=ji, n0=n0, pbuf=pbuf: g.activation(
                    pool4[ji].t[:, n0:n0 + 512], pbuf.t[:, :], AF.Copy,
                    scale=st["alph"].t[:, jj:jj + 1]),
               reads=[pbuf, st["alph"]], writes=[pool4[ji]], tag=f"shev{jj}_{n0}")

        for jj in range(NT):
            xa = xtile[jj % 2]
            dma("sync", f"xt{jj % 2}", xa.t[:, :], x_own[128 * jj:128 * jj + 128, :],
                reads=[xa], writes=[xa], tag=f"xsload{jj}")
            stats_tile(xa, jj, st["s1x"], st["s2x"], f"Rs{jj}")
            tern_stats_stream(sWg_d, HC, "sWg", wfb, cbase=0, k0=2 * jj,
                              k1=2 * jj + 2)
        batch_ln_math(st["s1x"], st["s2x"], "Rb")
        tern_stats_reduce(HC, "sWg", pb[5], cbase=0)
        tern_write(sWg_d, t_sWg, HC, "sWg", wfb)

        def pass2_tile(jj):
            xa = xtile[jj % 2]
            dma("sync", f"xt{jj % 2}", xa.t[:, :], x_own[128 * jj:128 * jj + 128, :],
                reads=[xa], writes=[xa], tag=f"xload{jj}")
            rstdc = st["rstd"].t[:, jj:jj + 1]
            op("scalar", lambda g, xa=xa, rstdc=rstdc, jj=jj: g.activation(
                    lnt.t[:, :], xa.t[:, :], AF.Identity,
                    bias=st["nmr8"].t[:, jj:jj + 1], scale=rstdc),
               reads=[xa, st["rstd"], st["nmr8"]], writes=[lnt], tag=f"R{jj}_ln1")

            lgb = pb[0] if jj < 4 else pb[1]
            sgb = pb[2] if jj < 4 else pb[3]
            off = 128 * (jj % 4)
            for k in range(HC):
                sl4 = 128 * (k % 4)
                op("tensor", lambda g, k=k, sl4=sl4: g.transpose(
                        pb[4].t[:, sl4:sl4 + 128], lnt.t[:, 128 * k:128 * k + 128],
                        c_idf.t[:, :]),
                   reads=[lnt, c_idf], writes=[pb[4]],
                    tag=f"R{jj}_T{k}")
                if (k % 4) == 3:
                    op("scalar", lambda g: g.activation(lf.t[:, :], pb[4].t[:, :], AF.Copy),
                       reads=[pb[4]], writes=[lf], tag=f"R{jj}_lfe{k}")
                    for kk in range(k - 3, k + 1):
                        op("tensor", lambda g, kk=kk, lgb=lgb, off=off: g.matmul(
                                lgb.t[0:8, off:off + 128],
                                wgp.t[:, 8 * kk:8 * kk + 8],
                                lf.t[:, 128 * (kk % 4):128 * (kk % 4) + 128],
                                start=(kk == 0), stop=(kk == HC - 1)),
                           reads=[wgp, lf], writes=[lgb],
                            tag=f"R{jj}_lg{kk}")
                        op("tensor", lambda g, kk=kk, sgb=sgb, off=off: g.matmul(
                                sgb.t[0:1, off:off + 128],
                                wsg_sb.t[:, kk:kk + 1],
                                lf.t[:, 128 * (kk % 4):128 * (kk % 4) + 128],
                                start=(kk == 0), stop=(kk == HC - 1)),
                           reads=[wsg_sb, lf], writes=[sgb],
                            tag=f"R{jj}_sg{kk}")

            quant_tile(lnt, 0, H, gain2048, st["rstdrms"], st["rstdrms"].t[:, jj:jj + 1],
                       "sWg", st["alph"], jj, f"R{jj}_q1")
            transpose_q_to_fm(qfm_a, jj, HC, f"R{jj}_q1T")

        stats_specs = [(sWd_d, IC, "sWd", 16), (Wg_d, HC, "Wg", 0),
                       (Wd_d, IC, "Wd", 16)]
        chunk_plan = []
        for srcp, kc, nm, cb in stats_specs:
            chunk_plan += [(srcp, kc, nm, cb, k) for k in range(kc)]
        for jj in range(4):
            pass2_tile(jj)
            for (srcp, kc, nm, cb, k) in chunk_plan[8 * jj:8 * jj + 8]:
                tern_stats_stream(srcp, kc, nm, wfb, cbase=cb, k0=k, k1=k + 1)
            if jj == 0:
                tern_stats_reduce(IC, "sWd", pb[5], cbase=16)
            elif jj == 2:
                tern_stats_reduce(HC, "Wg", pb[5], cbase=0)
            elif jj == 3:
                tern_stats_reduce(IC, "Wd", pb[5], cbase=16)
        op("vector", lambda g: g.tensor_copy(lgf.t[:, 0:512], pb[0].t[0:8, :]),
           reads=[pb[0]], writes=[lgf], tag="lg_ev0")
        op("vector", lambda g: g.tensor_copy(sgr.t[0:1, 0:512], pb[2].t[0:1, :]),
           reads=[pb[2]], writes=[sgr], tag="sg_ev0")
        mm1_toks = mlp_mm(qfm_a, HC, t_sWg, 2 * II, [0, 1, 2, 3], sh_mm1_evac,
                          "shmm1g0", banks=(0, 2, 5, 6), defer_qfm=True)
        for _w in wfb:
            qfm_b.rtoks += _w.rtoks + _w.wtok
        mlp_post_group([0, 1, 2, 3], st["osc"], gain1024, "sWd", qfm_b, "shp0")
        if want_tap("h0"):
            dma("sync", "tpG", tap_aps["h0"], pool4[0].t[:, 0:II],
                reads=[pool4[0]], tag="tp_h0")
        if want_tap("rrE0"):
            dma("sync", "tpH", tap_aps["rrE0"], st["rrE"].t[:, :],
                reads=[st["rrE"]], tag="tp_rrE0")
        if want_tap("y0"):
            dma("sync", "tpF", tap_aps["y0"], pool4[0].t[:, :], reads=[pool4[0]],
                tag="tp_y0")
        for jj in range(4, NT):
            pass2_tile(jj)
        op("vector", lambda g: g.tensor_copy(lgf.t[:, 512:1024], pb[1].t[0:8, :]),
           reads=[pb[1]], writes=[lgf], tag="lg_ev1")
        op("vector", lambda g: g.tensor_copy(sgr.t[0:1, 512:1024], pb[3].t[0:1, :]),
           reads=[pb[3]], writes=[sgr], tag="sg_ev1")
        mm1_toks += mlp_mm(qfm_a, HC, t_sWg, 2 * II, [4, 5, 6, 7], sh_mm1_evac,
                           "shmm1g4", banks=(0, 1, 2, 3), defer_qfm=True)
        qfm_a.rtoks += mm1_toks
        mlp_post_group([4, 5, 6, 7], st["osc"], gain1024, "sWd", qfm_b, "shp4")

        for j in range(NT):
            op("tensor", lambda g, j=j: g.transpose(
                    pb[4].t[:, 8 * j:8 * j + 8], lgf.t[0:8, 128 * j:128 * j + 128],
                    c_idf.t[0:8, 0:8]),
               reads=[lgf, c_idf], writes=[pb[4]],
                tag=f"lgT{j}")
        op("vector", lambda g: g.tensor_copy(ltm.t[:, :], pb[4].t[:, 0:64]),
           reads=[pb[4]], writes=[ltm], tag="ltm_ev")
        for j in range(NT):
            op("tensor", lambda g, j=j: g.transpose(
                    pb[4].t[:, j:j + 1], sgr.t[0:1, 128 * j:128 * j + 128],
                    c_idf.t[0:1, 0:1]),
               reads=[sgr, c_idf], writes=[pb[4]],
                tag=f"sgT{j}")
        op("vector", lambda g: g.tensor_copy(sgv.t[:, :], pb[4].t[:, 0:8]),
           reads=[pb[4]], writes=[sgv], tag="sgv_ev")

        for jj in range(NT):
            rrc = st["rstdr"].t[:, jj:jj + 1]
            op("vector", lambda g, jj=jj, rrc=rrc: g.tensor_scalar(
                    r8["p"].t[:, :], ltm.t[:, 8 * jj:8 * jj + 8], rrc, None, A.mult),
               reads=[ltm, st["rstdr"]], writes=[r8["p"]], tag=f"rt{jj}_l")
            op("vector", lambda g: g.reduce_max(rcol["nmax"].t[:, :], r8["p"].t[:, :],
                                                axis=AX.X, negate=True),
               reads=[r8["p"]], writes=[rcol["nmax"]], tag=f"rt{jj}_nm")
            op("scalar", lambda g: g.activation(r8["eq"].t[:, :], r8["p"].t[:, :], AF.Exp,
                                                bias=rcol["nmax"].t[:, :],
                                                accum_out=rcol["ssum"].t[:, :]),
               reads=[r8["p"], rcol["nmax"]], writes=[r8["eq"], rcol["ssum"]],
               tag=f"rt{jj}_exp")
            op("vector", lambda g: g.reciprocal(rcol["rs"].t[:, :], rcol["ssum"].t[:, :]),
               reads=[rcol["ssum"]], writes=[rcol["rs"]], tag=f"rt{jj}_rs")
            op("vector", lambda g: g.tensor_scalar(r8["p"].t[:, :], r8["eq"].t[:, :],
                                                   rcol["rs"].t[:, :], None, A.mult),
               reads=[r8["eq"], rcol["rs"]], writes=[r8["p"]], tag=f"rt{jj}_p")
            op("vector", lambda g, jj=jj: g.reduce_max(st["m1"].t[:, jj:jj + 1],
                                                       r8["p"].t[:, :], axis=AX.X),
               reads=[r8["p"]], writes=[st["m1"]], tag=f"rt{jj}_m1")
            m1c = st["m1"].t[:, jj:jj + 1]
            op("vector", lambda g, m1c=m1c: g.tensor_scalar(
                    r8["eq"].t[:, :], r8["p"].t[:, :], m1c, None, A.is_ge),
               reads=[r8["p"], st["m1"]], writes=[r8["eq"]], tag=f"rt{jj}_eq")
            op("vector", lambda g: g.tensor_tensor_scan(
                    r8["cs"].t[:, :], r8["eq"].t[:, :], z8.t[:, :], 0.0, A.add, A.add),
               reads=[r8["eq"], z8], writes=[r8["cs"]], tag=f"rt{jj}_cs")
            op("vector", lambda g: g.tensor_scalar(r8["c1"].t[:, :], r8["cs"].t[:, :],
                                                   1.0, None, A.is_le),
               reads=[r8["cs"]], writes=[r8["c1"]], tag=f"rt{jj}_c1")
            op("vector", lambda g, jj=jj: g.tensor_tensor(
                    stf1[jj].t[:, :], r8["eq"].t[:, :], r8["c1"].t[:, :], A.mult),
               reads=[r8["eq"], r8["c1"]], writes=[stf1[jj]], tag=f"rt{jj}_f1")
            op("vector", lambda g, jj=jj: g.scalar_tensor_tensor(
                    r8["p2"].t[:, :], stf1[jj].t[:, :], -1e30, r8["p"].t[:, :],
                    A.mult, A.add),
               reads=[stf1[jj], r8["p"]], writes=[r8["p2"]], tag=f"rt{jj}_p2")
            op("vector", lambda g, jj=jj: g.reduce_max(st["m2"].t[:, jj:jj + 1],
                                                       r8["p2"].t[:, :], axis=AX.X),
               reads=[r8["p2"]], writes=[st["m2"]], tag=f"rt{jj}_m2")
            m2c = st["m2"].t[:, jj:jj + 1]
            op("vector", lambda g, m2c=m2c: g.tensor_scalar(
                    r8["eq2"].t[:, :], r8["p2"].t[:, :], m2c, None, A.is_ge),
               reads=[r8["p2"], st["m2"]], writes=[r8["eq2"]], tag=f"rt{jj}_eq2")
            op("vector", lambda g: g.tensor_tensor_scan(
                    r8["cs2"].t[:, :], r8["eq2"].t[:, :], z8.t[:, :], 0.0, A.add, A.add),
               reads=[r8["eq2"], z8], writes=[r8["cs2"]], tag=f"rt{jj}_cs2")
            op("vector", lambda g: g.tensor_scalar(r8["c2"].t[:, :], r8["cs2"].t[:, :],
                                                   1.0, None, A.is_le),
               reads=[r8["cs2"]], writes=[r8["c2"]], tag=f"rt{jj}_c2")
            op("vector", lambda g, jj=jj: g.tensor_tensor(
                    stf2[jj].t[:, :], r8["eq2"].t[:, :], r8["c2"].t[:, :], A.mult),
               reads=[r8["eq2"], r8["c2"]], writes=[stf2[jj]], tag=f"rt{jj}_f2")
            op("vector", lambda g, jj=jj, m1c=m1c: g.tensor_scalar(
                    r8["tmp"].t[:, :], stf1[jj].t[:, :], m1c, None, A.mult),
               reads=[stf1[jj], st["m1"]], writes=[r8["tmp"]], tag=f"rt{jj}_wm1")
            op("vector", lambda g, jj=jj, m2c=m2c: g.scalar_tensor_tensor(
                    r8["wm"].t[:, :], stf2[jj].t[:, :], m2c, r8["tmp"].t[:, :],
                    A.mult, A.add),
               reads=[stf2[jj], st["m2"], r8["tmp"]], writes=[r8["wm"]], tag=f"rt{jj}_wm")
            op("tensor", lambda g, jj=jj: g.transpose(
                    pb[4].t[0:8, 128 * (jj % 4):128 * (jj % 4) + 128],
                    r8["wm"].t[:, :], c_idf.t[:, :]),
               reads=[r8["wm"], c_idf], writes=[pb[4]],
                tag=f"rt{jj}_wmT")
            if (jj % 4) == 3:
                j0 = jj - 3
                op("vector", lambda g, j0=j0: g.tensor_copy(
                        wm_sb.t[0:8, 128 * j0:128 * j0 + 512], pb[4].t[0:8, 0:512]),
                   reads=[pb[4]], writes=[wm_sb], tag=f"rt{jj}_wmev")
        op("scalar", lambda g: g.activation(st["gate"].t[:, :], sgv.t[:, :], AF.Sigmoid),
           reads=[sgv], writes=[st["gate"]], tag="gates")

        dma("gpsimd", "wm", wm_bounce.t, wm_sb.t[0:8, :], reads=[wm_sb], writes=[wm_bounce],
            tag="wm_st")
        P.op("gpsimd", lambda g: g.collective_compute(
                "AllGather", A.bypass, replica_groups=[list(range(NCORE))],
                ins=[wm_bounce.t.opt()], outs=[wm_all.t.opt()]),
             reads=[wm_bounce], writes=[wm_all], sem="ag1", tag="AG1")

        if want_tap("wm_all"):
            dma("gpsimd", "tp1", tap_aps["wm_all"], wm_all.t, reads=[wm_all], tag="tp_wma")
        if want_tap("gate"):
            dma("sync", "tp2", tap_aps["gate"], st["gate"].t[:, :], reads=[st["gate"]],
                tag="tp_gate")
        if want_tap("alph"):
            dma("sync", "tp3", tap_aps["alph"], st["alph"].t[:, :], reads=[st["alph"]],
                tag="tp_alph")
        if want_tap("q1fm"):
            dma("sync", "tp4", tap_aps["q1fm"], qfm_a.t[:, :], reads=[qfm_a], tag="tp_q1fm")

        tern_write(sWd_d, t_sWd, IC, "sWd", xtile)
        tern_write(Wg_d, t_Wg, HC, "Wg", xtile)
        tern_write(Wd_d, t_Wd, IC, "Wd", xtile)

        x1a_last = {}

        def sh_mm2_evac(jj, ji, n0, pbuf):
            acc = pool4[ji]
            op("scalar", lambda g, jj=jj, n0=n0, pbuf=pbuf, acc=acc: g.activation(
                    acc.t[:, n0:n0 + 512], pbuf.t[:, :], AF.Copy,
                    scale=st["osc"].t[:, jj:jj + 1]),
               reads=[pbuf, st["osc"]], writes=[acc], tag=f"s2e{jj}_{n0}")
            if n0 == 1024 and ji < 2:
                xs = xtile[jj % 2]
                dma("sync", f"xt{jj % 2}", xs.t[:, :],
                    x_own[128 * jj:128 * jj + 128, :],
                    reads=[xs], writes=[xs], tag=f"s2x{jj}")
            if n0 == 1536:
                xs = xtile[jj % 2]
                if ji >= 2:
                    dma("sync", f"xt{jj % 2}", xs.t[:, :],
                        x_own[128 * jj:128 * jj + 128, :],
                        reads=[xs], writes=[xs], tag=f"s2x{jj}")
                op("vector", lambda g, xs=xs, acc=acc: g.tensor_tensor(
                        acc.t[:, :], acc.t[:, :], xs.t[:, :], A.add),
                   reads=[xs, acc], writes=[acc], tag=f"s2a{jj}")
                tk = dma("sync", f"x1a{ji}", x1_dram.t[128 * jj:128 * jj + 128, :],
                         acc.t[:, :], reads=[acc], deps=list(x1_dram.rtoks),
                         tag=f"s2w{jj}")
                x1a_last[ji] = tk
                x1_dram.wtok = [t for t in x1a_last.values()]

        op("vector", lambda g: g.tensor_tensor(st["osc"].t[:, :], st["osc"].t[:, :],
                                               st["gate"].t[:, :], A.mult),
           reads=[st["osc"], st["gate"]], writes=[st["osc"]], tag="oscg")

        for grp in (0, 4):
            jjg = list(range(grp, grp + 4))
            mlp_mm(qfm_b, IC, t_sWd, H, jjg, sh_mm2_evac, f"shmm2g{grp}")

        dma("sync", "wa", wa.t[:, :], wm_all.t, reads=[wm_all], writes=[wa], tag="wa_ld")
        op("vector", lambda g: g.tensor_scalar(mk_sb.t[:, :], wa.t[:, :], 0.0, None,
                                               A.is_gt),
           reads=[wa], writes=[mk_sb], tag="mk")
        op("vector", lambda g: g.memset(wa.t[:, :], 0.0), reads=[wa], writes=[wa],
           tag="waz")
        op("vector", lambda g: g.tensor_tensor_scan(
                ci_sb.t[:, :], mk_sb.t[:, :], wa.t[:, :], 0.0, A.add, A.add),
           reads=[mk_sb, wa], writes=[ci_sb], tag="scan")
        for hh in range(2):
            sl = slice(512 * hh, 512 * hh + 512)
            bank = pb[0] if hh == 0 else pb[1]
            op("tensor", lambda g, sl=sl, bank=bank: g.matmul(
                    bank.t[0:8, :], c_selaU.t[0:64, :], ci_sb.t[:, sl],
                    start=True, stop=False),
               reads=[c_selaU, ci_sb], writes=[bank], tag=f"own1_{hh}")
            op("tensor", lambda g, sl=sl, bank=bank: g.matmul(
                    bank.t[0:8, :], c_selgt.t[0:64, :], mk_sb.t[:, sl],
                    start=False, stop=True),
               reads=[c_selgt, mk_sb], writes=[bank], tag=f"own2_{hh}")
            bank2 = pb[2] if hh == 0 else pb[3]
            op("tensor", lambda g, sl=sl, bank2=bank2: g.matmul(
                    bank2.t[0:8, :], c_selaE.t[0:64, :], ci_sb.t[:, sl],
                    start=True, stop=False),
               reads=[c_selaE, ci_sb], writes=[bank2], tag=f"exp1_{hh}")
            op("tensor", lambda g, sl=sl, bank2=bank2: g.matmul(
                    bank2.t[0:8, :], c_selgE.t[0:64, :], mk_sb.t[:, sl],
                    start=False, stop=True),
               reads=[c_selgE, mk_sb], writes=[bank2], tag=f"exp2_{hh}")
            bank3 = pb[5] if hh == 0 else pb[6]
            op("tensor", lambda g, sl=sl, bank3=bank3: g.matmul(
                    bank3.t[0:8, :], c_selidE.t[0:64, :], mk_sb.t[:, sl],
                    start=True, stop=True),
               reads=[c_selidE, mk_sb], writes=[bank3], tag=f"mke_{hh}")
        op("vector", lambda g: g.tensor_copy(inclo.t[:, 0:512], pb[0].t[0:8, :]),
           reads=[pb[0]], writes=[inclo], tag="io0")
        op("vector", lambda g: g.tensor_copy(inclo.t[:, 512:1024], pb[1].t[0:8, :]),
           reads=[pb[1]], writes=[inclo], tag="io1")
        op("vector", lambda g: g.tensor_copy(incle.t[:, 0:512], pb[2].t[0:8, :]),
           reads=[pb[2]], writes=[incle], tag="ie0")
        op("vector", lambda g: g.tensor_copy(incle.t[:, 512:1024], pb[3].t[0:8, :]),
           reads=[pb[3]], writes=[incle], tag="ie1")
        op("vector", lambda g: g.tensor_copy(mke.t[:, 0:512], pb[5].t[0:8, :]),
           reads=[pb[5]], writes=[mke], tag="me0")
        op("vector", lambda g: g.tensor_copy(mke.t[:, 512:1024], pb[6].t[0:8, :]),
           reads=[pb[6]], writes=[mke], tag="me1")
        op("vector", lambda g: g.tensor_scalar(wa.t[0:8, 0:1024], incle.t[:, :], float(CAP),
                                               None, A.is_le),
           reads=[incle], writes=[wa], tag="vE1")
        op("vector", lambda g: g.tensor_tensor(wa.t[0:8, 0:1024], wa.t[0:8, 0:1024], mke.t[:, :],
                                               A.mult),
           reads=[wa, mke], writes=[wa], tag="vE2")
        op("vector", lambda g: g.tensor_scalar(incle.t[:, :], incle.t[:, :], -1.0, None,
                                               A.add),
           reads=[incle], writes=[incle], tag="posE")
        op("vector", lambda g: g.tensor_tensor(scatf.t[:, :], incle.t[:, :],
                                               wa.t[0:8, 0:1024], A.mult),
           reads=[incle, wa], writes=[scatf], tag="posEm")
        op("vector", lambda g: g.tensor_scalar(wa.t[0:8, 0:1024], wa.t[0:8, 0:1024], -BIGF, BIGF,
                                               A.mult, A.add),
           reads=[wa], writes=[wa], tag="invE")
        op("vector", lambda g: g.tensor_tensor(scatf.t[:, :], scatf.t[:, :],
                                               wa.t[0:8, 0:1024], A.add),
           reads=[scatf, wa], writes=[scatf], tag="scatf")
        for j in range(8):
            op("tensor", lambda g, j=j: g.transpose(
                    pb[4].t[:, 8 * j:8 * j + 8], scatf.t[0:8, 128 * j:128 * j + 128],
                    c_idf.t[0:8, 0:8]),
               reads=[scatf, c_idf], writes=[pb[4]],
                tag=f"scT{j}")
        op("vector", lambda g: g.tensor_copy(scat_tm.t[:, :], pb[4].t[:, 0:64]),
           reads=[pb[4]], writes=[scat_tm], tag="sc_ev")
        op("vector", lambda g: g.tensor_copy(scat_i.t[:, :], scat_tm.t[:, :]),
           reads=[scat_tm], writes=[scat_i], tag="sc_i32")
        for j in range(8):
            op("tensor", lambda g, j=j: g.transpose(
                    pb[4].t[:, 8 * j:8 * j + 8], inclo.t[0:8, 128 * j:128 * j + 128],
                    c_idf.t[0:8, 0:8]),
               reads=[inclo, c_idf], writes=[pb[4]],
                tag=f"ioT{j}")
        op("vector", lambda g: g.tensor_copy(inclo_tm.t[:, :], pb[4].t[:, 0:64]),
           reads=[pb[4]], writes=[inclo_tm], tag="io_ev")

        if want_tap("scat_tm"):
            dma("sync", "tp5", tap_aps["scat_tm"], scat_tm.t[:, :], reads=[scat_tm], tag="tp_sc")
        if want_tap("inclo_tm"):
            dma("sync", "tp6", tap_aps["inclo_tm"], inclo_tm.t[:, :], reads=[inclo_tm],
                tag="tp_io")

        op("vector", lambda g: g.memset(dfill.t[:, :], T), writes=[dfill], tag="dfill")
        dma("gpsimd", "dd", disp_dram.t.rearrange("(p c) one -> p (c one)", p=128),
            dfill.t[:, :], reads=[dfill], writes=[disp_dram], tag="disp_pre")
        pre_tok = disp_dram.wtok
        last_tok = None
        for rr in range(8):
            for j in range(8):
                col = 8 * j + rr
                ic = 8 * rr + j
                last_tok = P.op("gpsimd", lambda g, col=col, ic=ic: g.indirect_dma_start(
                        disp_dram.t,
                        IndirectOffsetOnAxis(ap=scat_i.t[:, col:col + 1], axis=0),
                        c_iota.t[:, ic:ic + 1], None,
                        bounds_check=REGS["cap"], oob_is_err=False),
                     reads=[scat_i, c_iota], deps=pre_tok, dma=True, sem="dd",
                     tag=f"dsp{rr}_{j}")
        disp_dram.wtok = [last_tok]
        for g8 in range(8):
            tk = dma("sync", "dsp", dsp_sb.t[:, g8:g8 + 1],
                     disp_dram.t[128 * g8:128 * g8 + 128, :],
                     reads=[disp_dram], tag=f"dspld{g8}")
            dsp_sb.wtok = [tk]

        if want_tap("disp"):
            dma("sync", "tp11", tap_aps["disp"], dsp_sb.t[:, :], reads=[dsp_sb], tag="tp_dsp")

        if upto == "route":
            return finish()



        if want_tap("x_shared"):
            dma("sync", "tp12", tap_aps["x_shared"], x1_dram.t, reads=[x1_dram], tag="tp_xs")
        if want_tap("q2fm"):
            dma("sync", "tpD", tap_aps["q2fm"], qfm_b.t[:, :], reads=[qfm_b],
                tag="tp_q2fm")
        if want_tap("osc8"):
            dma("sync", "tpE", tap_aps["osc8"], st["osc"].t[:, :],
                reads=[st["osc"]], tag="tp_osc8")
        if upto == "shared":
            return finish()

        replicate_row(gain2048, Ng_r, H, "ng")
        replicate_row(gain1024, Nd_r, II, "nd")

        def ex_mm1_evac(jj, ji, n0, pbuf):
            op("scalar", lambda g, ji=ji, jj=jj, n0=n0, pbuf=pbuf: g.activation(
                    pool4[ji].t[:, n0:n0 + 512], pbuf.t[:, :], AF.Copy,
                    scale=ste["alph"].t[:, jj:jj + 1]),
               reads=[pbuf, ste["alph"]], writes=[pool4[ji]], tag=f"exev{jj}_{n0}")

        eo_last = {}

        def ex_mm2_evac(jj, ji, n0, pbuf):
            acc = pool4[ji]
            op("scalar", lambda g, jj=jj, n0=n0, pbuf=pbuf, acc=acc: g.activation(
                    acc.t[:, n0:n0 + 512], pbuf.t[:, :], AF.Copy,
                    scale=ste["osc"].t[:, jj:jj + 1]),
               reads=[pbuf, ste["osc"]], writes=[acc], tag=f"e2e{jj}_{n0}")
            if n0 == 1536:
                tk = dma("sync", f"eo{ji}", eo_bounce.t[128 * jj:128 * jj + 128, :],
                         acc.t[:, :], reads=[acc], deps=list(eo_bounce.rtoks),
                         tag=f"e2s{jj}")
                eo_last[ji] = tk
                eo_bounce.wtok = [t for t in eo_last.values()]

        exA_toks = []
        exB_toks = []

        def ex_S1(gi):
            base = 2 * (gi % 2)
            for ji in range(2):
                jj = 2 * gi + ji
                op("vector", lambda g, ji=ji, base=base: g.memset(
                        pool4[base + ji].t[:, :], 0.0),
                   writes=[pool4[base + ji]], tag=f"ex{gi}_ms{ji}")
                P.op("gpsimd", lambda g, ji=ji, jj=jj, base=base: g.indirect_dma_start(
                        pool4[base + ji].t[:, :], None,
                        x_full, IndirectOffsetOnAxis(ap=dsp_sb.t[:, jj:jj + 1], axis=0),
                        bounds_check=REGS["t"], oob_is_err=False),
                     reads=[dsp_sb], writes=[pool4[base + ji]], dma=True,
                     sem=f"g{base + ji}", tag=f"ex{gi}_g{jj}")

        def ex_S2(gi):
            base = 2 * (gi % 2)
            for ji in range(2):
                jj = 2 * gi + ji
                stats_tile(pool4[base + ji], jj, st["s1x"], st["s2x"], f"exs{jj}")
            batch_ln_math(st["s1x"], st["s2x"], f"exb{gi}", want_q="rms")
            for ji in range(2):
                jj = 2 * gi + ji
                xg = pool4[base + ji]
                rstdc = st["rstd"].t[:, jj:jj + 1]
                op("scalar", lambda g, xg=xg, rstdc=rstdc, jj=jj: g.activation(
                        lnt.t[:, :], xg.t[:, :], AF.Identity,
                        bias=st["nmr8"].t[:, jj:jj + 1], scale=rstdc),
                   reads=[xg, st["rstd"], st["nmr8"]], writes=[lnt], tag=f"exl{jj}_ln")
                quant_tile(lnt, 0, H, gain2048, st["rstdrms"],
                           st["rstdrms"].t[:, jj:jj + 1], "Wg",
                           ste["alph"], jj, f"exq1_{jj}")
                transpose_q_to_fm(qfm_a, jj, HC, f"exq1T{jj}")

        def ex_mk_evac1(base):
            def cb(jj, ji, n0, pbuf):
                op("scalar", lambda g, ji=ji, jj=jj, n0=n0, pbuf=pbuf: g.activation(
                        pool4[base + ji].t[:, n0:n0 + 512], pbuf.t[:, :], AF.Copy,
                        scale=ste["alph"].t[:, jj:jj + 1]),
                   reads=[pbuf, ste["alph"]], writes=[pool4[base + ji]],
                   tag=f"exev{jj}_{n0}")
            return cb

        eo_last = {}

        def ex_mk_evac2(base):
            def cb(jj, ji, n0, pbuf):
                acc = pool4[base + ji]
                op("scalar", lambda g, jj=jj, n0=n0, pbuf=pbuf, acc=acc: g.activation(
                        acc.t[:, n0:n0 + 512], pbuf.t[:, :], AF.Copy,
                        scale=ste["osc"].t[:, jj:jj + 1]),
                   reads=[pbuf, ste["osc"]], writes=[acc], tag=f"e2e{jj}_{n0}")
                if n0 == 1536:
                    tk = dma("sync", f"eo{base + ji}",
                             eo_bounce.t[128 * jj:128 * jj + 128, :],
                             acc.t[:, :], reads=[acc], deps=list(eo_bounce.rtoks),
                             tag=f"e2s{jj}")
                    eo_last[base + ji] = tk
                    eo_bounce.wtok = [t for t in eo_last.values()]
            return cb

        def ex_S3(gi):
            base = 2 * (gi % 2)
            jjg = [2 * gi, 2 * gi + 1]
            exA_toks.extend(mlp_mm(qfm_a, HC, t_Wg, 2 * II, jjg, ex_mk_evac1(base),
                                   f"exmm1g{gi}", banks=(0, 1), defer_qfm=True))

        def ex_S4(gi):
            base = 2 * (gi % 2)
            jjg = [2 * gi, 2 * gi + 1]
            mlp_post_group(jjg, ste["osc"], gain1024, "Wd", qfm_b, f"exp{gi}",
                           pbase=base)

        def ex_S5(gi):
            base = 2 * (gi % 2)
            jjg = [2 * gi, 2 * gi + 1]
            exB_toks.extend(mlp_mm(qfm_b, IC, t_Wd, H, jjg, ex_mk_evac2(base),
                                   f"exmm2g{gi}", banks=(4, 5), defer_qfm=True))

        ex_S1(0); ex_S2(0); ex_S1(1); ex_S3(0); ex_S2(1); ex_S4(0)
        ex_S3(1); ex_S5(0); ex_S1(2); ex_S4(1); ex_S2(2); ex_S5(1)
        ex_S1(3); ex_S3(2); ex_S2(3); ex_S4(2); ex_S3(3); ex_S5(2)
        ex_S4(3); ex_S5(3)
        qfm_a.rtoks += exA_toks
        qfm_b.rtoks += exB_toks

        P.op("gpsimd", lambda g: g.collective_compute(
                "AllGather", A.bypass, replica_groups=[list(range(NCORE))],
                ins=[eo_bounce.t.opt()], outs=[eo_all.t.opt()]),
             reads=[eo_bounce], writes=[eo_all], sem="ag2", tag="AG2")

        for jj in range(NT):
            isl = inclo_tm.t[:, 8 * jj:8 * jj + 8]
            op("vector", lambda g, isl=isl: g.tensor_scalar(
                    r8["va"].t[:, :], isl, float(CAP), None, A.is_le),
               reads=[inclo_tm], writes=[r8["va"]], tag=f"oi{jj}_v")
            op("vector", lambda g, isl=isl: g.tensor_scalar(
                    r8["po"].t[:, :], isl, -1.0, None, A.add),
               reads=[inclo_tm], writes=[r8["po"]], tag=f"oi{jj}_p")
            op("vector", lambda g: g.tensor_tensor(
                    r8["ia"].t[:, :], r8["po"].t[:, :], c_eoff.t[:, :], A.add),
               reads=[r8["po"], c_eoff], writes=[r8["ia"]], tag=f"oi{jj}_ia")
            for which, fst, idxf, wst in ((1, stf1[jj], "idx1f", "w1"),
                                          (2, stf2[jj], "idx2f", "w2")):
                op("vector", lambda g, fst=fst: g.tensor_tensor(
                        r8["v1"].t[:, :], fst.t[:, :], r8["va"].t[:, :], A.mult),
                   reads=[fst, r8["va"]], writes=[r8["v1"]], tag=f"oi{jj}_{which}v")
                op("vector", lambda g: g.tensor_tensor(
                        r8["t1"].t[:, :], r8["ia"].t[:, :], r8["v1"].t[:, :], A.mult),
                   reads=[r8["ia"], r8["v1"]], writes=[r8["t1"]], tag=f"oi{jj}_{which}i")
                op("vector", lambda g: g.reduce_sum(rcol["i1s"].t[:, :], r8["t1"].t[:, :],
                                                    axis=AX.X),
                   reads=[r8["t1"]], writes=[rcol["i1s"]], tag=f"oi{jj}_{which}s")
                op("vector", lambda g: g.reduce_sum(rcol["n1"].t[:, :], r8["v1"].t[:, :],
                                                    axis=AX.X),
                   reads=[r8["v1"]], writes=[rcol["n1"]], tag=f"oi{jj}_{which}n")
                op("vector", lambda g: g.tensor_scalar(
                        rcol["big1"].t[:, :], rcol["n1"].t[:, :], -BIGF, BIGF,
                        A.mult, A.add),
                   reads=[rcol["n1"]], writes=[rcol["big1"]], tag=f"oi{jj}_{which}b")
                op("vector", lambda g, jj=jj, idxf=idxf: g.tensor_tensor(
                        st[idxf].t[:, jj:jj + 1], rcol["i1s"].t[:, :],
                        rcol["big1"].t[:, :], A.add),
                   reads=[rcol["i1s"], rcol["big1"]], writes=[st[idxf]],
                   tag=f"oi{jj}_{which}x")
                mcol = st["m1"] if which == 1 else st["m2"]
                op("vector", lambda g, jj=jj, mcol=mcol, wst=wst: g.tensor_tensor(
                        st[wst].t[:, jj:jj + 1], mcol.t[:, jj:jj + 1],
                        rcol["n1"].t[:, :], A.mult),
                   reads=[mcol, rcol["n1"]], writes=[st[wst]], tag=f"oi{jj}_{which}w")
        op("vector", lambda g: g.tensor_copy(idx1_i.t[:, :], st["idx1f"].t[:, :]),
           reads=[st["idx1f"]], writes=[idx1_i], tag="ix1")
        op("vector", lambda g: g.tensor_copy(idx2_i.t[:, :], st["idx2f"].t[:, :]),
           reads=[st["idx2f"]], writes=[idx2_i], tag="ix2")
        if want_tap("idx1"):
            dma("sync", "tp7", tap_aps["idx1"], st["idx1f"].t[:, :], reads=[st["idx1f"]],
                tag="tp_i1")
        if want_tap("idx2"):
            dma("sync", "tp8", tap_aps["idx2"], st["idx2f"].t[:, :], reads=[st["idx2f"]],
                tag="tp_i2")
        if want_tap("w1"):
            dma("sync", "tp9", tap_aps["w1"], st["w1"].t[:, :], reads=[st["w1"]], tag="tp_w1")
        if want_tap("w2"):
            dma("sync", "tp10", tap_aps["w2"], st["w2"].t[:, :], reads=[st["w2"]], tag="tp_w2")



        if want_tap("eo_own"):
            dma("gpsimd", "tp13", tap_aps["eo_own"], eo_bounce.t, reads=[eo_bounce], tag="tp_eo")

        r1t, r2t = pool4[0], pool4[1]
        for jj in range(NT):
            xs = xtile[jj % 2]
            op("vector", lambda g: g.memset(r1t.t[:, :], 0.0), writes=[r1t],
               tag=f"oc{jj}_m1")
            op("vector", lambda g: g.memset(r2t.t[:, :], 0.0), writes=[r2t],
               tag=f"oc{jj}_m2")
            P.op("gpsimd", lambda g, jj=jj: g.indirect_dma_start(
                    r1t.t[:, :], None, eo_all.t,
                    IndirectOffsetOnAxis(ap=idx1_i.t[:, jj:jj + 1], axis=0),
                    bounds_check=REGS["t"], oob_is_err=False),
                 reads=[idx1_i, eo_all], writes=[r1t], dma=True, sem="g0",
                 tag=f"oc{jj}_g1")
            P.op("gpsimd", lambda g, jj=jj: g.indirect_dma_start(
                    r2t.t[:, :], None, eo_all.t,
                    IndirectOffsetOnAxis(ap=idx2_i.t[:, jj:jj + 1], axis=0),
                    bounds_check=REGS["t"], oob_is_err=False),
                 reads=[idx2_i, eo_all], writes=[r2t], dma=True, sem="g1",
                 tag=f"oc{jj}_g2")
            dma("sync", f"xt{jj % 2}", xs.t[:, :], x1_dram.t[128 * jj:128 * jj + 128, :],
                reads=[x1_dram, xs], writes=[xs], tag=f"oc{jj}_xl")
            op("vector", lambda g, jj=jj: g.tensor_scalar(
                    pool4[2].t[:, :], r1t.t[:, :], st["w1"].t[:, jj:jj + 1], None,
                    A.mult),
               reads=[r1t, st["w1"]], writes=[pool4[2]], tag=f"oc{jj}_t1")
            op("vector", lambda g, jj=jj: g.scalar_tensor_tensor(
                    pool4[3].t[:, :], r2t.t[:, :], st["w2"].t[:, jj:jj + 1],
                    pool4[2].t[:, :], A.mult, A.add),
               reads=[r2t, st["w2"], pool4[2]], writes=[pool4[3]], tag=f"oc{jj}_t2")
            op("vector", lambda g, jj=jj, xs=xs: g.tensor_tensor(
                    xs.t[:, :], xs.t[:, :], pool4[3].t[:, :], A.add),
               reads=[pool4[3], xs], writes=[xs], tag=f"oc{jj}_x1")
            stats_tile(xs, jj, st["s1x"], st["s2x"], f"L2s{jj}")
            tk = dma("sync", "x1b", x1_dram.t[128 * jj:128 * jj + 128, :], xs.t[:, :],
                     reads=[xs], deps=list(x1_dram.rtoks), tag=f"oc{jj}_xw")
            x1_dram.wtok = [tk]

        if want_tap("x1"):
            dma("sync", "tp14", tap_aps["x1"], x1_dram.t, reads=[x1_dram], tag="tp_x1")
        if upto == "x1":
            return finish()

        batch_ln_math(st["s1x"], st["s2x"], "L2b", want_q=False)
        for jj in range(NT):
            xa = xtile[jj % 2]
            dma("sync", f"xt{jj % 2}", xa.t[:, :], x1_dram.t[128 * jj:128 * jj + 128, :],
                reads=[x1_dram, xa], writes=[xa], tag=f"L2{jj}_xl")
            rstdc = st["rstd"].t[:, jj:jj + 1]
            op("scalar", lambda g, xa=xa, rstdc=rstdc, jj=jj: g.activation(
                    qtm.t[:, :], xa.t[:, :], AF.Identity,
                    bias=st["nmr8"].t[:, jj:jj + 1], scale=rstdc),
               reads=[xa, st["rstd"], st["nmr8"]], writes=[qtm], tag=f"L2{jj}_c")
            transpose_q_to_fm(qfm_a, jj, HC, f"L2{jj}_T")

        if want_tap("ln2fm"):
            dma("sync", "tpA", tap_aps["ln2fm"], qfm_a.t[:, :], reads=[qfm_a],
                tag="tp_ln2fm")

        moe_deps = P.all_toks()
        moe_stack.close()

        h1 = Buf(nc.alloc_sbuf_tensor("h1", [128, 32 * 1024], BF16), "h1")
        h2 = Buf(nc.alloc_sbuf_tensor("h2", [128, 32 * 1024], BF16), "h2")
        wf32 = [sb(f"wf32_{i}", (128, 384), F32) for i in range(5)]
        wbf = [sb(f"wbf_{i}", (128, 384), BF16) for i in range(5)]
        x1s = [sb(f"x1s_{i}", (128, 512), F32) for i in range(2)]
        outsb = [sb(f"outsb_{i}", (128, 512), F32) for i in range(2)]
        w3f = [sb(f"w3f_{i}", (128, 512), F32) for i in range(4)]
        w3b = [sb(f"w3b_{i}", (128, 512), BF16) for i in range(4)]
        b3row = sb("b3row", (1, H), F32)
        for b in (h1, h2, *wf32, *wbf, x1s[0], x1s[1],
                  outsb[0], outsb[1], *w3f, *w3b, b3row):
            b.wtok = list(moe_deps)
        dma("sync", "b3", b3row.t[:, :], b3r, reads=[b3row], writes=[b3row], tag="b3_ld")

        def ffn_layer(src_fm, kchunks, wt_ap, n_out, bias_t, dst_fm, tag):
            nch = n_out // 128
            groups = [list(range(i, min(i + 3, nch))) for i in range(0, nch, 3)]
            for gi, grp in enumerate(groups):
                gw = 128 * len(grp)
                n0 = 128 * grp[0]
                for k in range(kchunks):
                    wfb = wf32[k % 5]
                    wbb = wbf[k % 5]
                    dma("sync" if k % 2 == 0 else "scalar", f"wf{k % 5}",
                        wfb.t[:, 0:gw],
                        wt_ap[128 * k:128 * k + 128, n0:n0 + gw],
                        reads=[wfb], writes=[wfb], tag=f"{tag}_wl{gi}_{k}")
                    op("vector", lambda g, wfb=wfb, wbb=wbb, gw=gw: g.tensor_copy(
                            wbb.t[:, 0:gw], wfb.t[:, 0:gw]),
                       reads=[wfb], writes=[wbb], tag=f"{tag}_wc{gi}_{k}")
                    for ni in range(len(grp)):
                        for th in range(2):
                            bank = pb[2 * ni + th]
                            op("tensor", lambda g, k=k, ni=ni, th=th, wbb=wbb, bank=bank,
                               src_fm=src_fm, kchunks=kchunks: g.matmul(
                                    bank.t[:, :], wbb.t[:, 128 * ni:128 * ni + 128],
                                    src_fm.t[:, 1024 * k + 512 * th:
                                             1024 * k + 512 * th + 512],
                                    start=(k == 0), stop=(k == kchunks - 1)),
                               reads=[wbb, src_fm], writes=[bank],
                               
                               tag=f"{tag}_mm{gi}_{k}_{ni}_{th}")
                for ni, nn in enumerate(grp):
                    for th in range(2):
                        bank = pb[2 * ni + th]
                        dst = dst_fm.t[:, 1024 * nn + 512 * th:1024 * nn + 512 * th + 512]
                        op("scalar", lambda g, bank=bank, dst=dst, nn=nn,
                           bias_t=bias_t: g.activation(
                                dst, bank.t[:, :], AF.Relu, bias=bias_t.t[:, nn:nn + 1]),
                           reads=[bank, bias_t], writes=[dst_fm],
                           tag=f"{tag}_ev{gi}_{ni}_{th}")

        ffn_layer(qfm_a, HC, W1T, D1, b1t, h1, "f1")
        if want_tap("h1"):
            dma("sync", "tpB", tap_aps["h1"], h1.t[:, :], reads=[h1], tag="tp_h1")
        ffn_layer(h1, 32, W2T, D1, b2t, h2, "f2")
        if want_tap("h2"):
            dma("sync", "tpC", tap_aps["h2"], h2.t[:, :], reads=[h2], tag="tp_h2")

        for grp in (0, 4):
            jjg = list(range(grp, grp + 4))
            for nb in range(4):
                for k in range(32):
                    w3fb = w3f[k % 4]
                    w3bb = w3b[k % 4]
                    dma("sync" if k % 2 == 0 else "scalar", f"w3{k % 4}", w3fb.t[:, :],
                        W3T[128 * k:128 * k + 128, 512 * nb:512 * nb + 512],
                        reads=[w3fb], writes=[w3fb], tag=f"f3w{grp}_{nb}_{k}")
                    op("vector", lambda g, w3fb=w3fb, w3bb=w3bb: g.tensor_copy(
                            w3bb.t[:, :], w3fb.t[:, :]),
                       reads=[w3fb], writes=[w3bb], tag=f"f3c{grp}_{nb}_{k}")
                    for ji, jj in enumerate(jjg):
                        bank = pb[ji]
                        if k == 0:
                            op("tensor", lambda g, nb=nb, bank=bank: g.matmul(
                                    bank.t[:, :], c_onesr.t[0:1, :],
                                    b3row.t[0:1, 512 * nb:512 * nb + 512],
                                    start=True, stop=False),
                               reads=[c_onesr, b3row], writes=[bank],
                               tag=f"f3b{grp}_{nb}_{jj}")
                        op("tensor", lambda g, k=k, jj=jj, bank=bank, w3bb=w3bb: g.matmul(
                                bank.t[:, :],
                                h2.t[:, 1024 * k + 128 * jj:1024 * k + 128 * jj + 128],
                                w3bb.t[:, :],
                                start=False, stop=(k == 31)),
                           reads=[h2, w3bb], writes=[bank],
                           tag=f"f3m{grp}_{nb}_{k}_{jj}")
                for ji, jj in enumerate(jjg):
                    bank = pb[ji]
                    xs = x1s[ji % 2]
                    ob = outsb[ji % 2]
                    dma("sync", f"xr{ji % 2}", xs.t[:, :],
                        x1_dram.t[128 * jj:128 * jj + 128, 512 * nb:512 * nb + 512],
                        reads=[x1_dram, xs], writes=[xs], tag=f"f3x{grp}_{nb}_{jj}")
                    op("vector", lambda g, bank=bank, xs=xs, ob=ob: g.tensor_tensor(
                            ob.t[:, :], bank.t[:, :], xs.t[:, :], A.add),
                       reads=[bank, xs], writes=[ob], tag=f"f3o{grp}_{nb}_{jj}")
                    dma("sync", f"out{ji % 2}",
                        out_own[128 * jj:128 * jj + 128, 512 * nb:512 * nb + 512],
                        ob.t[:, :], reads=[ob], tag=f"f3s{grp}_{nb}_{jj}")

        return finish()



def _consts(core):
    r = core
    ident_bf = np.eye(128, dtype=ml_dtypes.bfloat16)
    ident_f = np.eye(128, dtype=np.float32)
    ones_col = np.ones((128, 1), np.float32)
    ones_row = np.ones((1, 128), np.float32)
    rows = np.arange(64)
    selall = (rows[:, None] % 8 == np.arange(8)[None, :]).astype(np.float32)
    selgt_own = -((rows[:, None] % 8 == np.arange(8)[None, :]) &
                  (rows[:, None] // 8 > r)).astype(np.float32)
    selallE = np.repeat((rows % 8 == r).astype(np.float32)[:, None], 8, axis=1)
    selgtnegE = -(((rows[:, None] % 8) == r) &
                  (rows[:, None] // 8 > np.arange(8)[None, :])).astype(np.float32)
    selidE = (rows[:, None] == 8 * np.arange(8)[None, :] + r).astype(np.float32)
    eoff = np.repeat((np.arange(8) * CAP).astype(np.float32)[None, :], 128, axis=0)
    iota = np.zeros((128, 64), np.int32)
    for rr in range(8):
        for j in range(8):
            iota[:, 8 * rr + j] = 8 * (128 * j + np.arange(128)) + rr
    return dict(ident_bf=ident_bf, ident_f=ident_f, ones_col=ones_col,
                ones_row=ones_row, selall=selall, selgt_own=selgt_own,
                selallE=selallE, selgtnegE=selgtnegE, selidE=selidE,
                eoff_rep=eoff, iota_rm=iota)


def make_in_maps(inputs):
    x = np.ascontiguousarray(np.asarray(inputs["L0_attn_x"], dtype=np.float32)[0])
    W1T = np.ascontiguousarray(np.asarray(inputs["L0_ffn_W1"], np.float32).T)
    W2T = np.ascontiguousarray(np.asarray(inputs["L0_ffn_W2"], np.float32).T)
    W3T = np.ascontiguousarray(np.asarray(inputs["L0_ffn_W3"], np.float32).T)
    b1m = np.ascontiguousarray(np.asarray(inputs["L0_ffn_b1"], np.float32).reshape(32, 128))
    b2m = np.ascontiguousarray(np.asarray(inputs["L0_ffn_b2"], np.float32).reshape(32, 128))
    b3r = np.ascontiguousarray(np.asarray(inputs["L0_ffn_b3"], np.float32).reshape(1, H))
    sWg = np.ascontiguousarray(np.asarray(inputs["L0_attn_sWg"], np.float32))
    sWd = np.ascontiguousarray(np.asarray(inputs["L0_attn_sWd"], np.float32))
    Wg = np.asarray(inputs["L0_attn_Wg"], np.float32)
    Wd = np.asarray(inputs["L0_attn_Wd"], np.float32)
    in_maps = []
    for r in range(NCORE):
        cm = _consts(r)
        m = dict(
            x_own=np.ascontiguousarray(x[r::8, :]),
            x_full=x,
            W1T=W1T, W2T=W2T, W3T=W3T, b1m=b1m, b2m=b2m, b3r=b3r,
            sWg=sWg, sWd=sWd,
            Wg_e=np.ascontiguousarray(Wg[r]),
            Wd_e=np.ascontiguousarray(Wd[r]),
            sNg_r=np.ascontiguousarray(np.asarray(inputs["L0_attn_sNg"], np.float32).reshape(1, H)),
            sNd_r=np.ascontiguousarray(np.asarray(inputs["L0_attn_sNd"], np.float32).reshape(1, II)),
            Ng_r=np.ascontiguousarray(np.asarray(inputs["L0_attn_Ng"], np.float32)[r].reshape(1, H)),
            Nd_r=np.ascontiguousarray(np.asarray(inputs["L0_attn_Nd"], np.float32)[r].reshape(1, II)),
            gnw_r=np.ascontiguousarray(np.asarray(inputs["L0_attn_gate_norm_w"], np.float32).reshape(1, H)),
            Wgate=np.ascontiguousarray(np.asarray(inputs["L0_attn_W_gate"], np.float32)),
            Wsg=np.ascontiguousarray(np.asarray(inputs["L0_attn_W_shared_gate"], np.float32)),
            **cm,
        )
        in_maps.append(m)
    return in_maps


def kernel(**inputs):
    nc = build_program()
    in_maps = make_in_maps(inputs)
    res = run_bass_kernel_spmd(nc, in_maps, core_ids=list(range(NCORE)))
    out = np.empty((T, H), np.float32)
    for r in range(NCORE):
        out[r::8, :] = res.results[r]["out_own"]
    return out.reshape(1, T, H)


# version 21
# speedup vs baseline: 1.0111x; 1.0111x over previous
import contextlib

import numpy as np
import ml_dtypes
import concourse.bass as bass
import concourse.mybir as mybir
from concourse.bass import IndirectOffsetOnAxis
from concourse.bass_utils import run_bass_kernel_spmd

F32 = mybir.dt.float32
BF16 = mybir.dt.bfloat16
I32 = mybir.dt.int32
A = mybir.AluOpType
AF = mybir.ActivationFunctionType
AX = mybir.AxisListType

NCORE = 8
T = 8192
TPC = 1024
NT = 8
H = 2048
HC = 16
II = 1024
IC = 8
E = 8
CAP = 1024
D1 = 4096
BIGF = 100000.0
LADDER = [128.0, 64.0, 32.0, 16.0, 8.0, 4.0, 2.0, 1.0]

ENGS = ("sync", "scalar", "vector", "tensor", "gpsimd")


class Buf:
    __slots__ = ("t", "wtok", "rtoks", "name")

    def __init__(self, t, name=""):
        self.t = t
        self.name = name
        self.wtok = []
        self.rtoks = []


class Plan:
    def __init__(self):
        self.streams = {e: [] for e in ENGS}
        self.counts = {e: 0 for e in ENGS}
        self.marks = {e: {} for e in ENGS}
        self.sem_names = set(ENGS)

    def op(self, eng, fn, reads=(), writes=(), deps=(), dma=False, sem=None, tag=""):
        want = []
        for b in reads:
            want += b.wtok
        for b in writes:
            want += b.wtok
            want += b.rtoks
        want += list(deps)
        need = {}
        for (s, v) in want:
            if eng == "tensor" and s == "tensor":
                continue
            if self.marks[eng].get(s, 0) >= v:
                continue
            if need.get(s, 0) < v:
                need[s] = v
        waits = sorted(need.items())
        for s, v in waits:
            self.marks[eng][s] = v
        amt = 16 if dma else 1
        cnt_key = sem if sem is not None else eng
        if cnt_key not in self.counts:
            self.counts[cnt_key] = 0
            self.sem_names.add(cnt_key)
        self.counts[cnt_key] += amt
        tok = (cnt_key, self.counts[cnt_key])
        self.streams[eng].append((waits, fn, amt, cnt_key, tag))
        for b in writes:
            b.wtok = [tok]
            b.rtoks = []
        for b in reads:
            b.rtoks.append(tok)
        return tok

    def all_toks(self):
        return [(k, v) for k, v in self.counts.items() if v > 0]

    def emit(self, block, sems, pre_emit=None):
        def mk(eng_name):
            def body(eng):
                if pre_emit and eng_name in pre_emit:
                    pre_emit[eng_name](eng)
                for waits, fn, amt, cnt_key, tag in self.streams[eng_name]:
                    for s, v in waits:
                        eng.wait_ge(sems[s], v)
                    ins = fn(eng)
                    ins.then_inc(sems[cnt_key], amt)
            return body
        for e in ENGS:
            getattr(block, e)(mk(e))


def build_program(upto="all", taps=()):
    nc = bass.Bass("TRN2", target_bir_lowering=False)
    P = Plan()

    def din(name, shape, dt=F32):
        return nc.dram_tensor(name, list(shape), dt, kind="ExternalInput").ap()

    x_own = din("x_own", (TPC, H))
    x_full = din("x_full", (T, H))
    W1T = din("W1T", (H, D1))
    W2T = din("W2T", (D1, D1))
    W3T = din("W3T", (D1, H))
    b1m = din("b1m", (32, 128))
    b2m = din("b2m", (32, 128))
    b3r = din("b3r", (1, H))
    sWg_d = din("sWg", (H, 2 * II))
    sWd_d = din("sWd", (II, H))
    Wg_d = din("Wg_e", (H, 2 * II))
    Wd_d = din("Wd_e", (II, H))
    sNg_r = din("sNg_r", (1, H))
    sNd_r = din("sNd_r", (1, II))
    Ng_r = din("Ng_r", (1, H))
    Nd_r = din("Nd_r", (1, II))
    gnw_r = din("gnw_r", (1, H))
    Wgate = din("Wgate", (H, E))
    Wsg = din("Wsg", (H, 1))
    ident_bf = din("ident_bf", (128, 128), BF16)
    ident_f = din("ident_f", (128, 128), F32)
    ones_col = din("ones_col", (128, 1))
    ones_row = din("ones_row", (1, 128))
    selall_d = din("selall", (64, 8))
    selgt_own = din("selgt_own", (64, 8))
    selallE_d = din("selallE", (64, 8))
    selgtnegE_d = din("selgtnegE", (64, 8))
    selidE_d = din("selidE", (64, 8))
    eoff_d = din("eoff_rep", (128, 8))
    iota_d = din("iota_rm", (128, 64), I32)

    out_own = nc.dram_tensor("out_own", [TPC, H], F32, kind="ExternalOutput").ap()

    tap_aps = {}
    for nm, shape, dt in taps:
        tap_aps[nm] = nc.dram_tensor("tap_" + nm, list(shape), dt, kind="ExternalOutput").ap()

    def want_tap(nm):
        return nm in tap_aps

    def dbuf(name, shape, dt):
        return Buf(nc.dram_tensor(name, list(shape), dt).ap(), name)

    t_sWg = dbuf("t_sWg", (H, 2 * II), BF16)
    t_sWd = dbuf("t_sWd", (II, H), BF16)
    t_Wg = dbuf("t_Wg", (H, 2 * II), BF16)
    t_Wd = dbuf("t_Wd", (II, H), BF16)
    wm_bounce = dbuf("wm_bounce", (8, 1024), F32)
    wm_all = dbuf("wm_all", (64, 1024), F32)
    eo_bounce = dbuf("eo_bounce", (TPC, H), F32)
    eo_all = dbuf("eo_all", (T, H), F32)
    x1_dram = dbuf("x1_dram", (TPC, H), F32)
    disp_dram = dbuf("disp_dram", (CAP, 1), I32)

    stack = contextlib.ExitStack()
    moe_stack = contextlib.ExitStack()

    def sb(name, shape, dt=F32):
        return Buf(nc.alloc_sbuf_tensor(name, list(shape), dt), name)

    def msb(name, shape, dt=F32):
        return Buf(moe_stack.enter_context(nc.sbuf_tensor(name, list(shape), dt)), name)

    def psum(name, shape, dt=F32):
        return Buf(stack.enter_context(nc.psum_tensor(name, list(shape), dt)), name)

    sems = {}

    def dma(eng, sem, dst_ap, src_ap, reads=(), writes=(), deps=(), tag=""):
        return P.op(eng, lambda g, d=dst_ap, s=src_ap: g.dma_start(out=d, in_=s),
                    reads=reads, writes=writes, deps=deps, dma=True, sem=sem, tag=tag)

    def op(eng, fn, reads=(), writes=(), deps=(), tag=""):
        return P.op(eng, fn, reads=reads, writes=writes, deps=deps, tag=tag)

    REGS = {}

    def _pre_gpsimd(eng):
        rc = eng.alloc_register("bc_cap")
        rt = eng.alloc_register("bc_t")
        eng.reg_mov(rc, CAP - 1)
        eng.reg_mov(rt, T - 1)
        REGS["cap"] = rc
        REGS["t"] = rt

    def finish():
        op("sync", lambda g: g.nop(), deps=P.all_toks(), tag="fin")
        make_sems()
        P.emit(stack.enter_context(nc.Block()), sems, pre_emit={"gpsimd": _pre_gpsimd})
        return nc

    def make_sems():
        for nm in sorted(P.sem_names):
            if nm not in sems:
                sems[nm] = stack.enter_context(nc.semaphore("sem_" + nm))

    with stack:
        pass_placeholder = None

        qfm_a = sb("qfm_a", (128, HC * 1024), BF16)
        c_idbf = sb("c_idbf", (128, 128), BF16)
        c_idf = sb("c_idf", (128, 128), F32)
        c_ones = sb("c_ones", (128, 1), F32)
        c_onesr = sb("c_onesr", (1, 128), F32)
        c_eoff = sb("c_eoff", (128, 8), F32)
        c_iota = sb("c_iota", (128, 64), I32)
        c_selaU = sb("c_selaU", (64, 8), F32)
        c_selgt = sb("c_selgt", (64, 8), F32)
        c_selaE = sb("c_selaE", (64, 8), F32)
        c_selgE = sb("c_selgE", (64, 8), F32)
        c_selidE = sb("c_selidE", (64, 8), F32)
        wgp = sb("wgp", (128, HC * 8), F32)
        wsg_sb = sb("wsg_sb", (128, HC), F32)
        gnw_c = sb("gnw_c", (128, HC), F32)
        b1t = sb("b1t", (128, 32), F32)
        b2t = sb("b2t", (128, 32), F32)
        row_scr = sb("row_scr", (1, 128), F32)
        st = {nm: sb("st_" + nm, (128, 8), F32) for nm in
              ("mu", "rstd", "var", "rmsq", "rstdrms", "rstdr", "alph", "gate",
               "m1", "m2", "w1", "w2", "idx1f", "idx2f", "osc", "t18",
               "s1x", "s2x", "nmr8", "rmsE", "rrE", "tb1", "tb2")}
        ste = {nm: sb("ste_" + nm, (128, 8), F32) for nm in ("alph", "osc")}
        stf1 = [sb(f"stf1_{j}", (128, 8), F32) for j in range(NT)]
        stf2 = [sb(f"stf2_{j}", (128, 8), F32) for j in range(NT)]
        idx1_i = sb("idx1_i", (128, 8), I32)
        idx2_i = sb("idx2_i", (128, 8), I32)
        z8 = sb("z8", (128, 8), F32)
        trep = {nm: sb("trep_" + nm, (128, 2), F32) for nm in ("sWg", "sWd", "Wg", "Wd")}
        tmc = {nm: sb("tmc_" + nm, (128, 1), F32) for nm in ("sWg", "sWd", "Wg", "Wd")}
        rcol = {nm: sb("rc_" + nm, (128, 1), F32) for nm in
                ("s2", "atmp", "m", "mm", "rc", "s127", "csc", "nmax", "ssum",
                 "rs", "n1", "big1", "i1s")}

        pb = [psum(f"pb{i}", (128, 512), F32) for i in range(7)]
        pQ = psum("pQ", (128, 1024), BF16)

        const_bufs = (c_idbf, c_idf, c_ones, c_onesr, c_eoff, c_iota, c_selaU,
                      c_selgt, c_selaE, c_selgE, c_selidE)
        for cbuf, src in zip(const_bufs, (ident_bf, ident_f, ones_col, ones_row,
                                          eoff_d, iota_d, selall_d, selgt_own,
                                          selallE_d, selgtnegE_d, selidE_d)):
            dma("sync", "c", cbuf.t[:, :], src, writes=[cbuf])
        for k in range(HC):
            ctk = dma("sync", "c", wsg_sb.t[:, k:k + 1], Wsg[128 * k:128 * k + 128, :],
                      writes=[wsg_sb])
        for cbuf in const_bufs + (wsg_sb,):
            cbuf.wtok = [ctk]
        op("vector", lambda g: g.memset(z8.t[:, :], 0.0), writes=[z8])
        op("vector", lambda g: g.memset(st["rmsE"].t[:, :], 1.0),
           writes=[st["rmsE"]], tag="ms_rmsE")
        epsb = {}
        for enm, ev in (("e5", 1e-5), ("e6", 1e-6), ("e8", 1e-8),
                        ("invII", 1.0 / 1024.0)):
            epsb[enm] = sb("eps_" + enm, (128, 1), F32)
            op("vector", lambda g, enm=enm, ev=ev: g.memset(
                    epsb[enm].t[:, :], ev), writes=[epsb[enm]], tag=f"ms_{enm}")

        qfm_b = msb("qfm_b", (128, IC * 1024), BF16)
        xtile = [msb(f"xtile{i}", (128, H), F32) for i in range(2)]
        pool4 = [msb(f"pool4_{i}", (128, H), F32) for i in range(4)]
        gain2048 = msb("gain2048", (128, H), F32)
        gain1024 = msb("gain1024", (128, II), F32)
        zq = msb("zq", (128, 2048), F32)
        c0q = msb("c0q", (128, 2048), F32)
        ddq = msb("ddq", (128, 2048), F32)
        qtm = msb("qtm", (128, 2048), BF16)
        lnt = msb("lnt", (128, H), F32)
        wtile = [msb(f"wtile{i}", (128, 512), BF16) for i in range(8)]
        twt = [msb(f"twt{i}", (128, 2048), BF16) for i in range(2)]
        tern_rs = msb("tern_rs", (128, 32), F32)
        lgf = msb("lgf", (8, 1024), F32)
        sgr = msb("sgr", (1, 1024), F32)
        ltm = msb("ltm", (128, 64), F32)
        sgv = msb("sgv", (128, 8), F32)
        r8 = {nm: msb("r8_" + nm, (128, 8), F32) for nm in
              ("p", "eq", "cs", "c1", "p2", "eq2", "cs2", "c2", "tmp", "wm",
               "va", "po", "ia", "v1", "t1")}
        lf = msb("lf", (128, 512), F32)
        wa = msb("wa", (64, 1024), F32)
        mk_sb = msb("mk_sb", (64, 1024), F32)
        ci_sb = msb("ci_sb", (64, 1024), F32)
        inclo = msb("inclo", (8, 1024), F32)
        incle = msb("incle", (8, 1024), F32)
        mke = msb("mke", (8, 1024), F32)
        inclo_tm = msb("inclo_tm", (128, 64), F32)
        scat_tm = msb("scat_tm", (128, 64), F32)
        scat_i = msb("scat_i", (128, 64), I32)
        dsp_sb = msb("dsp_sb", (128, 8), I32)
        dfill = msb("dfill", (128, 8), I32)
        ysc = msb("ysc", (128, 512), F32)
        ysc2 = msb("ysc2", (128, 512), F32)
        scatf = mke
        wm_sb = lgf
        tern_p, tern_n = c0q, ddq

        for hh in range(2):
            dma("sync", "sgr", sgr.t[0:1, :], gnw_r[0:1, 1024 * hh:1024 * hh + 1024],
                writes=[sgr], tag=f"gnw_ld{hh}")
            for kk in range(8):
                k = 8 * hh + kk
                op("tensor", lambda g, k=k, kk=kk: g.transpose(
                        pb[6].t[:, k:k + 1], sgr.t[0:1, 128 * kk:128 * kk + 128],
                        c_idf.t[0:1, 0:1]),
                   reads=[sgr, c_idf], writes=[pb[6]],
                    tag=f"gnwT{k}")
        op("scalar", lambda g: g.activation(gnw_c.t[:, :], pb[6].t[:, 0:HC], AF.Copy),
           reads=[pb[6]], writes=[gnw_c], tag="gnw_ev")
        for k in range(HC):
            dma("sync", "lf", lf.t[:, 8 * k % 512:8 * k % 512 + 8],
                Wgate[128 * k:128 * k + 128, :], writes=[lf], tag=f"wg_ld{k}")
            op("vector", lambda g, k=k: g.tensor_scalar(
                    wgp.t[:, 8 * k:8 * k + 8], lf.t[:, 8 * k % 512:8 * k % 512 + 8],
                    gnw_c.t[:, k:k + 1], None, A.mult),
               reads=[lf, gnw_c], writes=[wgp], tag=f"wgp{k}")

        dma("sync", "lf", lf.t[0:32, 0:128], b1m, reads=[lf], writes=[lf], tag="b1_ld")
        op("tensor", lambda g: g.transpose(pb[6].t[:, 32:64], lf.t[0:32, 0:128],
                                           c_idf.t[0:32, 0:32]),
           reads=[lf, c_idf], writes=[pb[6]], tag="b1T")
        op("scalar", lambda g: g.activation(b1t.t[:, :], pb[6].t[:, 32:64], AF.Copy),
           reads=[pb[6]], writes=[b1t], tag="b1ev")
        dma("sync", "lf", lf.t[0:32, 0:128], b2m, reads=[lf], writes=[lf], tag="b2_ld")
        op("tensor", lambda g: g.transpose(pb[6].t[:, 64:96], lf.t[0:32, 0:128],
                                           c_idf.t[0:32, 0:32]),
           reads=[lf, c_idf], writes=[pb[6]], tag="b2T")
        op("scalar", lambda g: g.activation(b2t.t[:, :], pb[6].t[:, 64:96], AF.Copy),
           reads=[pb[6]], writes=[b2t], tag="b2ev")

        def replicate_row(dst, src_ap, width, tag):
            for hh in range((width + 1023) // 1024):
                w1 = min(1024, width - 1024 * hh)
                dma("sync", "sgr", sgr.t[0:1, 0:w1], src_ap[0:1, 1024 * hh:1024 * hh + w1],
                    reads=[sgr], writes=[sgr], tag=f"rr_{tag}{hh}")
                for qq in range((w1 + 511) // 512):
                    w2 = min(512, w1 - 512 * qq)
                    op("tensor", lambda g, qq=qq, w2=w2: g.matmul(
                            pb[5].t[:, 0:w2], c_onesr.t[0:1, :],
                            sgr.t[0:1, 512 * qq:512 * qq + w2], start=True, stop=True),
                       reads=[c_onesr, sgr], writes=[pb[5]], tag=f"rm_{tag}{hh}_{qq}")
                    op("scalar", lambda g, hh=hh, qq=qq, w2=w2: g.activation(
                            dst.t[:, 1024 * hh + 512 * qq:1024 * hh + 512 * qq + w2],
                            pb[5].t[:, 0:w2], AF.Copy),
                       reads=[pb[5]], writes=[dst], tag=f"re_{tag}{hh}_{qq}")

        replicate_row(gain2048, sNg_r, H, "sng")
        replicate_row(gain1024, sNd_r, II, "snd")

        qbf = qfm_b.t[:, :].bitcast(F32)
        wfb = [Buf(qbf[:, 0:2048], "wfb0"), Buf(qbf[:, 2048:4096], "wfb1")]

        def tern_stats_stream(src_ap, kchunks, name, slots, cbase=0, k0=0,
                              k1=None):
            for k in range(k0, kchunks if k1 is None else k1):
                wf = slots[k % 2]
                dma("gpsimd", f"swf_{wf.name}", wf.t[:, :],
                    src_ap[128 * k:128 * k + 128, :],
                    reads=[wf], writes=[wf], tag=f"ts1_{name}{k}")
                op("scalar", lambda g, wf=wf, k=k, cbase=cbase: g.activation(
                        pool4[0].t[:, :], wf.t[:, :], AF.Abs,
                        accum_out=tern_rs.t[:, cbase + k:cbase + k + 1]),
                   reads=[wf], writes=[pool4[0], tern_rs], tag=f"ta_{name}{k}")

        def tern_stats_reduce(kchunks, name, bank, cbase=0):
            ncc = kchunks
            op("tensor", lambda g, ncc=ncc, bank=bank, cbase=cbase: g.matmul(
                    bank.t[0:1, 0:ncc], c_ones.t[:, :],
                    tern_rs.t[:, cbase:cbase + ncc],
                    start=True, stop=True),
               reads=[c_ones, tern_rs], writes=[bank], tag=f"tsu_{name}")
            op("vector", lambda g, ncc=ncc, bank=bank: g.tensor_copy(
                    row_scr.t[0:1, 32:32 + ncc], bank.t[0:1, 0:ncc]),
               reads=[bank], writes=[row_scr], tag=f"tc_{name}")
            op("vector", lambda g, ncc=ncc: g.reduce_sum(
                    row_scr.t[0:1, 127:128], row_scr.t[0:1, 32:32 + ncc], axis=AX.X),
               reads=[row_scr], writes=[row_scr], tag=f"tr_{name}")
            inv_n = 1.0 / (kchunks * 128 * 2048)
            op("vector", lambda g, inv_n=inv_n: g.tensor_scalar(
                    row_scr.t[0:1, 126:127], row_scr.t[0:1, 127:128], inv_n, 1e-5,
                    A.mult, A.max),
               reads=[row_scr], writes=[row_scr], tag=f"tm_{name}")
            op("vector", lambda g: g.reciprocal(row_scr.t[0:1, 125:126],
                                                row_scr.t[0:1, 126:127]),
               reads=[row_scr], writes=[row_scr], tag=f"tre_{name}")
            op("vector", lambda g: g.tensor_scalar(row_scr.t[0:1, 124:125],
                                                   row_scr.t[0:1, 125:126], -1.0, None,
                                                   A.mult),
               reads=[row_scr], writes=[row_scr], tag=f"tn_{name}")
            op("tensor", lambda g, bank=bank: g.matmul(
                    bank.t[:, 16:19], c_onesr.t[0:1, :],
                    row_scr.t[0:1, 124:127], start=True, stop=True),
               reads=[c_onesr, row_scr], writes=[bank], tag=f"trp_{name}")
            op("scalar", lambda g, bank=bank: g.activation(
                    trep[name].t[:, 0:1], bank.t[:, 17:18], AF.Copy),
               reads=[bank], writes=[trep[name]], tag=f"tv1_{name}")
            op("scalar", lambda g, bank=bank: g.activation(
                    trep[name].t[:, 1:2], bank.t[:, 16:17], AF.Copy),
               reads=[bank], writes=[trep[name]], tag=f"tv2_{name}")
            op("scalar", lambda g, bank=bank: g.activation(
                    tmc[name].t[:, :], bank.t[:, 18:19], AF.Copy),
               reads=[bank], writes=[tmc[name]], tag=f"tv3_{name}")

        def tern_write(src_ap, dst_dram, kchunks, name, slots):
            ts_last = {}
            for k in range(kchunks):
                wf = slots[k % 2]
                tw = twt[k % 2]
                dma("scalar", f"wf_{wf.name}", wf.t[:, :],
                    src_ap[128 * k:128 * k + 128, :],
                    reads=[wf], writes=[wf], tag=f"tw_{name}{k}")
                op("vector", lambda g, wf=wf: g.tensor_scalar(
                        tern_p.t[:, :], wf.t[:, :], trep[name].t[:, 0:1],
                        0.5, A.mult, A.is_gt),
                   reads=[wf, trep[name]], writes=[tern_p], tag=f"tp_{name}{k}")
                op("vector", lambda g, wf=wf: g.tensor_scalar(
                        tern_n.t[:, :], wf.t[:, :], trep[name].t[:, 1:2],
                        0.5, A.mult, A.is_gt),
                   reads=[wf, trep[name]], writes=[tern_n], tag=f"tq_{name}{k}")
                op("vector", lambda g, tw=tw: g.tensor_tensor(
                        tw.t[:, :], tern_p.t[:, :], tern_n.t[:, :], A.subtract),
                   reads=[tern_p, tern_n], writes=[tw], tag=f"tt_{name}{k}")
                tk = dma("sync", f"ts{k % 2}_" + name,
                         dst_dram.t[128 * k:128 * k + 128, :], tw.t[:, :],
                         reads=[tw], deps=list(dst_dram.rtoks), tag=f"tso_{name}{k}")
                ts_last[k % 2] = tk
                dst_dram.wtok = [t for t in ts_last.values()]

        def quant_ladder(zh, c0h, ch, dh, qh, csc_buf, eng, width, tag):
            op(eng, lambda g, width=width: g.tensor_scalar(
                    c0h.t[:, 0:width], zh.t[:, 0:width], csc_buf.t[:, :], 128.5,
                    A.mult, A.add),
               reads=[zh, csc_buf], writes=[c0h], tag=tag + "_c0")
            cur = c0h
            for si, tk in enumerate(LADDER):
                op(eng, lambda g, tk=tk, cur=cur, width=width: g.tensor_scalar(
                        dh.t[:, 0:width], cur.t[:, 0:width], tk, -tk, A.is_ge, A.mult),
                   reads=[cur], writes=[dh], tag=tag + f"_d{si}")
                op(eng, lambda g, cur=cur, width=width: g.tensor_tensor(
                        ch.t[:, 0:width], cur.t[:, 0:width], dh.t[:, 0:width], A.add),
                   reads=[cur, dh], writes=[ch], tag=tag + f"_a{si}")
                cur = ch
            if eng == "vector":
                op(eng, lambda g, width=width: g.scalar_tensor_tensor(
                        qh.t[:, 0:width], c0h.t[:, 0:width], -128.0, ch.t[:, 0:width],
                        A.add, A.subtract),
                   reads=[c0h, ch], writes=[qh], tag=tag + "_q")
            else:
                op(eng, lambda g, width=width: g.tensor_scalar(
                        dh.t[:, 0:width], c0h.t[:, 0:width], -128.0, None, A.add),
                   reads=[c0h], writes=[dh], tag=tag + "_q1")
                op(eng, lambda g, width=width: g.tensor_tensor(
                        qh.t[:, 0:width], dh.t[:, 0:width], ch.t[:, 0:width],
                        A.subtract),
                   reads=[dh, ch], writes=[qh], tag=tag + "_q2")

        def quant_tile(hsrc, off0, off1, gain_buf, rrms_buf, rrms_ap, mc_name,
                       alph_st, jj, tag):
            width = off1 - off0
            op("vector", lambda g, off0=off0, off1=off1, width=width: g.tensor_tensor(
                    zq.t[:, 0:width], hsrc.t[:, off0:off1],
                    gain_buf.t[:, 0:width], A.mult),
               reads=[hsrc, gain_buf], writes=[zq], tag=f"{tag}_z")
            op("vector", lambda g, width=width: g.reduce_max(
                    rcol["m"].t[:, :], zq.t[:, 0:width], axis=AX.X,
                    apply_absolute_value=True),
               reads=[zq], writes=[rcol["m"]], tag=f"{tag}_m")
            op("vector", lambda g, rrms_ap=rrms_ap: g.tensor_scalar(
                    rcol["mm"].t[:, :], rcol["m"].t[:, :], rrms_ap, 1e-5, A.mult, A.max),
               reads=[rcol["m"], rrms_buf], writes=[rcol["mm"]], tag=f"{tag}_mm")
            op("vector", lambda g: g.reciprocal(rcol["rc"].t[:, :], rcol["mm"].t[:, :]),
               reads=[rcol["mm"]], writes=[rcol["rc"]], tag=f"{tag}_rc")
            op("vector", lambda g: g.tensor_scalar(rcol["s127"].t[:, :], rcol["rc"].t[:, :],
                                                   127.0, None, A.mult),
               reads=[rcol["rc"]], writes=[rcol["s127"]], tag=f"{tag}_s")
            op("vector", lambda g, rrms_ap=rrms_ap: g.tensor_scalar(
                    rcol["csc"].t[:, :], rcol["s127"].t[:, :], rrms_ap, None, A.mult),
               reads=[rcol["s127"], rrms_buf], writes=[rcol["csc"]], tag=f"{tag}_csc")
            op("vector", lambda g: g.tensor_scalar(rcol["atmp"].t[:, :], rcol["mm"].t[:, :],
                                                   1.0 / 127.0, None, A.mult),
               reads=[rcol["mm"]], writes=[rcol["atmp"]], tag=f"{tag}_i")
            op("vector", lambda g, jj=jj, alph_st=alph_st: g.tensor_tensor(
                    alph_st.t[:, jj:jj + 1], rcol["atmp"].t[:, :], tmc[mc_name].t[:, :],
                    A.mult),
               reads=[rcol["atmp"], tmc[mc_name]], writes=[alph_st], tag=f"{tag}_al")
            quant_ladder(zq, c0q, zq, ddq, qtm, rcol["csc"], "vector", width, tag + "_L")

        def transpose_q_to_fm(dst_fm, jj, kchunks, tag):
            for k in range(kchunks):
                sl8 = 128 * (k % 8)
                op("tensor", lambda g, k=k, sl8=sl8: g.transpose(
                        pQ.t[:, sl8:sl8 + 128], qtm.t[:, 128 * k:128 * k + 128],
                        c_idbf.t[:, :]),
                   reads=[qtm, c_idbf], writes=[pQ],
                   tag=f"{tag}_T{k}")
                if (k % 8) == 7 or k == kchunks - 1:
                    k0 = (k // 8) * 8
                    nblk = k - k0 + 1
                    dst = dst_fm.t[:, :].rearrange("p (a b) -> p a b", b=1024)[
                        :, k0:k0 + nblk, 128 * jj:128 * jj + 128]
                    op("scalar", lambda g, dst=dst, nblk=nblk: g.activation(
                            dst, pQ.t[:, 0:128 * nblk], AF.Copy),
                       reads=[pQ], writes=[dst_fm], tag=f"{tag}_ev{k0}")


        def batch_rsqrt(src_buf, eps_buf, dst_buf, tag, scale_buf=None):
            def fn(g):
                kw = {}
                if scale_buf is not None:
                    kw["scale"] = scale_buf.t[:, :]
                return g.activation(st["tb1"].t[:, :], src_buf.t[:, :], AF.Sqrt,
                                    bias=eps_buf.t[:, :], **kw)
            rd = [src_buf, eps_buf] + ([scale_buf] if scale_buf is not None else [])
            op("scalar", fn, reads=rd, writes=[st["tb1"]], tag=tag + "_sq")
            op("vector", lambda g: g.reciprocal(dst_buf.t[:, :], st["tb1"].t[:, :]),
               reads=[st["tb1"]], writes=[dst_buf], tag=tag + "_rc")

        def stats_tile(xa, jj, s1_st, s2_st, tag):
            op("scalar", lambda g, xa=xa, jj=jj, s1_st=s1_st: g.activation(
                    twt[0].t[:, :], xa.t[:, :], AF.Copy,
                    accum_out=s1_st.t[:, jj:jj + 1]),
               reads=[xa], writes=[twt[0], s1_st], tag=f"{tag}_s1")
            op("scalar", lambda g, xa=xa, jj=jj, s2_st=s2_st: g.activation(
                    twt[1].t[:, :], xa.t[:, :], AF.Square,
                    accum_out=s2_st.t[:, jj:jj + 1]),
               reads=[xa], writes=[twt[1], s2_st], tag=f"{tag}_s2")

        def batch_ln_math(s1_st, s2_st, tag, want_q=True):
            op("vector", lambda g, s1_st=s1_st: g.tensor_scalar(
                    st["mu"].t[:, :], s1_st.t[:, :], 1.0 / H, None, A.mult),
               reads=[s1_st], writes=[st["mu"]], tag=tag + "_mu")
            op("vector", lambda g, s2_st=s2_st: g.tensor_scalar(
                    st["t18"].t[:, :], s2_st.t[:, :], 1.0 / H, None, A.mult),
               reads=[s2_st], writes=[st["t18"]], tag=tag + "_t1")
            op("vector", lambda g: g.tensor_tensor(
                    st["tb1"].t[:, :], st["mu"].t[:, :], st["mu"].t[:, :], A.mult),
               reads=[st["mu"]], writes=[st["tb1"]], tag=tag + "_m2")
            op("vector", lambda g: g.tensor_tensor(
                    st["var"].t[:, :], st["t18"].t[:, :], st["tb1"].t[:, :],
                    A.subtract),
               reads=[st["t18"], st["tb1"]], writes=[st["var"]], tag=tag + "_var")
            batch_rsqrt(st["var"], epsb["e5"], st["rstd"], tag + "_rsq")
            if want_q:
                op("vector", lambda g: g.tensor_tensor(
                        st["tb1"].t[:, :], st["var"].t[:, :], st["rstd"].t[:, :],
                        A.mult),
                   reads=[st["var"], st["rstd"]], writes=[st["tb1"]], tag=tag + "_q1")
                op("vector", lambda g: g.tensor_tensor(
                        st["rmsq"].t[:, :], st["tb1"].t[:, :], st["rstd"].t[:, :],
                        A.mult),
                   reads=[st["tb1"], st["rstd"]], writes=[st["rmsq"]], tag=tag + "_q2")
                if want_q is True:
                    batch_rsqrt(st["rmsq"], epsb["e6"], st["rstdr"], tag + "_rsr")
                batch_rsqrt(st["rmsq"], epsb["e8"], st["rstdrms"], tag + "_rsm")
            op("vector", lambda g: g.tensor_tensor(
                    st["tb1"].t[:, :], st["mu"].t[:, :], st["rstd"].t[:, :], A.mult),
               reads=[st["mu"], st["rstd"]], writes=[st["tb1"]], tag=tag + "_n1")
            op("vector", lambda g: g.tensor_scalar(
                    st["nmr8"].t[:, :], st["tb1"].t[:, :], -1.0, None, A.mult),
               reads=[st["tb1"]], writes=[st["nmr8"]], tag=tag + "_n2")

        def mlp_post_a(jj, ji, tag):
            yb = pool4[ji]
            op("scalar", lambda g, yb=yb: g.activation(lnt.t[:, 0:II], yb.t[:, 0:II],
                                                       AF.Silu),
               reads=[yb], writes=[lnt], tag=f"{tag}_silu")
            op("vector", lambda g, yb=yb: g.tensor_tensor(
                    yb.t[:, 0:II], lnt.t[:, 0:II], yb.t[:, II:2 * II], A.mult),
               reads=[lnt, yb], writes=[yb], tag=f"{tag}_h")
            op("scalar", lambda g, yb=yb, jj=jj: g.activation(
                    ddq.t[:, 0:II], yb.t[:, 0:II], AF.Square,
                    accum_out=st["rmsE"].t[:, jj:jj + 1]),
               reads=[yb], writes=[ddq, st["rmsE"]], tag=f"{tag}_h2")

        def mlp_post_b(jj, ji, osc_st, gain_i, mc_d, qdst, tag):
            quant_tile(pool4[ji], 0, II, gain_i, st["rrE"],
                       st["rrE"].t[:, jj:jj + 1], mc_d, osc_st, jj, tag + "_q2")
            transpose_q_to_fm(qdst, jj, IC, f"{tag}_qT")

        def mlp_post_group(jjg, osc_st, gain_i, mc_d, qdst, tag):
            for ji, jj in enumerate(jjg):
                mlp_post_a(jj, ji, f"{tag}a{jj}")
            batch_rsqrt(st["rmsE"], epsb["e8"], st["rrE"], f"{tag}_rr",
                        scale_buf=epsb["invII"])
            for ji, jj in enumerate(jjg):
                mlp_post_b(jj, ji, osc_st, gain_i, mc_d, qdst, f"{tag}b{jj}")




        def mlp_mm(qfm, kchunks, wt_dram, n_out, jjgrp, evac_cb, tag,
                   banks=(0, 1, 2, 3), defer_qfm=False):
            toks = []
            nblocks = n_out // 512
            for nb in range(nblocks):
                for k in range(kchunks):
                    wt = wtile[(nb * kchunks + k) % 8]
                    dma("sync", f"wt{(nb * kchunks + k) % 8}", wt.t[:, :],
                        wt_dram.t[128 * k:128 * k + 128, 512 * nb:512 * nb + 512],
                        reads=[wt_dram, wt], writes=[wt], tag=f"{tag}_w{nb}_{k}")
                    for ji, jj in enumerate(jjgrp):
                        bank = pb[banks[ji]]
                        fn = lambda g, k=k, jj=jj, wt=wt, bank=bank, qfm=qfm, \
                            kchunks=kchunks: g.matmul(
                                bank.t[:, :],
                                qfm.t[:, 1024 * k + 128 * jj:1024 * k + 128 * jj + 128],
                                wt.t[:, :],
                                start=(k == 0), stop=(k == kchunks - 1))
                        if defer_qfm:
                            t_ = op("tensor", fn, reads=[wt], writes=[bank],
                                    deps=list(qfm.wtok), tag=f"{tag}_mm{nb}_{k}_{jj}")
                            toks.append(t_)
                        else:
                            op("tensor", fn, reads=[qfm, wt], writes=[bank],
                               tag=f"{tag}_mm{nb}_{k}_{jj}")
                for ji, jj in enumerate(jjgrp):
                    evac_cb(jj, ji, 512 * nb, pb[banks[ji]])
            return toks

        def sh_mm1_evac(jj, ji, n0, pbuf):
            op("scalar", lambda g, jj=jj, ji=ji, n0=n0, pbuf=pbuf: g.activation(
                    pool4[ji].t[:, n0:n0 + 512], pbuf.t[:, :], AF.Copy,
                    scale=st["alph"].t[:, jj:jj + 1]),
               reads=[pbuf, st["alph"]], writes=[pool4[ji]], tag=f"shev{jj}_{n0}")

        for jj in range(NT):
            xa = xtile[jj % 2]
            dma("sync", f"xt{jj % 2}", xa.t[:, :], x_own[128 * jj:128 * jj + 128, :],
                reads=[xa], writes=[xa], tag=f"xsload{jj}")
            stats_tile(xa, jj, st["s1x"], st["s2x"], f"Rs{jj}")
            tern_stats_stream(sWg_d, HC, "sWg", wfb, cbase=0, k0=2 * jj,
                              k1=2 * jj + 2)
        batch_ln_math(st["s1x"], st["s2x"], "Rb")
        tern_stats_reduce(HC, "sWg", pb[5], cbase=0)
        tern_write(sWg_d, t_sWg, HC, "sWg", wfb)

        def pass2_tile(jj):
            xa = xtile[jj % 2]
            dma("sync", f"xt{jj % 2}", xa.t[:, :], x_own[128 * jj:128 * jj + 128, :],
                reads=[xa], writes=[xa], tag=f"xload{jj}")
            rstdc = st["rstd"].t[:, jj:jj + 1]
            op("scalar", lambda g, xa=xa, rstdc=rstdc, jj=jj: g.activation(
                    lnt.t[:, :], xa.t[:, :], AF.Identity,
                    bias=st["nmr8"].t[:, jj:jj + 1], scale=rstdc),
               reads=[xa, st["rstd"], st["nmr8"]], writes=[lnt], tag=f"R{jj}_ln1")

            lgb = pb[0] if jj < 4 else pb[1]
            sgb = pb[2] if jj < 4 else pb[3]
            off = 128 * (jj % 4)
            for k in range(HC):
                sl4 = 128 * (k % 4)
                op("tensor", lambda g, k=k, sl4=sl4: g.transpose(
                        pb[4].t[:, sl4:sl4 + 128], lnt.t[:, 128 * k:128 * k + 128],
                        c_idf.t[:, :]),
                   reads=[lnt, c_idf], writes=[pb[4]],
                    tag=f"R{jj}_T{k}")
                if (k % 4) == 3:
                    op("scalar", lambda g: g.activation(lf.t[:, :], pb[4].t[:, :], AF.Copy),
                       reads=[pb[4]], writes=[lf], tag=f"R{jj}_lfe{k}")
                    for kk in range(k - 3, k + 1):
                        op("tensor", lambda g, kk=kk, lgb=lgb, off=off: g.matmul(
                                lgb.t[0:8, off:off + 128],
                                wgp.t[:, 8 * kk:8 * kk + 8],
                                lf.t[:, 128 * (kk % 4):128 * (kk % 4) + 128],
                                start=(kk == 0), stop=(kk == HC - 1)),
                           reads=[wgp, lf], writes=[lgb],
                            tag=f"R{jj}_lg{kk}")
                        op("tensor", lambda g, kk=kk, sgb=sgb, off=off: g.matmul(
                                sgb.t[0:1, off:off + 128],
                                wsg_sb.t[:, kk:kk + 1],
                                lf.t[:, 128 * (kk % 4):128 * (kk % 4) + 128],
                                start=(kk == 0), stop=(kk == HC - 1)),
                           reads=[wsg_sb, lf], writes=[sgb],
                            tag=f"R{jj}_sg{kk}")

            quant_tile(lnt, 0, H, gain2048, st["rstdrms"], st["rstdrms"].t[:, jj:jj + 1],
                       "sWg", st["alph"], jj, f"R{jj}_q1")
            transpose_q_to_fm(qfm_a, jj, HC, f"R{jj}_q1T")

        stats_specs = [(sWd_d, IC, "sWd", 16), (Wg_d, HC, "Wg", 0),
                       (Wd_d, IC, "Wd", 16)]
        chunk_plan = []
        for srcp, kc, nm, cb in stats_specs:
            chunk_plan += [(srcp, kc, nm, cb, k) for k in range(kc)]
        for jj in range(4):
            pass2_tile(jj)
            for (srcp, kc, nm, cb, k) in chunk_plan[8 * jj:8 * jj + 8]:
                tern_stats_stream(srcp, kc, nm, wfb, cbase=cb, k0=k, k1=k + 1)
            if jj == 0:
                tern_stats_reduce(IC, "sWd", pb[5], cbase=16)
            elif jj == 2:
                tern_stats_reduce(HC, "Wg", pb[5], cbase=0)
            elif jj == 3:
                tern_stats_reduce(IC, "Wd", pb[5], cbase=16)
        op("vector", lambda g: g.tensor_copy(lgf.t[:, 0:512], pb[0].t[0:8, :]),
           reads=[pb[0]], writes=[lgf], tag="lg_ev0")
        op("vector", lambda g: g.tensor_copy(sgr.t[0:1, 0:512], pb[2].t[0:1, :]),
           reads=[pb[2]], writes=[sgr], tag="sg_ev0")
        mm1_toks = mlp_mm(qfm_a, HC, t_sWg, 2 * II, [0, 1, 2, 3], sh_mm1_evac,
                          "shmm1g0", banks=(0, 2, 5, 6), defer_qfm=True)
        for _w in wfb:
            qfm_b.rtoks += _w.rtoks + _w.wtok
        mlp_post_group([0, 1, 2, 3], st["osc"], gain1024, "sWd", qfm_b, "shp0")
        if want_tap("h0"):
            dma("sync", "tpG", tap_aps["h0"], pool4[0].t[:, 0:II],
                reads=[pool4[0]], tag="tp_h0")
        if want_tap("rrE0"):
            dma("sync", "tpH", tap_aps["rrE0"], st["rrE"].t[:, :],
                reads=[st["rrE"]], tag="tp_rrE0")
        if want_tap("y0"):
            dma("sync", "tpF", tap_aps["y0"], pool4[0].t[:, :], reads=[pool4[0]],
                tag="tp_y0")
        for jj in range(4, NT):
            pass2_tile(jj)
        op("vector", lambda g: g.tensor_copy(lgf.t[:, 512:1024], pb[1].t[0:8, :]),
           reads=[pb[1]], writes=[lgf], tag="lg_ev1")
        op("vector", lambda g: g.tensor_copy(sgr.t[0:1, 512:1024], pb[3].t[0:1, :]),
           reads=[pb[3]], writes=[sgr], tag="sg_ev1")
        mm1_toks += mlp_mm(qfm_a, HC, t_sWg, 2 * II, [4, 5, 6, 7], sh_mm1_evac,
                           "shmm1g4", banks=(0, 1, 2, 3), defer_qfm=True)
        qfm_a.rtoks += mm1_toks
        mlp_post_group([4, 5, 6, 7], st["osc"], gain1024, "sWd", qfm_b, "shp4")

        for j in range(NT):
            op("tensor", lambda g, j=j: g.transpose(
                    pb[4].t[:, 8 * j:8 * j + 8], lgf.t[0:8, 128 * j:128 * j + 128],
                    c_idf.t[0:8, 0:8]),
               reads=[lgf, c_idf], writes=[pb[4]],
                tag=f"lgT{j}")
        op("vector", lambda g: g.tensor_copy(ltm.t[:, :], pb[4].t[:, 0:64]),
           reads=[pb[4]], writes=[ltm], tag="ltm_ev")
        for j in range(NT):
            op("tensor", lambda g, j=j: g.transpose(
                    pb[4].t[:, j:j + 1], sgr.t[0:1, 128 * j:128 * j + 128],
                    c_idf.t[0:1, 0:1]),
               reads=[sgr, c_idf], writes=[pb[4]],
                tag=f"sgT{j}")
        op("vector", lambda g: g.tensor_copy(sgv.t[:, :], pb[4].t[:, 0:8]),
           reads=[pb[4]], writes=[sgv], tag="sgv_ev")

        for jj in range(NT):
            rrc = st["rstdr"].t[:, jj:jj + 1]
            op("vector", lambda g, jj=jj, rrc=rrc: g.tensor_scalar(
                    r8["p"].t[:, :], ltm.t[:, 8 * jj:8 * jj + 8], rrc, None, A.mult),
               reads=[ltm, st["rstdr"]], writes=[r8["p"]], tag=f"rt{jj}_l")
            op("vector", lambda g: g.reduce_max(rcol["nmax"].t[:, :], r8["p"].t[:, :],
                                                axis=AX.X, negate=True),
               reads=[r8["p"]], writes=[rcol["nmax"]], tag=f"rt{jj}_nm")
            op("scalar", lambda g: g.activation(r8["eq"].t[:, :], r8["p"].t[:, :], AF.Exp,
                                                bias=rcol["nmax"].t[:, :],
                                                accum_out=rcol["ssum"].t[:, :]),
               reads=[r8["p"], rcol["nmax"]], writes=[r8["eq"], rcol["ssum"]],
               tag=f"rt{jj}_exp")
            op("vector", lambda g: g.reciprocal(rcol["rs"].t[:, :], rcol["ssum"].t[:, :]),
               reads=[rcol["ssum"]], writes=[rcol["rs"]], tag=f"rt{jj}_rs")
            op("vector", lambda g: g.tensor_scalar(r8["p"].t[:, :], r8["eq"].t[:, :],
                                                   rcol["rs"].t[:, :], None, A.mult),
               reads=[r8["eq"], rcol["rs"]], writes=[r8["p"]], tag=f"rt{jj}_p")
            op("vector", lambda g, jj=jj: g.reduce_max(st["m1"].t[:, jj:jj + 1],
                                                       r8["p"].t[:, :], axis=AX.X),
               reads=[r8["p"]], writes=[st["m1"]], tag=f"rt{jj}_m1")
            m1c = st["m1"].t[:, jj:jj + 1]
            op("vector", lambda g, m1c=m1c: g.tensor_scalar(
                    r8["eq"].t[:, :], r8["p"].t[:, :], m1c, None, A.is_ge),
               reads=[r8["p"], st["m1"]], writes=[r8["eq"]], tag=f"rt{jj}_eq")
            op("vector", lambda g: g.tensor_tensor_scan(
                    r8["cs"].t[:, :], r8["eq"].t[:, :], z8.t[:, :], 0.0, A.add, A.add),
               reads=[r8["eq"], z8], writes=[r8["cs"]], tag=f"rt{jj}_cs")
            op("vector", lambda g: g.tensor_scalar(r8["c1"].t[:, :], r8["cs"].t[:, :],
                                                   1.0, None, A.is_le),
               reads=[r8["cs"]], writes=[r8["c1"]], tag=f"rt{jj}_c1")
            op("vector", lambda g, jj=jj: g.tensor_tensor(
                    stf1[jj].t[:, :], r8["eq"].t[:, :], r8["c1"].t[:, :], A.mult),
               reads=[r8["eq"], r8["c1"]], writes=[stf1[jj]], tag=f"rt{jj}_f1")
            op("vector", lambda g, jj=jj: g.scalar_tensor_tensor(
                    r8["p2"].t[:, :], stf1[jj].t[:, :], -1e30, r8["p"].t[:, :],
                    A.mult, A.add),
               reads=[stf1[jj], r8["p"]], writes=[r8["p2"]], tag=f"rt{jj}_p2")
            op("vector", lambda g, jj=jj: g.reduce_max(st["m2"].t[:, jj:jj + 1],
                                                       r8["p2"].t[:, :], axis=AX.X),
               reads=[r8["p2"]], writes=[st["m2"]], tag=f"rt{jj}_m2")
            m2c = st["m2"].t[:, jj:jj + 1]
            op("vector", lambda g, m2c=m2c: g.tensor_scalar(
                    r8["eq2"].t[:, :], r8["p2"].t[:, :], m2c, None, A.is_ge),
               reads=[r8["p2"], st["m2"]], writes=[r8["eq2"]], tag=f"rt{jj}_eq2")
            op("vector", lambda g: g.tensor_tensor_scan(
                    r8["cs2"].t[:, :], r8["eq2"].t[:, :], z8.t[:, :], 0.0, A.add, A.add),
               reads=[r8["eq2"], z8], writes=[r8["cs2"]], tag=f"rt{jj}_cs2")
            op("vector", lambda g: g.tensor_scalar(r8["c2"].t[:, :], r8["cs2"].t[:, :],
                                                   1.0, None, A.is_le),
               reads=[r8["cs2"]], writes=[r8["c2"]], tag=f"rt{jj}_c2")
            op("vector", lambda g, jj=jj: g.tensor_tensor(
                    stf2[jj].t[:, :], r8["eq2"].t[:, :], r8["c2"].t[:, :], A.mult),
               reads=[r8["eq2"], r8["c2"]], writes=[stf2[jj]], tag=f"rt{jj}_f2")
            op("vector", lambda g, jj=jj, m1c=m1c: g.tensor_scalar(
                    r8["tmp"].t[:, :], stf1[jj].t[:, :], m1c, None, A.mult),
               reads=[stf1[jj], st["m1"]], writes=[r8["tmp"]], tag=f"rt{jj}_wm1")
            op("vector", lambda g, jj=jj, m2c=m2c: g.scalar_tensor_tensor(
                    r8["wm"].t[:, :], stf2[jj].t[:, :], m2c, r8["tmp"].t[:, :],
                    A.mult, A.add),
               reads=[stf2[jj], st["m2"], r8["tmp"]], writes=[r8["wm"]], tag=f"rt{jj}_wm")
            op("tensor", lambda g, jj=jj: g.transpose(
                    pb[4].t[0:8, 128 * (jj % 4):128 * (jj % 4) + 128],
                    r8["wm"].t[:, :], c_idf.t[:, :]),
               reads=[r8["wm"], c_idf], writes=[pb[4]],
                tag=f"rt{jj}_wmT")
            if (jj % 4) == 3:
                j0 = jj - 3
                op("vector", lambda g, j0=j0: g.tensor_copy(
                        wm_sb.t[0:8, 128 * j0:128 * j0 + 512], pb[4].t[0:8, 0:512]),
                   reads=[pb[4]], writes=[wm_sb], tag=f"rt{jj}_wmev")
        op("scalar", lambda g: g.activation(st["gate"].t[:, :], sgv.t[:, :], AF.Sigmoid),
           reads=[sgv], writes=[st["gate"]], tag="gates")

        dma("gpsimd", "wm", wm_bounce.t, wm_sb.t[0:8, :], reads=[wm_sb], writes=[wm_bounce],
            tag="wm_st")
        P.op("gpsimd", lambda g: g.collective_compute(
                "AllGather", A.bypass, replica_groups=[list(range(NCORE))],
                ins=[wm_bounce.t.opt()], outs=[wm_all.t.opt()]),
             reads=[wm_bounce], writes=[wm_all], sem="ag1", tag="AG1")

        if want_tap("wm_all"):
            dma("gpsimd", "tp1", tap_aps["wm_all"], wm_all.t, reads=[wm_all], tag="tp_wma")
        if want_tap("gate"):
            dma("sync", "tp2", tap_aps["gate"], st["gate"].t[:, :], reads=[st["gate"]],
                tag="tp_gate")
        if want_tap("alph"):
            dma("sync", "tp3", tap_aps["alph"], st["alph"].t[:, :], reads=[st["alph"]],
                tag="tp_alph")
        if want_tap("q1fm"):
            dma("sync", "tp4", tap_aps["q1fm"], qfm_a.t[:, :], reads=[qfm_a], tag="tp_q1fm")

        tern_write(sWd_d, t_sWd, IC, "sWd", xtile)
        tern_write(Wg_d, t_Wg, HC, "Wg", xtile)
        tern_write(Wd_d, t_Wd, IC, "Wd", xtile)

        x1a_last = {}

        def sh_mm2_evac(jj, ji, n0, pbuf):
            acc = pool4[ji]
            op("scalar", lambda g, jj=jj, n0=n0, pbuf=pbuf, acc=acc: g.activation(
                    acc.t[:, n0:n0 + 512], pbuf.t[:, :], AF.Copy,
                    scale=st["osc"].t[:, jj:jj + 1]),
               reads=[pbuf, st["osc"]], writes=[acc], tag=f"s2e{jj}_{n0}")
            if n0 == 1024 and ji < 2:
                xs = xtile[jj % 2]
                dma("sync", f"xt{jj % 2}", xs.t[:, :],
                    x_own[128 * jj:128 * jj + 128, :],
                    reads=[xs], writes=[xs], tag=f"s2x{jj}")
            if n0 == 1536:
                xs = xtile[jj % 2]
                if ji >= 2:
                    dma("sync", f"xt{jj % 2}", xs.t[:, :],
                        x_own[128 * jj:128 * jj + 128, :],
                        reads=[xs], writes=[xs], tag=f"s2x{jj}")
                op("vector", lambda g, xs=xs, acc=acc: g.tensor_tensor(
                        acc.t[:, :], acc.t[:, :], xs.t[:, :], A.add),
                   reads=[xs, acc], writes=[acc], tag=f"s2a{jj}")
                tk = dma("sync", f"x1a{ji}", x1_dram.t[128 * jj:128 * jj + 128, :],
                         acc.t[:, :], reads=[acc], deps=list(x1_dram.rtoks),
                         tag=f"s2w{jj}")
                x1a_last[ji] = tk
                x1_dram.wtok = [t for t in x1a_last.values()]

        op("vector", lambda g: g.tensor_tensor(st["osc"].t[:, :], st["osc"].t[:, :],
                                               st["gate"].t[:, :], A.mult),
           reads=[st["osc"], st["gate"]], writes=[st["osc"]], tag="oscg")

        for grp in (0, 4):
            jjg = list(range(grp, grp + 4))
            mlp_mm(qfm_b, IC, t_sWd, H, jjg, sh_mm2_evac, f"shmm2g{grp}")

        dma("sync", "wa", wa.t[:, :], wm_all.t, reads=[wm_all], writes=[wa], tag="wa_ld")
        op("vector", lambda g: g.tensor_scalar(mk_sb.t[:, :], wa.t[:, :], 0.0, None,
                                               A.is_gt),
           reads=[wa], writes=[mk_sb], tag="mk")
        op("vector", lambda g: g.memset(wa.t[:, :], 0.0), reads=[wa], writes=[wa],
           tag="waz")
        op("vector", lambda g: g.tensor_tensor_scan(
                ci_sb.t[:, :], mk_sb.t[:, :], wa.t[:, :], 0.0, A.add, A.add),
           reads=[mk_sb, wa], writes=[ci_sb], tag="scan")
        for hh in range(2):
            sl = slice(512 * hh, 512 * hh + 512)
            bank = pb[0] if hh == 0 else pb[1]
            op("tensor", lambda g, sl=sl, bank=bank: g.matmul(
                    bank.t[0:8, :], c_selaU.t[0:64, :], ci_sb.t[:, sl],
                    start=True, stop=False),
               reads=[c_selaU, ci_sb], writes=[bank], tag=f"own1_{hh}")
            op("tensor", lambda g, sl=sl, bank=bank: g.matmul(
                    bank.t[0:8, :], c_selgt.t[0:64, :], mk_sb.t[:, sl],
                    start=False, stop=True),
               reads=[c_selgt, mk_sb], writes=[bank], tag=f"own2_{hh}")
            bank2 = pb[2] if hh == 0 else pb[3]
            op("tensor", lambda g, sl=sl, bank2=bank2: g.matmul(
                    bank2.t[0:8, :], c_selaE.t[0:64, :], ci_sb.t[:, sl],
                    start=True, stop=False),
               reads=[c_selaE, ci_sb], writes=[bank2], tag=f"exp1_{hh}")
            op("tensor", lambda g, sl=sl, bank2=bank2: g.matmul(
                    bank2.t[0:8, :], c_selgE.t[0:64, :], mk_sb.t[:, sl],
                    start=False, stop=True),
               reads=[c_selgE, mk_sb], writes=[bank2], tag=f"exp2_{hh}")
            bank3 = pb[5] if hh == 0 else pb[6]
            op("tensor", lambda g, sl=sl, bank3=bank3: g.matmul(
                    bank3.t[0:8, :], c_selidE.t[0:64, :], mk_sb.t[:, sl],
                    start=True, stop=True),
               reads=[c_selidE, mk_sb], writes=[bank3], tag=f"mke_{hh}")
        op("vector", lambda g: g.tensor_copy(inclo.t[:, 0:512], pb[0].t[0:8, :]),
           reads=[pb[0]], writes=[inclo], tag="io0")
        op("vector", lambda g: g.tensor_copy(inclo.t[:, 512:1024], pb[1].t[0:8, :]),
           reads=[pb[1]], writes=[inclo], tag="io1")
        op("vector", lambda g: g.tensor_copy(incle.t[:, 0:512], pb[2].t[0:8, :]),
           reads=[pb[2]], writes=[incle], tag="ie0")
        op("vector", lambda g: g.tensor_copy(incle.t[:, 512:1024], pb[3].t[0:8, :]),
           reads=[pb[3]], writes=[incle], tag="ie1")
        op("vector", lambda g: g.tensor_copy(mke.t[:, 0:512], pb[5].t[0:8, :]),
           reads=[pb[5]], writes=[mke], tag="me0")
        op("vector", lambda g: g.tensor_copy(mke.t[:, 512:1024], pb[6].t[0:8, :]),
           reads=[pb[6]], writes=[mke], tag="me1")
        op("vector", lambda g: g.tensor_scalar(wa.t[0:8, 0:1024], incle.t[:, :], float(CAP),
                                               None, A.is_le),
           reads=[incle], writes=[wa], tag="vE1")
        op("vector", lambda g: g.tensor_tensor(wa.t[0:8, 0:1024], wa.t[0:8, 0:1024], mke.t[:, :],
                                               A.mult),
           reads=[wa, mke], writes=[wa], tag="vE2")
        op("vector", lambda g: g.tensor_scalar(incle.t[:, :], incle.t[:, :], -1.0, None,
                                               A.add),
           reads=[incle], writes=[incle], tag="posE")
        op("vector", lambda g: g.tensor_tensor(scatf.t[:, :], incle.t[:, :],
                                               wa.t[0:8, 0:1024], A.mult),
           reads=[incle, wa], writes=[scatf], tag="posEm")
        op("vector", lambda g: g.tensor_scalar(wa.t[0:8, 0:1024], wa.t[0:8, 0:1024], -BIGF, BIGF,
                                               A.mult, A.add),
           reads=[wa], writes=[wa], tag="invE")
        op("vector", lambda g: g.tensor_tensor(scatf.t[:, :], scatf.t[:, :],
                                               wa.t[0:8, 0:1024], A.add),
           reads=[scatf, wa], writes=[scatf], tag="scatf")
        for j in range(8):
            op("tensor", lambda g, j=j: g.transpose(
                    pb[4].t[:, 8 * j:8 * j + 8], scatf.t[0:8, 128 * j:128 * j + 128],
                    c_idf.t[0:8, 0:8]),
               reads=[scatf, c_idf], writes=[pb[4]],
                tag=f"scT{j}")
        op("vector", lambda g: g.tensor_copy(scat_tm.t[:, :], pb[4].t[:, 0:64]),
           reads=[pb[4]], writes=[scat_tm], tag="sc_ev")
        op("vector", lambda g: g.tensor_copy(scat_i.t[:, :], scat_tm.t[:, :]),
           reads=[scat_tm], writes=[scat_i], tag="sc_i32")
        for j in range(8):
            op("tensor", lambda g, j=j: g.transpose(
                    pb[4].t[:, 8 * j:8 * j + 8], inclo.t[0:8, 128 * j:128 * j + 128],
                    c_idf.t[0:8, 0:8]),
               reads=[inclo, c_idf], writes=[pb[4]],
                tag=f"ioT{j}")
        op("vector", lambda g: g.tensor_copy(inclo_tm.t[:, :], pb[4].t[:, 0:64]),
           reads=[pb[4]], writes=[inclo_tm], tag="io_ev")

        if want_tap("scat_tm"):
            dma("sync", "tp5", tap_aps["scat_tm"], scat_tm.t[:, :], reads=[scat_tm], tag="tp_sc")
        if want_tap("inclo_tm"):
            dma("sync", "tp6", tap_aps["inclo_tm"], inclo_tm.t[:, :], reads=[inclo_tm],
                tag="tp_io")

        op("vector", lambda g: g.memset(dfill.t[:, :], T), writes=[dfill], tag="dfill")
        dma("gpsimd", "dd", disp_dram.t.rearrange("(p c) one -> p (c one)", p=128),
            dfill.t[:, :], reads=[dfill], writes=[disp_dram], tag="disp_pre")
        pre_tok = disp_dram.wtok
        last_tok = None
        for rr in range(8):
            for j in range(8):
                col = 8 * j + rr
                ic = 8 * rr + j
                last_tok = P.op("gpsimd", lambda g, col=col, ic=ic: g.indirect_dma_start(
                        disp_dram.t,
                        IndirectOffsetOnAxis(ap=scat_i.t[:, col:col + 1], axis=0),
                        c_iota.t[:, ic:ic + 1], None,
                        bounds_check=REGS["cap"], oob_is_err=False),
                     reads=[scat_i, c_iota], deps=pre_tok, dma=True, sem="dd",
                     tag=f"dsp{rr}_{j}")
        disp_dram.wtok = [last_tok]
        for g8 in range(8):
            tk = dma("sync", "dsp", dsp_sb.t[:, g8:g8 + 1],
                     disp_dram.t[128 * g8:128 * g8 + 128, :],
                     reads=[disp_dram], tag=f"dspld{g8}")
            dsp_sb.wtok = [tk]

        if want_tap("disp"):
            dma("sync", "tp11", tap_aps["disp"], dsp_sb.t[:, :], reads=[dsp_sb], tag="tp_dsp")

        if upto == "route":
            return finish()



        if want_tap("x_shared"):
            dma("sync", "tp12", tap_aps["x_shared"], x1_dram.t, reads=[x1_dram], tag="tp_xs")
        if want_tap("q2fm"):
            dma("sync", "tpD", tap_aps["q2fm"], qfm_b.t[:, :], reads=[qfm_b],
                tag="tp_q2fm")
        if want_tap("osc8"):
            dma("sync", "tpE", tap_aps["osc8"], st["osc"].t[:, :],
                reads=[st["osc"]], tag="tp_osc8")
        if upto == "shared":
            return finish()

        replicate_row(gain2048, Ng_r, H, "ng")
        replicate_row(gain1024, Nd_r, II, "nd")

        def ex_mm1_evac(jj, ji, n0, pbuf):
            op("scalar", lambda g, ji=ji, jj=jj, n0=n0, pbuf=pbuf: g.activation(
                    pool4[ji].t[:, n0:n0 + 512], pbuf.t[:, :], AF.Copy,
                    scale=ste["alph"].t[:, jj:jj + 1]),
               reads=[pbuf, ste["alph"]], writes=[pool4[ji]], tag=f"exev{jj}_{n0}")

        eo_last = {}

        def ex_mm2_evac(jj, ji, n0, pbuf):
            acc = pool4[ji]
            op("scalar", lambda g, jj=jj, n0=n0, pbuf=pbuf, acc=acc: g.activation(
                    acc.t[:, n0:n0 + 512], pbuf.t[:, :], AF.Copy,
                    scale=ste["osc"].t[:, jj:jj + 1]),
               reads=[pbuf, ste["osc"]], writes=[acc], tag=f"e2e{jj}_{n0}")
            if n0 == 1536:
                tk = dma("sync", f"eo{ji}", eo_bounce.t[128 * jj:128 * jj + 128, :],
                         acc.t[:, :], reads=[acc], deps=list(eo_bounce.rtoks),
                         tag=f"e2s{jj}")
                eo_last[ji] = tk
                eo_bounce.wtok = [t for t in eo_last.values()]

        for grp in (0, 4):
            jjg = list(range(grp, grp + 4))
            for ji, jj in enumerate(jjg):
                op("vector", lambda g, ji=ji: g.memset(pool4[ji].t[:, :], 0.0),
                   writes=[pool4[ji]], tag=f"ex{grp}_ms{ji}")
                P.op("gpsimd", lambda g, ji=ji, jj=jj: g.indirect_dma_start(
                        pool4[ji].t[:, :], None,
                        x_full, IndirectOffsetOnAxis(ap=dsp_sb.t[:, jj:jj + 1], axis=0),
                        bounds_check=REGS["t"], oob_is_err=False),
                     reads=[dsp_sb], writes=[pool4[ji]], dma=True, sem=f"g{ji}",
                     tag=f"ex{grp}_g{jj}")
            for ji, jj in enumerate(jjg):
                stats_tile(pool4[ji], jj, st["s1x"], st["s2x"], f"exs{jj}")
            batch_ln_math(st["s1x"], st["s2x"], f"exb{grp}", want_q="rms")
            for ji, jj in enumerate(jjg):
                xg = pool4[ji]
                rstdc = st["rstd"].t[:, jj:jj + 1]
                op("scalar", lambda g, xg=xg, rstdc=rstdc, jj=jj: g.activation(
                        lnt.t[:, :], xg.t[:, :], AF.Identity,
                        bias=st["nmr8"].t[:, jj:jj + 1], scale=rstdc),
                   reads=[xg, st["rstd"], st["nmr8"]], writes=[lnt], tag=f"exl{jj}_ln")
                quant_tile(lnt, 0, H, gain2048, st["rstdrms"],
                           st["rstdrms"].t[:, jj:jj + 1], "Wg",
                           ste["alph"], jj, f"exq1_{jj}")
                transpose_q_to_fm(qfm_a, jj, HC, f"exq1T{jj}")
            mlp_mm(qfm_a, HC, t_Wg, 2 * II, jjg, ex_mm1_evac, f"exmm1g{grp}")
            mlp_post_group(jjg, ste["osc"], gain1024, "Wd", qfm_b, f"exp{grp}")
            mlp_mm(qfm_b, IC, t_Wd, H, jjg, ex_mm2_evac, f"exmm2g{grp}")

        P.op("gpsimd", lambda g: g.collective_compute(
                "AllGather", A.bypass, replica_groups=[list(range(NCORE))],
                ins=[eo_bounce.t.opt()], outs=[eo_all.t.opt()]),
             reads=[eo_bounce], writes=[eo_all], sem="ag2", tag="AG2")

        for jj in range(NT):
            isl = inclo_tm.t[:, 8 * jj:8 * jj + 8]
            op("vector", lambda g, isl=isl: g.tensor_scalar(
                    r8["va"].t[:, :], isl, float(CAP), None, A.is_le),
               reads=[inclo_tm], writes=[r8["va"]], tag=f"oi{jj}_v")
            op("vector", lambda g, isl=isl: g.tensor_scalar(
                    r8["po"].t[:, :], isl, -1.0, None, A.add),
               reads=[inclo_tm], writes=[r8["po"]], tag=f"oi{jj}_p")
            op("vector", lambda g: g.tensor_tensor(
                    r8["ia"].t[:, :], r8["po"].t[:, :], c_eoff.t[:, :], A.add),
               reads=[r8["po"], c_eoff], writes=[r8["ia"]], tag=f"oi{jj}_ia")
            for which, fst, idxf, wst in ((1, stf1[jj], "idx1f", "w1"),
                                          (2, stf2[jj], "idx2f", "w2")):
                op("vector", lambda g, fst=fst: g.tensor_tensor(
                        r8["v1"].t[:, :], fst.t[:, :], r8["va"].t[:, :], A.mult),
                   reads=[fst, r8["va"]], writes=[r8["v1"]], tag=f"oi{jj}_{which}v")
                op("vector", lambda g: g.tensor_tensor(
                        r8["t1"].t[:, :], r8["ia"].t[:, :], r8["v1"].t[:, :], A.mult),
                   reads=[r8["ia"], r8["v1"]], writes=[r8["t1"]], tag=f"oi{jj}_{which}i")
                op("vector", lambda g: g.reduce_sum(rcol["i1s"].t[:, :], r8["t1"].t[:, :],
                                                    axis=AX.X),
                   reads=[r8["t1"]], writes=[rcol["i1s"]], tag=f"oi{jj}_{which}s")
                op("vector", lambda g: g.reduce_sum(rcol["n1"].t[:, :], r8["v1"].t[:, :],
                                                    axis=AX.X),
                   reads=[r8["v1"]], writes=[rcol["n1"]], tag=f"oi{jj}_{which}n")
                op("vector", lambda g: g.tensor_scalar(
                        rcol["big1"].t[:, :], rcol["n1"].t[:, :], -BIGF, BIGF,
                        A.mult, A.add),
                   reads=[rcol["n1"]], writes=[rcol["big1"]], tag=f"oi{jj}_{which}b")
                op("vector", lambda g, jj=jj, idxf=idxf: g.tensor_tensor(
                        st[idxf].t[:, jj:jj + 1], rcol["i1s"].t[:, :],
                        rcol["big1"].t[:, :], A.add),
                   reads=[rcol["i1s"], rcol["big1"]], writes=[st[idxf]],
                   tag=f"oi{jj}_{which}x")
                mcol = st["m1"] if which == 1 else st["m2"]
                op("vector", lambda g, jj=jj, mcol=mcol, wst=wst: g.tensor_tensor(
                        st[wst].t[:, jj:jj + 1], mcol.t[:, jj:jj + 1],
                        rcol["n1"].t[:, :], A.mult),
                   reads=[mcol, rcol["n1"]], writes=[st[wst]], tag=f"oi{jj}_{which}w")
        op("vector", lambda g: g.tensor_copy(idx1_i.t[:, :], st["idx1f"].t[:, :]),
           reads=[st["idx1f"]], writes=[idx1_i], tag="ix1")
        op("vector", lambda g: g.tensor_copy(idx2_i.t[:, :], st["idx2f"].t[:, :]),
           reads=[st["idx2f"]], writes=[idx2_i], tag="ix2")
        if want_tap("idx1"):
            dma("sync", "tp7", tap_aps["idx1"], st["idx1f"].t[:, :], reads=[st["idx1f"]],
                tag="tp_i1")
        if want_tap("idx2"):
            dma("sync", "tp8", tap_aps["idx2"], st["idx2f"].t[:, :], reads=[st["idx2f"]],
                tag="tp_i2")
        if want_tap("w1"):
            dma("sync", "tp9", tap_aps["w1"], st["w1"].t[:, :], reads=[st["w1"]], tag="tp_w1")
        if want_tap("w2"):
            dma("sync", "tp10", tap_aps["w2"], st["w2"].t[:, :], reads=[st["w2"]], tag="tp_w2")



        if want_tap("eo_own"):
            dma("gpsimd", "tp13", tap_aps["eo_own"], eo_bounce.t, reads=[eo_bounce], tag="tp_eo")

        for jj in range(NT):
            xs = xtile[jj % 2]
            r1t = pool4[2 * (jj % 2)]
            r2t = pool4[2 * (jj % 2) + 1]
            op("vector", lambda g, r1t=r1t: g.memset(r1t.t[:, :], 0.0),
               writes=[r1t], tag=f"oc{jj}_m1")
            op("vector", lambda g, r2t=r2t: g.memset(r2t.t[:, :], 0.0),
               writes=[r2t], tag=f"oc{jj}_m2")
            P.op("gpsimd", lambda g, jj=jj, r1t=r1t: g.indirect_dma_start(
                    r1t.t[:, :], None, eo_all.t,
                    IndirectOffsetOnAxis(ap=idx1_i.t[:, jj:jj + 1], axis=0),
                    bounds_check=REGS["t"], oob_is_err=False),
                 reads=[idx1_i, eo_all], writes=[r1t], dma=True,
                 sem=f"g{2 * (jj % 2)}", tag=f"oc{jj}_g1")
            P.op("gpsimd", lambda g, jj=jj, r2t=r2t: g.indirect_dma_start(
                    r2t.t[:, :], None, eo_all.t,
                    IndirectOffsetOnAxis(ap=idx2_i.t[:, jj:jj + 1], axis=0),
                    bounds_check=REGS["t"], oob_is_err=False),
                 reads=[idx2_i, eo_all], writes=[r2t], dma=True,
                 sem=f"g{2 * (jj % 2) + 1}", tag=f"oc{jj}_g2")
            dma("sync", f"xt{jj % 2}", xs.t[:, :], x1_dram.t[128 * jj:128 * jj + 128, :],
                reads=[x1_dram, xs], writes=[xs], tag=f"oc{jj}_xl")
            op("vector", lambda g, jj=jj, r1t=r1t: g.tensor_scalar(
                    zq.t[:, :], r1t.t[:, :], st["w1"].t[:, jj:jj + 1], None,
                    A.mult),
               reads=[r1t, st["w1"]], writes=[zq], tag=f"oc{jj}_t1")
            op("vector", lambda g, jj=jj, r2t=r2t: g.scalar_tensor_tensor(
                    c0q.t[:, :], r2t.t[:, :], st["w2"].t[:, jj:jj + 1],
                    zq.t[:, :], A.mult, A.add),
               reads=[r2t, st["w2"], zq], writes=[c0q], tag=f"oc{jj}_t2")
            op("vector", lambda g, jj=jj, xs=xs: g.tensor_tensor(
                    xs.t[:, :], xs.t[:, :], c0q.t[:, :], A.add),
               reads=[c0q, xs], writes=[xs], tag=f"oc{jj}_x1")
            stats_tile(xs, jj, st["s1x"], st["s2x"], f"L2s{jj}")
            tk = dma("sync", "x1b", x1_dram.t[128 * jj:128 * jj + 128, :], xs.t[:, :],
                     reads=[xs], deps=list(x1_dram.rtoks), tag=f"oc{jj}_xw")
            x1_dram.wtok = [tk]

        if want_tap("x1"):
            dma("sync", "tp14", tap_aps["x1"], x1_dram.t, reads=[x1_dram], tag="tp_x1")
        if upto == "x1":
            return finish()

        batch_ln_math(st["s1x"], st["s2x"], "L2b", want_q=False)
        for jj in range(NT):
            xa = xtile[jj % 2]
            dma("sync", f"xt{jj % 2}", xa.t[:, :], x1_dram.t[128 * jj:128 * jj + 128, :],
                reads=[x1_dram, xa], writes=[xa], tag=f"L2{jj}_xl")
            rstdc = st["rstd"].t[:, jj:jj + 1]
            op("scalar", lambda g, xa=xa, rstdc=rstdc, jj=jj: g.activation(
                    qtm.t[:, :], xa.t[:, :], AF.Identity,
                    bias=st["nmr8"].t[:, jj:jj + 1], scale=rstdc),
               reads=[xa, st["rstd"], st["nmr8"]], writes=[qtm], tag=f"L2{jj}_c")
            transpose_q_to_fm(qfm_a, jj, HC, f"L2{jj}_T")

        if want_tap("ln2fm"):
            dma("sync", "tpA", tap_aps["ln2fm"], qfm_a.t[:, :], reads=[qfm_a],
                tag="tp_ln2fm")

        moe_deps = P.all_toks()
        moe_stack.close()

        h1 = Buf(nc.alloc_sbuf_tensor("h1", [128, 32 * 1024], BF16), "h1")
        h2 = Buf(nc.alloc_sbuf_tensor("h2", [128, 32 * 1024], BF16), "h2")
        wf32 = [sb(f"wf32_{i}", (128, 384), F32) for i in range(5)]
        wbf = [sb(f"wbf_{i}", (128, 384), BF16) for i in range(5)]
        x1s = [sb(f"x1s_{i}", (128, 512), F32) for i in range(2)]
        outsb = [sb(f"outsb_{i}", (128, 512), F32) for i in range(2)]
        w3f = [sb(f"w3f_{i}", (128, 512), F32) for i in range(4)]
        w3b = [sb(f"w3b_{i}", (128, 512), BF16) for i in range(4)]
        b3row = sb("b3row", (1, H), F32)
        for b in (h1, h2, *wf32, *wbf, x1s[0], x1s[1],
                  outsb[0], outsb[1], *w3f, *w3b, b3row):
            b.wtok = list(moe_deps)
        dma("sync", "b3", b3row.t[:, :], b3r, reads=[b3row], writes=[b3row], tag="b3_ld")

        def ffn_layer(src_fm, kchunks, wt_ap, n_out, bias_t, dst_fm, tag):
            nch = n_out // 128
            groups = [list(range(i, min(i + 3, nch))) for i in range(0, nch, 3)]
            for gi, grp in enumerate(groups):
                gw = 128 * len(grp)
                n0 = 128 * grp[0]
                for k in range(kchunks):
                    wfb = wf32[k % 5]
                    wbb = wbf[k % 5]
                    dma("sync" if k % 2 == 0 else "scalar", f"wf{k % 5}",
                        wfb.t[:, 0:gw],
                        wt_ap[128 * k:128 * k + 128, n0:n0 + gw],
                        reads=[wfb], writes=[wfb], tag=f"{tag}_wl{gi}_{k}")
                    op("vector", lambda g, wfb=wfb, wbb=wbb, gw=gw: g.tensor_copy(
                            wbb.t[:, 0:gw], wfb.t[:, 0:gw]),
                       reads=[wfb], writes=[wbb], tag=f"{tag}_wc{gi}_{k}")
                    for ni in range(len(grp)):
                        for th in range(2):
                            bank = pb[2 * ni + th]
                            op("tensor", lambda g, k=k, ni=ni, th=th, wbb=wbb, bank=bank,
                               src_fm=src_fm, kchunks=kchunks: g.matmul(
                                    bank.t[:, :], wbb.t[:, 128 * ni:128 * ni + 128],
                                    src_fm.t[:, 1024 * k + 512 * th:
                                             1024 * k + 512 * th + 512],
                                    start=(k == 0), stop=(k == kchunks - 1)),
                               reads=[wbb, src_fm], writes=[bank],
                               
                               tag=f"{tag}_mm{gi}_{k}_{ni}_{th}")
                for ni, nn in enumerate(grp):
                    for th in range(2):
                        bank = pb[2 * ni + th]
                        dst = dst_fm.t[:, 1024 * nn + 512 * th:1024 * nn + 512 * th + 512]
                        op("scalar", lambda g, bank=bank, dst=dst, nn=nn,
                           bias_t=bias_t: g.activation(
                                dst, bank.t[:, :], AF.Relu, bias=bias_t.t[:, nn:nn + 1]),
                           reads=[bank, bias_t], writes=[dst_fm],
                           tag=f"{tag}_ev{gi}_{ni}_{th}")

        ffn_layer(qfm_a, HC, W1T, D1, b1t, h1, "f1")
        if want_tap("h1"):
            dma("sync", "tpB", tap_aps["h1"], h1.t[:, :], reads=[h1], tag="tp_h1")
        ffn_layer(h1, 32, W2T, D1, b2t, h2, "f2")
        if want_tap("h2"):
            dma("sync", "tpC", tap_aps["h2"], h2.t[:, :], reads=[h2], tag="tp_h2")

        for grp in (0, 4):
            jjg = list(range(grp, grp + 4))
            for nb in range(4):
                for k in range(32):
                    w3fb = w3f[k % 4]
                    w3bb = w3b[k % 4]
                    dma("sync" if k % 2 == 0 else "scalar", f"w3{k % 4}", w3fb.t[:, :],
                        W3T[128 * k:128 * k + 128, 512 * nb:512 * nb + 512],
                        reads=[w3fb], writes=[w3fb], tag=f"f3w{grp}_{nb}_{k}")
                    op("vector", lambda g, w3fb=w3fb, w3bb=w3bb: g.tensor_copy(
                            w3bb.t[:, :], w3fb.t[:, :]),
                       reads=[w3fb], writes=[w3bb], tag=f"f3c{grp}_{nb}_{k}")
                    for ji, jj in enumerate(jjg):
                        bank = pb[ji]
                        if k == 0:
                            op("tensor", lambda g, nb=nb, bank=bank: g.matmul(
                                    bank.t[:, :], c_onesr.t[0:1, :],
                                    b3row.t[0:1, 512 * nb:512 * nb + 512],
                                    start=True, stop=False),
                               reads=[c_onesr, b3row], writes=[bank],
                               tag=f"f3b{grp}_{nb}_{jj}")
                        op("tensor", lambda g, k=k, jj=jj, bank=bank, w3bb=w3bb: g.matmul(
                                bank.t[:, :],
                                h2.t[:, 1024 * k + 128 * jj:1024 * k + 128 * jj + 128],
                                w3bb.t[:, :],
                                start=False, stop=(k == 31)),
                           reads=[h2, w3bb], writes=[bank],
                           tag=f"f3m{grp}_{nb}_{k}_{jj}")
                for ji, jj in enumerate(jjg):
                    bank = pb[ji]
                    xs = x1s[ji % 2]
                    ob = outsb[ji % 2]
                    dma("sync", f"xr{ji % 2}", xs.t[:, :],
                        x1_dram.t[128 * jj:128 * jj + 128, 512 * nb:512 * nb + 512],
                        reads=[x1_dram, xs], writes=[xs], tag=f"f3x{grp}_{nb}_{jj}")
                    op("vector", lambda g, bank=bank, xs=xs, ob=ob: g.tensor_tensor(
                            ob.t[:, :], bank.t[:, :], xs.t[:, :], A.add),
                       reads=[bank, xs], writes=[ob], tag=f"f3o{grp}_{nb}_{jj}")
                    dma("sync", f"out{ji % 2}",
                        out_own[128 * jj:128 * jj + 128, 512 * nb:512 * nb + 512],
                        ob.t[:, :], reads=[ob], tag=f"f3s{grp}_{nb}_{jj}")

        return finish()



def _consts(core):
    r = core
    ident_bf = np.eye(128, dtype=ml_dtypes.bfloat16)
    ident_f = np.eye(128, dtype=np.float32)
    ones_col = np.ones((128, 1), np.float32)
    ones_row = np.ones((1, 128), np.float32)
    rows = np.arange(64)
    selall = (rows[:, None] % 8 == np.arange(8)[None, :]).astype(np.float32)
    selgt_own = -((rows[:, None] % 8 == np.arange(8)[None, :]) &
                  (rows[:, None] // 8 > r)).astype(np.float32)
    selallE = np.repeat((rows % 8 == r).astype(np.float32)[:, None], 8, axis=1)
    selgtnegE = -(((rows[:, None] % 8) == r) &
                  (rows[:, None] // 8 > np.arange(8)[None, :])).astype(np.float32)
    selidE = (rows[:, None] == 8 * np.arange(8)[None, :] + r).astype(np.float32)
    eoff = np.repeat((np.arange(8) * CAP).astype(np.float32)[None, :], 128, axis=0)
    iota = np.zeros((128, 64), np.int32)
    for rr in range(8):
        for j in range(8):
            iota[:, 8 * rr + j] = 8 * (128 * j + np.arange(128)) + rr
    return dict(ident_bf=ident_bf, ident_f=ident_f, ones_col=ones_col,
                ones_row=ones_row, selall=selall, selgt_own=selgt_own,
                selallE=selallE, selgtnegE=selgtnegE, selidE=selidE,
                eoff_rep=eoff, iota_rm=iota)


def make_in_maps(inputs):
    x = np.ascontiguousarray(np.asarray(inputs["L0_attn_x"], dtype=np.float32)[0])
    W1T = np.ascontiguousarray(np.asarray(inputs["L0_ffn_W1"], np.float32).T)
    W2T = np.ascontiguousarray(np.asarray(inputs["L0_ffn_W2"], np.float32).T)
    W3T = np.ascontiguousarray(np.asarray(inputs["L0_ffn_W3"], np.float32).T)
    b1m = np.ascontiguousarray(np.asarray(inputs["L0_ffn_b1"], np.float32).reshape(32, 128))
    b2m = np.ascontiguousarray(np.asarray(inputs["L0_ffn_b2"], np.float32).reshape(32, 128))
    b3r = np.ascontiguousarray(np.asarray(inputs["L0_ffn_b3"], np.float32).reshape(1, H))
    sWg = np.ascontiguousarray(np.asarray(inputs["L0_attn_sWg"], np.float32))
    sWd = np.ascontiguousarray(np.asarray(inputs["L0_attn_sWd"], np.float32))
    Wg = np.asarray(inputs["L0_attn_Wg"], np.float32)
    Wd = np.asarray(inputs["L0_attn_Wd"], np.float32)
    in_maps = []
    for r in range(NCORE):
        cm = _consts(r)
        m = dict(
            x_own=np.ascontiguousarray(x[r::8, :]),
            x_full=x,
            W1T=W1T, W2T=W2T, W3T=W3T, b1m=b1m, b2m=b2m, b3r=b3r,
            sWg=sWg, sWd=sWd,
            Wg_e=np.ascontiguousarray(Wg[r]),
            Wd_e=np.ascontiguousarray(Wd[r]),
            sNg_r=np.ascontiguousarray(np.asarray(inputs["L0_attn_sNg"], np.float32).reshape(1, H)),
            sNd_r=np.ascontiguousarray(np.asarray(inputs["L0_attn_sNd"], np.float32).reshape(1, II)),
            Ng_r=np.ascontiguousarray(np.asarray(inputs["L0_attn_Ng"], np.float32)[r].reshape(1, H)),
            Nd_r=np.ascontiguousarray(np.asarray(inputs["L0_attn_Nd"], np.float32)[r].reshape(1, II)),
            gnw_r=np.ascontiguousarray(np.asarray(inputs["L0_attn_gate_norm_w"], np.float32).reshape(1, H)),
            Wgate=np.ascontiguousarray(np.asarray(inputs["L0_attn_W_gate"], np.float32)),
            Wsg=np.ascontiguousarray(np.asarray(inputs["L0_attn_W_shared_gate"], np.float32)),
            **cm,
        )
        in_maps.append(m)
    return in_maps


def kernel(**inputs):
    nc = build_program()
    in_maps = make_in_maps(inputs)
    res = run_bass_kernel_spmd(nc, in_maps, core_ids=list(range(NCORE)))
    out = np.empty((T, H), np.float32)
    for r in range(NCORE):
        out[r::8, :] = res.results[r]["out_own"]
    return out.reshape(1, T, H)


# version 22
# speedup vs baseline: 1.0425x; 1.0310x over previous
import contextlib

import numpy as np
import ml_dtypes
import concourse.bass as bass
import concourse.mybir as mybir
from concourse.bass import IndirectOffsetOnAxis
from concourse.bass_utils import run_bass_kernel_spmd

F32 = mybir.dt.float32
BF16 = mybir.dt.bfloat16
I32 = mybir.dt.int32
A = mybir.AluOpType
AF = mybir.ActivationFunctionType
AX = mybir.AxisListType

NCORE = 8
T = 8192
TPC = 1024
NT = 8
H = 2048
HC = 16
II = 1024
IC = 8
E = 8
CAP = 1024
D1 = 4096
BIGF = 100000.0
LADDER = [128.0, 64.0, 32.0, 16.0, 8.0, 4.0, 2.0, 1.0]

ENGS = ("sync", "scalar", "vector", "tensor", "gpsimd")


class Buf:
    __slots__ = ("t", "wtok", "rtoks", "name")

    def __init__(self, t, name=""):
        self.t = t
        self.name = name
        self.wtok = []
        self.rtoks = []


class Plan:
    def __init__(self):
        self.streams = {e: [] for e in ENGS}
        self.counts = {e: 0 for e in ENGS}
        self.marks = {e: {} for e in ENGS}
        self.sem_names = set(ENGS)

    def op(self, eng, fn, reads=(), writes=(), deps=(), dma=False, sem=None, tag=""):
        want = []
        for b in reads:
            want += b.wtok
        for b in writes:
            want += b.wtok
            want += b.rtoks
        want += list(deps)
        need = {}
        for (s, v) in want:
            if eng == "tensor" and s == "tensor":
                continue
            if self.marks[eng].get(s, 0) >= v:
                continue
            if need.get(s, 0) < v:
                need[s] = v
        waits = sorted(need.items())
        for s, v in waits:
            self.marks[eng][s] = v
        amt = 16 if dma else 1
        cnt_key = sem if sem is not None else eng
        if cnt_key not in self.counts:
            self.counts[cnt_key] = 0
            self.sem_names.add(cnt_key)
        self.counts[cnt_key] += amt
        tok = (cnt_key, self.counts[cnt_key])
        self.streams[eng].append((waits, fn, amt, cnt_key, tag))
        for b in writes:
            b.wtok = [tok]
            b.rtoks = []
        for b in reads:
            b.rtoks.append(tok)
        return tok

    def all_toks(self):
        return [(k, v) for k, v in self.counts.items() if v > 0]

    def emit(self, block, sems, pre_emit=None):
        def mk(eng_name):
            def body(eng):
                if pre_emit and eng_name in pre_emit:
                    pre_emit[eng_name](eng)
                for waits, fn, amt, cnt_key, tag in self.streams[eng_name]:
                    for s, v in waits:
                        eng.wait_ge(sems[s], v)
                    ins = fn(eng)
                    ins.then_inc(sems[cnt_key], amt)
            return body
        for e in ENGS:
            getattr(block, e)(mk(e))


def build_program(upto="all", taps=()):
    nc = bass.Bass("TRN2", target_bir_lowering=False)
    P = Plan()

    def din(name, shape, dt=F32):
        return nc.dram_tensor(name, list(shape), dt, kind="ExternalInput").ap()

    x_own = din("x_own", (TPC, H))
    x_full = din("x_full", (T, H))
    W1T = din("W1T", (H, D1))
    W2T = din("W2T", (D1, D1))
    W3T = din("W3T", (D1, H))
    b1m = din("b1m", (32, 128))
    b2m = din("b2m", (32, 128))
    b3r = din("b3r", (1, H))
    sWg_d = din("sWg", (H, 2 * II))
    sWd_d = din("sWd", (II, H))
    Wg_d = din("Wg_e", (H, 2 * II))
    Wd_d = din("Wd_e", (II, H))
    sNg_r = din("sNg_r", (1, H))
    sNd_r = din("sNd_r", (1, II))
    Ng_r = din("Ng_r", (1, H))
    Nd_r = din("Nd_r", (1, II))
    gnw_r = din("gnw_r", (1, H))
    Wgate = din("Wgate", (H, E))
    Wsg = din("Wsg", (H, 1))
    ident_bf = din("ident_bf", (128, 128), BF16)
    ident_f = din("ident_f", (128, 128), F32)
    ones_col = din("ones_col", (128, 1))
    ones_row = din("ones_row", (1, 128))
    selall_d = din("selall", (64, 8))
    selgt_own = din("selgt_own", (64, 8))
    selallE_d = din("selallE", (64, 8))
    selgtnegE_d = din("selgtnegE", (64, 8))
    selidE_d = din("selidE", (64, 8))
    eoff_d = din("eoff_rep", (128, 8))
    iota_d = din("iota_rm", (128, 64), I32)

    out_own = nc.dram_tensor("out_own", [TPC, H], F32, kind="ExternalOutput").ap()

    tap_aps = {}
    for nm, shape, dt in taps:
        tap_aps[nm] = nc.dram_tensor("tap_" + nm, list(shape), dt, kind="ExternalOutput").ap()

    def want_tap(nm):
        return nm in tap_aps

    def dbuf(name, shape, dt):
        return Buf(nc.dram_tensor(name, list(shape), dt).ap(), name)

    t_sWg = dbuf("t_sWg", (H, 2 * II), BF16)
    t_sWd = dbuf("t_sWd", (II, H), BF16)
    t_Wg = dbuf("t_Wg", (H, 2 * II), BF16)
    t_Wd = dbuf("t_Wd", (II, H), BF16)
    wm_bounce = dbuf("wm_bounce", (8, 1024), F32)
    wm_all = dbuf("wm_all", (64, 1024), F32)
    eo_bounce = dbuf("eo_bounce", (TPC, H), F32)
    eo_A = dbuf("eo_A", (T // 2, H), F32)
    eo_B = dbuf("eo_B", (T // 2, H), F32)
    x1_dram = dbuf("x1_dram", (TPC, H), F32)
    disp_dram = dbuf("disp_dram", (CAP, 1), I32)

    stack = contextlib.ExitStack()
    moe_stack = contextlib.ExitStack()

    def sb(name, shape, dt=F32):
        return Buf(nc.alloc_sbuf_tensor(name, list(shape), dt), name)

    def msb(name, shape, dt=F32):
        return Buf(moe_stack.enter_context(nc.sbuf_tensor(name, list(shape), dt)), name)

    def psum(name, shape, dt=F32):
        return Buf(stack.enter_context(nc.psum_tensor(name, list(shape), dt)), name)

    sems = {}

    def dma(eng, sem, dst_ap, src_ap, reads=(), writes=(), deps=(), tag=""):
        return P.op(eng, lambda g, d=dst_ap, s=src_ap: g.dma_start(out=d, in_=s),
                    reads=reads, writes=writes, deps=deps, dma=True, sem=sem, tag=tag)

    def op(eng, fn, reads=(), writes=(), deps=(), tag=""):
        return P.op(eng, fn, reads=reads, writes=writes, deps=deps, tag=tag)

    REGS = {}

    def _pre_gpsimd(eng):
        rc = eng.alloc_register("bc_cap")
        rt = eng.alloc_register("bc_t")
        rh = eng.alloc_register("bc_h")
        eng.reg_mov(rc, CAP - 1)
        eng.reg_mov(rt, T - 1)
        eng.reg_mov(rh, T // 2 - 1)
        REGS["cap"] = rc
        REGS["t"] = rt
        REGS["h"] = rh

    def finish():
        op("sync", lambda g: g.nop(), deps=P.all_toks(), tag="fin")
        make_sems()
        P.emit(stack.enter_context(nc.Block()), sems, pre_emit={"gpsimd": _pre_gpsimd})
        return nc

    def make_sems():
        for nm in sorted(P.sem_names):
            if nm not in sems:
                sems[nm] = stack.enter_context(nc.semaphore("sem_" + nm))

    with stack:
        pass_placeholder = None

        qfm_a = sb("qfm_a", (128, HC * 1024), BF16)
        c_idbf = sb("c_idbf", (128, 128), BF16)
        c_idf = sb("c_idf", (128, 128), F32)
        c_ones = sb("c_ones", (128, 1), F32)
        c_onesr = sb("c_onesr", (1, 128), F32)
        c_eoff = sb("c_eoff", (128, 8), F32)
        c_iota = sb("c_iota", (128, 64), I32)
        c_selaU = sb("c_selaU", (64, 8), F32)
        c_selgt = sb("c_selgt", (64, 8), F32)
        c_selaE = sb("c_selaE", (64, 8), F32)
        c_selgE = sb("c_selgE", (64, 8), F32)
        c_selidE = sb("c_selidE", (64, 8), F32)
        wgp = sb("wgp", (128, HC * 8), F32)
        wsg_sb = sb("wsg_sb", (128, HC), F32)
        gnw_c = sb("gnw_c", (128, HC), F32)
        b1t = sb("b1t", (128, 32), F32)
        b2t = sb("b2t", (128, 32), F32)
        row_scr = sb("row_scr", (1, 128), F32)
        st = {nm: sb("st_" + nm, (128, 8), F32) for nm in
              ("mu", "rstd", "var", "rmsq", "rstdrms", "rstdr", "alph", "gate",
               "m1", "m2", "w1", "w2", "i1A", "i1B", "i2A", "i2B", "osc", "t18",
               "s1x", "s2x", "nmr8", "rmsE", "rrE", "tb1", "tb2")}
        ste = {nm: sb("ste_" + nm, (128, 8), F32) for nm in ("alph", "osc")}
        stf1 = [sb(f"stf1_{j}", (128, 8), F32) for j in range(NT)]
        stf2 = [sb(f"stf2_{j}", (128, 8), F32) for j in range(NT)]
        idxh_i = {nm: sb(f"idxh_{nm}", (128, 8), I32)
                  for nm in ("1A", "1B", "2A", "2B")}
        c_eoff2 = sb("c_eoff2", (128, 8), F32)
        z8 = sb("z8", (128, 8), F32)
        trep = {nm: sb("trep_" + nm, (128, 2), F32) for nm in ("sWg", "sWd", "Wg", "Wd")}
        tmc = {nm: sb("tmc_" + nm, (128, 1), F32) for nm in ("sWg", "sWd", "Wg", "Wd")}
        rcol = {nm: sb("rc_" + nm, (128, 1), F32) for nm in
                ("s2", "atmp", "m", "mm", "rc", "s127", "csc", "nmax", "ssum",
                 "rs", "n1", "big1", "i1s", "pm", "vb", "vbn", "van", "isb")}

        pb = [psum(f"pb{i}", (128, 512), F32) for i in range(7)]
        pQ = psum("pQ", (128, 1024), BF16)

        const_bufs = (c_idbf, c_idf, c_ones, c_onesr, c_eoff, c_iota, c_selaU,
                      c_selgt, c_selaE, c_selgE, c_selidE)
        for cbuf, src in zip(const_bufs, (ident_bf, ident_f, ones_col, ones_row,
                                          eoff_d, iota_d, selall_d, selgt_own,
                                          selallE_d, selgtnegE_d, selidE_d)):
            dma("sync", "c", cbuf.t[:, :], src, writes=[cbuf])
        for k in range(HC):
            ctk = dma("sync", "c", wsg_sb.t[:, k:k + 1], Wsg[128 * k:128 * k + 128, :],
                      writes=[wsg_sb])
        for cbuf in const_bufs + (wsg_sb,):
            cbuf.wtok = [ctk]
        op("vector", lambda g: g.memset(z8.t[:, :], 0.0), writes=[z8])
        op("vector", lambda g: g.tensor_scalar(c_eoff2.t[:, :], c_eoff.t[:, :],
                                               0.5, None, A.mult),
           reads=[c_eoff], writes=[c_eoff2], tag="eoff2")
        op("vector", lambda g: g.memset(st["rmsE"].t[:, :], 1.0),
           writes=[st["rmsE"]], tag="ms_rmsE")
        epsb = {}
        for enm, ev in (("e5", 1e-5), ("e6", 1e-6), ("e8", 1e-8),
                        ("invII", 1.0 / 1024.0)):
            epsb[enm] = sb("eps_" + enm, (128, 1), F32)
            op("vector", lambda g, enm=enm, ev=ev: g.memset(
                    epsb[enm].t[:, :], ev), writes=[epsb[enm]], tag=f"ms_{enm}")

        qfm_b = msb("qfm_b", (128, IC * 1024), BF16)
        xtile = [msb(f"xtile{i}", (128, H), F32) for i in range(2)]
        pool4 = [msb(f"pool4_{i}", (128, H), F32) for i in range(4)]
        gain2048 = msb("gain2048", (128, H), F32)
        gain1024 = msb("gain1024", (128, II), F32)
        zq = msb("zq", (128, 2048), F32)
        c0q = msb("c0q", (128, 2048), F32)
        ddq = msb("ddq", (128, 2048), F32)
        qtm = msb("qtm", (128, 2048), BF16)
        lnt = msb("lnt", (128, H), F32)
        wtile = [msb(f"wtile{i}", (128, 512), BF16) for i in range(8)]
        twt = [msb(f"twt{i}", (128, 2048), BF16) for i in range(2)]
        tern_rs = msb("tern_rs", (128, 32), F32)
        lgf = msb("lgf", (8, 1024), F32)
        sgr = msb("sgr", (1, 1024), F32)
        ltm = msb("ltm", (128, 64), F32)
        sgv = msb("sgv", (128, 8), F32)
        r8 = {nm: msb("r8_" + nm, (128, 8), F32) for nm in
              ("p", "eq", "cs", "c1", "p2", "eq2", "cs2", "c2", "tmp", "wm",
               "va", "po", "ia", "v1", "t1")}
        lf = msb("lf", (128, 512), F32)
        wa = msb("wa", (64, 1024), F32)
        mk_sb = msb("mk_sb", (64, 1024), F32)
        ci_sb = msb("ci_sb", (64, 1024), F32)
        inclo = msb("inclo", (8, 1024), F32)
        incle = msb("incle", (8, 1024), F32)
        mke = msb("mke", (8, 1024), F32)
        inclo_tm = msb("inclo_tm", (128, 64), F32)
        scat_tm = msb("scat_tm", (128, 64), F32)
        scat_i = msb("scat_i", (128, 64), I32)
        dsp_sb = msb("dsp_sb", (128, 8), I32)
        dfill = msb("dfill", (128, 8), I32)
        ysc = msb("ysc", (128, 512), F32)
        ysc2 = msb("ysc2", (128, 512), F32)
        scatf = mke
        wm_sb = lgf
        tern_p, tern_n = c0q, ddq

        for hh in range(2):
            dma("sync", "sgr", sgr.t[0:1, :], gnw_r[0:1, 1024 * hh:1024 * hh + 1024],
                writes=[sgr], tag=f"gnw_ld{hh}")
            for kk in range(8):
                k = 8 * hh + kk
                op("tensor", lambda g, k=k, kk=kk: g.transpose(
                        pb[6].t[:, k:k + 1], sgr.t[0:1, 128 * kk:128 * kk + 128],
                        c_idf.t[0:1, 0:1]),
                   reads=[sgr, c_idf], writes=[pb[6]],
                    tag=f"gnwT{k}")
        op("scalar", lambda g: g.activation(gnw_c.t[:, :], pb[6].t[:, 0:HC], AF.Copy),
           reads=[pb[6]], writes=[gnw_c], tag="gnw_ev")
        for k in range(HC):
            dma("sync", "lf", lf.t[:, 8 * k % 512:8 * k % 512 + 8],
                Wgate[128 * k:128 * k + 128, :], writes=[lf], tag=f"wg_ld{k}")
            op("vector", lambda g, k=k: g.tensor_scalar(
                    wgp.t[:, 8 * k:8 * k + 8], lf.t[:, 8 * k % 512:8 * k % 512 + 8],
                    gnw_c.t[:, k:k + 1], None, A.mult),
               reads=[lf, gnw_c], writes=[wgp], tag=f"wgp{k}")

        dma("sync", "lf", lf.t[0:32, 0:128], b1m, reads=[lf], writes=[lf], tag="b1_ld")
        op("tensor", lambda g: g.transpose(pb[6].t[:, 32:64], lf.t[0:32, 0:128],
                                           c_idf.t[0:32, 0:32]),
           reads=[lf, c_idf], writes=[pb[6]], tag="b1T")
        op("scalar", lambda g: g.activation(b1t.t[:, :], pb[6].t[:, 32:64], AF.Copy),
           reads=[pb[6]], writes=[b1t], tag="b1ev")
        dma("sync", "lf", lf.t[0:32, 0:128], b2m, reads=[lf], writes=[lf], tag="b2_ld")
        op("tensor", lambda g: g.transpose(pb[6].t[:, 64:96], lf.t[0:32, 0:128],
                                           c_idf.t[0:32, 0:32]),
           reads=[lf, c_idf], writes=[pb[6]], tag="b2T")
        op("scalar", lambda g: g.activation(b2t.t[:, :], pb[6].t[:, 64:96], AF.Copy),
           reads=[pb[6]], writes=[b2t], tag="b2ev")

        def replicate_row(dst, src_ap, width, tag):
            for hh in range((width + 1023) // 1024):
                w1 = min(1024, width - 1024 * hh)
                dma("sync", "sgr", sgr.t[0:1, 0:w1], src_ap[0:1, 1024 * hh:1024 * hh + w1],
                    reads=[sgr], writes=[sgr], tag=f"rr_{tag}{hh}")
                for qq in range((w1 + 511) // 512):
                    w2 = min(512, w1 - 512 * qq)
                    op("tensor", lambda g, qq=qq, w2=w2: g.matmul(
                            pb[5].t[:, 0:w2], c_onesr.t[0:1, :],
                            sgr.t[0:1, 512 * qq:512 * qq + w2], start=True, stop=True),
                       reads=[c_onesr, sgr], writes=[pb[5]], tag=f"rm_{tag}{hh}_{qq}")
                    op("scalar", lambda g, hh=hh, qq=qq, w2=w2: g.activation(
                            dst.t[:, 1024 * hh + 512 * qq:1024 * hh + 512 * qq + w2],
                            pb[5].t[:, 0:w2], AF.Copy),
                       reads=[pb[5]], writes=[dst], tag=f"re_{tag}{hh}_{qq}")

        replicate_row(gain2048, sNg_r, H, "sng")
        replicate_row(gain1024, sNd_r, II, "snd")

        qbf = qfm_b.t[:, :].bitcast(F32)
        wfb = [Buf(qbf[:, 0:2048], "wfb0"), Buf(qbf[:, 2048:4096], "wfb1")]

        def tern_stats_stream(src_ap, kchunks, name, slots, cbase=0, k0=0,
                              k1=None):
            for k in range(k0, kchunks if k1 is None else k1):
                wf = slots[k % 2]
                dma("gpsimd", f"swf_{wf.name}", wf.t[:, :],
                    src_ap[128 * k:128 * k + 128, :],
                    reads=[wf], writes=[wf], tag=f"ts1_{name}{k}")
                op("scalar", lambda g, wf=wf, k=k, cbase=cbase: g.activation(
                        pool4[0].t[:, :], wf.t[:, :], AF.Abs,
                        accum_out=tern_rs.t[:, cbase + k:cbase + k + 1]),
                   reads=[wf], writes=[pool4[0], tern_rs], tag=f"ta_{name}{k}")

        def tern_stats_reduce(kchunks, name, bank, cbase=0):
            ncc = kchunks
            op("tensor", lambda g, ncc=ncc, bank=bank, cbase=cbase: g.matmul(
                    bank.t[0:1, 0:ncc], c_ones.t[:, :],
                    tern_rs.t[:, cbase:cbase + ncc],
                    start=True, stop=True),
               reads=[c_ones, tern_rs], writes=[bank], tag=f"tsu_{name}")
            op("vector", lambda g, ncc=ncc, bank=bank: g.tensor_copy(
                    row_scr.t[0:1, 32:32 + ncc], bank.t[0:1, 0:ncc]),
               reads=[bank], writes=[row_scr], tag=f"tc_{name}")
            op("vector", lambda g, ncc=ncc: g.reduce_sum(
                    row_scr.t[0:1, 127:128], row_scr.t[0:1, 32:32 + ncc], axis=AX.X),
               reads=[row_scr], writes=[row_scr], tag=f"tr_{name}")
            inv_n = 1.0 / (kchunks * 128 * 2048)
            op("vector", lambda g, inv_n=inv_n: g.tensor_scalar(
                    row_scr.t[0:1, 126:127], row_scr.t[0:1, 127:128], inv_n, 1e-5,
                    A.mult, A.max),
               reads=[row_scr], writes=[row_scr], tag=f"tm_{name}")
            op("vector", lambda g: g.reciprocal(row_scr.t[0:1, 125:126],
                                                row_scr.t[0:1, 126:127]),
               reads=[row_scr], writes=[row_scr], tag=f"tre_{name}")
            op("vector", lambda g: g.tensor_scalar(row_scr.t[0:1, 124:125],
                                                   row_scr.t[0:1, 125:126], -1.0, None,
                                                   A.mult),
               reads=[row_scr], writes=[row_scr], tag=f"tn_{name}")
            op("tensor", lambda g, bank=bank: g.matmul(
                    bank.t[:, 16:19], c_onesr.t[0:1, :],
                    row_scr.t[0:1, 124:127], start=True, stop=True),
               reads=[c_onesr, row_scr], writes=[bank], tag=f"trp_{name}")
            op("scalar", lambda g, bank=bank: g.activation(
                    trep[name].t[:, 0:1], bank.t[:, 17:18], AF.Copy),
               reads=[bank], writes=[trep[name]], tag=f"tv1_{name}")
            op("scalar", lambda g, bank=bank: g.activation(
                    trep[name].t[:, 1:2], bank.t[:, 16:17], AF.Copy),
               reads=[bank], writes=[trep[name]], tag=f"tv2_{name}")
            op("scalar", lambda g, bank=bank: g.activation(
                    tmc[name].t[:, :], bank.t[:, 18:19], AF.Copy),
               reads=[bank], writes=[tmc[name]], tag=f"tv3_{name}")

        def tern_write(src_ap, dst_dram, kchunks, name, slots):
            ts_last = {}
            for k in range(kchunks):
                wf = slots[k % 2]
                tw = twt[k % 2]
                dma("scalar", f"wf_{wf.name}", wf.t[:, :],
                    src_ap[128 * k:128 * k + 128, :],
                    reads=[wf], writes=[wf], tag=f"tw_{name}{k}")
                op("vector", lambda g, wf=wf: g.tensor_scalar(
                        tern_p.t[:, :], wf.t[:, :], trep[name].t[:, 0:1],
                        0.5, A.mult, A.is_gt),
                   reads=[wf, trep[name]], writes=[tern_p], tag=f"tp_{name}{k}")
                op("vector", lambda g, wf=wf: g.tensor_scalar(
                        tern_n.t[:, :], wf.t[:, :], trep[name].t[:, 1:2],
                        0.5, A.mult, A.is_gt),
                   reads=[wf, trep[name]], writes=[tern_n], tag=f"tq_{name}{k}")
                op("vector", lambda g, tw=tw: g.tensor_tensor(
                        tw.t[:, :], tern_p.t[:, :], tern_n.t[:, :], A.subtract),
                   reads=[tern_p, tern_n], writes=[tw], tag=f"tt_{name}{k}")
                tk = dma("sync", f"ts{k % 2}_" + name,
                         dst_dram.t[128 * k:128 * k + 128, :], tw.t[:, :],
                         reads=[tw], deps=list(dst_dram.rtoks), tag=f"tso_{name}{k}")
                ts_last[k % 2] = tk
                dst_dram.wtok = [t for t in ts_last.values()]

        def quant_ladder(zh, c0h, ch, dh, qh, csc_buf, eng, width, tag):
            op(eng, lambda g, width=width: g.tensor_scalar(
                    c0h.t[:, 0:width], zh.t[:, 0:width], csc_buf.t[:, :], 128.5,
                    A.mult, A.add),
               reads=[zh, csc_buf], writes=[c0h], tag=tag + "_c0")
            cur = c0h
            for si, tk in enumerate(LADDER):
                op(eng, lambda g, tk=tk, cur=cur, width=width: g.tensor_scalar(
                        dh.t[:, 0:width], cur.t[:, 0:width], tk, -tk, A.is_ge, A.mult),
                   reads=[cur], writes=[dh], tag=tag + f"_d{si}")
                op(eng, lambda g, cur=cur, width=width: g.tensor_tensor(
                        ch.t[:, 0:width], cur.t[:, 0:width], dh.t[:, 0:width], A.add),
                   reads=[cur, dh], writes=[ch], tag=tag + f"_a{si}")
                cur = ch
            if eng == "vector":
                op(eng, lambda g, width=width: g.scalar_tensor_tensor(
                        qh.t[:, 0:width], c0h.t[:, 0:width], -128.0, ch.t[:, 0:width],
                        A.add, A.subtract),
                   reads=[c0h, ch], writes=[qh], tag=tag + "_q")
            else:
                op(eng, lambda g, width=width: g.tensor_scalar(
                        dh.t[:, 0:width], c0h.t[:, 0:width], -128.0, None, A.add),
                   reads=[c0h], writes=[dh], tag=tag + "_q1")
                op(eng, lambda g, width=width: g.tensor_tensor(
                        qh.t[:, 0:width], dh.t[:, 0:width], ch.t[:, 0:width],
                        A.subtract),
                   reads=[dh, ch], writes=[qh], tag=tag + "_q2")

        def quant_tile(hsrc, off0, off1, gain_buf, rrms_buf, rrms_ap, mc_name,
                       alph_st, jj, tag):
            width = off1 - off0
            op("vector", lambda g, off0=off0, off1=off1, width=width: g.tensor_tensor(
                    zq.t[:, 0:width], hsrc.t[:, off0:off1],
                    gain_buf.t[:, 0:width], A.mult),
               reads=[hsrc, gain_buf], writes=[zq], tag=f"{tag}_z")
            op("vector", lambda g, width=width: g.reduce_max(
                    rcol["m"].t[:, :], zq.t[:, 0:width], axis=AX.X,
                    apply_absolute_value=True),
               reads=[zq], writes=[rcol["m"]], tag=f"{tag}_m")
            op("vector", lambda g, rrms_ap=rrms_ap: g.tensor_scalar(
                    rcol["mm"].t[:, :], rcol["m"].t[:, :], rrms_ap, 1e-5, A.mult, A.max),
               reads=[rcol["m"], rrms_buf], writes=[rcol["mm"]], tag=f"{tag}_mm")
            op("vector", lambda g: g.reciprocal(rcol["rc"].t[:, :], rcol["mm"].t[:, :]),
               reads=[rcol["mm"]], writes=[rcol["rc"]], tag=f"{tag}_rc")
            op("vector", lambda g: g.tensor_scalar(rcol["s127"].t[:, :], rcol["rc"].t[:, :],
                                                   127.0, None, A.mult),
               reads=[rcol["rc"]], writes=[rcol["s127"]], tag=f"{tag}_s")
            op("vector", lambda g, rrms_ap=rrms_ap: g.tensor_scalar(
                    rcol["csc"].t[:, :], rcol["s127"].t[:, :], rrms_ap, None, A.mult),
               reads=[rcol["s127"], rrms_buf], writes=[rcol["csc"]], tag=f"{tag}_csc")
            op("vector", lambda g: g.tensor_scalar(rcol["atmp"].t[:, :], rcol["mm"].t[:, :],
                                                   1.0 / 127.0, None, A.mult),
               reads=[rcol["mm"]], writes=[rcol["atmp"]], tag=f"{tag}_i")
            op("vector", lambda g, jj=jj, alph_st=alph_st: g.tensor_tensor(
                    alph_st.t[:, jj:jj + 1], rcol["atmp"].t[:, :], tmc[mc_name].t[:, :],
                    A.mult),
               reads=[rcol["atmp"], tmc[mc_name]], writes=[alph_st], tag=f"{tag}_al")
            quant_ladder(zq, c0q, zq, ddq, qtm, rcol["csc"], "vector", width, tag + "_L")

        def transpose_q_to_fm(dst_fm, jj, kchunks, tag):
            for k in range(kchunks):
                sl8 = 128 * (k % 8)
                op("tensor", lambda g, k=k, sl8=sl8: g.transpose(
                        pQ.t[:, sl8:sl8 + 128], qtm.t[:, 128 * k:128 * k + 128],
                        c_idbf.t[:, :]),
                   reads=[qtm, c_idbf], writes=[pQ],
                   tag=f"{tag}_T{k}")
                if (k % 8) == 7 or k == kchunks - 1:
                    k0 = (k // 8) * 8
                    nblk = k - k0 + 1
                    dst = dst_fm.t[:, :].rearrange("p (a b) -> p a b", b=1024)[
                        :, k0:k0 + nblk, 128 * jj:128 * jj + 128]
                    op("scalar", lambda g, dst=dst, nblk=nblk: g.activation(
                            dst, pQ.t[:, 0:128 * nblk], AF.Copy),
                       reads=[pQ], writes=[dst_fm], tag=f"{tag}_ev{k0}")


        def batch_rsqrt(src_buf, eps_buf, dst_buf, tag, scale_buf=None):
            def fn(g):
                kw = {}
                if scale_buf is not None:
                    kw["scale"] = scale_buf.t[:, :]
                return g.activation(st["tb1"].t[:, :], src_buf.t[:, :], AF.Sqrt,
                                    bias=eps_buf.t[:, :], **kw)
            rd = [src_buf, eps_buf] + ([scale_buf] if scale_buf is not None else [])
            op("scalar", fn, reads=rd, writes=[st["tb1"]], tag=tag + "_sq")
            op("vector", lambda g: g.reciprocal(dst_buf.t[:, :], st["tb1"].t[:, :]),
               reads=[st["tb1"]], writes=[dst_buf], tag=tag + "_rc")

        def stats_tile(xa, jj, s1_st, s2_st, tag):
            op("scalar", lambda g, xa=xa, jj=jj, s1_st=s1_st: g.activation(
                    twt[0].t[:, :], xa.t[:, :], AF.Copy,
                    accum_out=s1_st.t[:, jj:jj + 1]),
               reads=[xa], writes=[twt[0], s1_st], tag=f"{tag}_s1")
            op("scalar", lambda g, xa=xa, jj=jj, s2_st=s2_st: g.activation(
                    twt[1].t[:, :], xa.t[:, :], AF.Square,
                    accum_out=s2_st.t[:, jj:jj + 1]),
               reads=[xa], writes=[twt[1], s2_st], tag=f"{tag}_s2")

        def batch_ln_math(s1_st, s2_st, tag, want_q=True):
            op("vector", lambda g, s1_st=s1_st: g.tensor_scalar(
                    st["mu"].t[:, :], s1_st.t[:, :], 1.0 / H, None, A.mult),
               reads=[s1_st], writes=[st["mu"]], tag=tag + "_mu")
            op("vector", lambda g, s2_st=s2_st: g.tensor_scalar(
                    st["t18"].t[:, :], s2_st.t[:, :], 1.0 / H, None, A.mult),
               reads=[s2_st], writes=[st["t18"]], tag=tag + "_t1")
            op("vector", lambda g: g.tensor_tensor(
                    st["tb1"].t[:, :], st["mu"].t[:, :], st["mu"].t[:, :], A.mult),
               reads=[st["mu"]], writes=[st["tb1"]], tag=tag + "_m2")
            op("vector", lambda g: g.tensor_tensor(
                    st["var"].t[:, :], st["t18"].t[:, :], st["tb1"].t[:, :],
                    A.subtract),
               reads=[st["t18"], st["tb1"]], writes=[st["var"]], tag=tag + "_var")
            batch_rsqrt(st["var"], epsb["e5"], st["rstd"], tag + "_rsq")
            if want_q:
                op("vector", lambda g: g.tensor_tensor(
                        st["tb1"].t[:, :], st["var"].t[:, :], st["rstd"].t[:, :],
                        A.mult),
                   reads=[st["var"], st["rstd"]], writes=[st["tb1"]], tag=tag + "_q1")
                op("vector", lambda g: g.tensor_tensor(
                        st["rmsq"].t[:, :], st["tb1"].t[:, :], st["rstd"].t[:, :],
                        A.mult),
                   reads=[st["tb1"], st["rstd"]], writes=[st["rmsq"]], tag=tag + "_q2")
                if want_q is True:
                    batch_rsqrt(st["rmsq"], epsb["e6"], st["rstdr"], tag + "_rsr")
                batch_rsqrt(st["rmsq"], epsb["e8"], st["rstdrms"], tag + "_rsm")
            op("vector", lambda g: g.tensor_tensor(
                    st["tb1"].t[:, :], st["mu"].t[:, :], st["rstd"].t[:, :], A.mult),
               reads=[st["mu"], st["rstd"]], writes=[st["tb1"]], tag=tag + "_n1")
            op("vector", lambda g: g.tensor_scalar(
                    st["nmr8"].t[:, :], st["tb1"].t[:, :], -1.0, None, A.mult),
               reads=[st["tb1"]], writes=[st["nmr8"]], tag=tag + "_n2")

        def mlp_post_a(jj, ji, tag):
            yb = pool4[ji]
            op("scalar", lambda g, yb=yb: g.activation(lnt.t[:, 0:II], yb.t[:, 0:II],
                                                       AF.Silu),
               reads=[yb], writes=[lnt], tag=f"{tag}_silu")
            op("vector", lambda g, yb=yb: g.tensor_tensor(
                    yb.t[:, 0:II], lnt.t[:, 0:II], yb.t[:, II:2 * II], A.mult),
               reads=[lnt, yb], writes=[yb], tag=f"{tag}_h")
            op("scalar", lambda g, yb=yb, jj=jj: g.activation(
                    ddq.t[:, 0:II], yb.t[:, 0:II], AF.Square,
                    accum_out=st["rmsE"].t[:, jj:jj + 1]),
               reads=[yb], writes=[ddq, st["rmsE"]], tag=f"{tag}_h2")

        def mlp_post_b(jj, ji, osc_st, gain_i, mc_d, qdst, tag):
            quant_tile(pool4[ji], 0, II, gain_i, st["rrE"],
                       st["rrE"].t[:, jj:jj + 1], mc_d, osc_st, jj, tag + "_q2")
            transpose_q_to_fm(qdst, jj, IC, f"{tag}_qT")

        def mlp_post_group(jjg, osc_st, gain_i, mc_d, qdst, tag):
            for ji, jj in enumerate(jjg):
                mlp_post_a(jj, ji, f"{tag}a{jj}")
            batch_rsqrt(st["rmsE"], epsb["e8"], st["rrE"], f"{tag}_rr",
                        scale_buf=epsb["invII"])
            for ji, jj in enumerate(jjg):
                mlp_post_b(jj, ji, osc_st, gain_i, mc_d, qdst, f"{tag}b{jj}")




        def mlp_mm(qfm, kchunks, wt_dram, n_out, jjgrp, evac_cb, tag,
                   banks=(0, 1, 2, 3), defer_qfm=False):
            toks = []
            nblocks = n_out // 512
            for nb in range(nblocks):
                for k in range(kchunks):
                    wt = wtile[(nb * kchunks + k) % 8]
                    dma("sync", f"wt{(nb * kchunks + k) % 8}", wt.t[:, :],
                        wt_dram.t[128 * k:128 * k + 128, 512 * nb:512 * nb + 512],
                        reads=[wt_dram, wt], writes=[wt], tag=f"{tag}_w{nb}_{k}")
                    for ji, jj in enumerate(jjgrp):
                        bank = pb[banks[ji]]
                        fn = lambda g, k=k, jj=jj, wt=wt, bank=bank, qfm=qfm, \
                            kchunks=kchunks: g.matmul(
                                bank.t[:, :],
                                qfm.t[:, 1024 * k + 128 * jj:1024 * k + 128 * jj + 128],
                                wt.t[:, :],
                                start=(k == 0), stop=(k == kchunks - 1))
                        if defer_qfm:
                            t_ = op("tensor", fn, reads=[wt], writes=[bank],
                                    deps=list(qfm.wtok), tag=f"{tag}_mm{nb}_{k}_{jj}")
                            toks.append(t_)
                        else:
                            op("tensor", fn, reads=[qfm, wt], writes=[bank],
                               tag=f"{tag}_mm{nb}_{k}_{jj}")
                for ji, jj in enumerate(jjgrp):
                    evac_cb(jj, ji, 512 * nb, pb[banks[ji]])
            return toks

        def sh_mm1_evac(jj, ji, n0, pbuf):
            op("scalar", lambda g, jj=jj, ji=ji, n0=n0, pbuf=pbuf: g.activation(
                    pool4[ji].t[:, n0:n0 + 512], pbuf.t[:, :], AF.Copy,
                    scale=st["alph"].t[:, jj:jj + 1]),
               reads=[pbuf, st["alph"]], writes=[pool4[ji]], tag=f"shev{jj}_{n0}")

        for jj in range(NT):
            xa = xtile[jj % 2]
            dma("sync", f"xt{jj % 2}", xa.t[:, :], x_own[128 * jj:128 * jj + 128, :],
                reads=[xa], writes=[xa], tag=f"xsload{jj}")
            stats_tile(xa, jj, st["s1x"], st["s2x"], f"Rs{jj}")
            tern_stats_stream(sWg_d, HC, "sWg", wfb, cbase=0, k0=2 * jj,
                              k1=2 * jj + 2)
        batch_ln_math(st["s1x"], st["s2x"], "Rb")
        tern_stats_reduce(HC, "sWg", pb[5], cbase=0)
        tern_write(sWg_d, t_sWg, HC, "sWg", wfb)

        def pass2_tile(jj):
            xa = xtile[jj % 2]
            dma("sync", f"xt{jj % 2}", xa.t[:, :], x_own[128 * jj:128 * jj + 128, :],
                reads=[xa], writes=[xa], tag=f"xload{jj}")
            rstdc = st["rstd"].t[:, jj:jj + 1]
            op("scalar", lambda g, xa=xa, rstdc=rstdc, jj=jj: g.activation(
                    lnt.t[:, :], xa.t[:, :], AF.Identity,
                    bias=st["nmr8"].t[:, jj:jj + 1], scale=rstdc),
               reads=[xa, st["rstd"], st["nmr8"]], writes=[lnt], tag=f"R{jj}_ln1")

            lgb = pb[0] if jj < 4 else pb[1]
            sgb = pb[2] if jj < 4 else pb[3]
            off = 128 * (jj % 4)
            for k in range(HC):
                sl4 = 128 * (k % 4)
                op("tensor", lambda g, k=k, sl4=sl4: g.transpose(
                        pb[4].t[:, sl4:sl4 + 128], lnt.t[:, 128 * k:128 * k + 128],
                        c_idf.t[:, :]),
                   reads=[lnt, c_idf], writes=[pb[4]],
                    tag=f"R{jj}_T{k}")
                if (k % 4) == 3:
                    op("scalar", lambda g: g.activation(lf.t[:, :], pb[4].t[:, :], AF.Copy),
                       reads=[pb[4]], writes=[lf], tag=f"R{jj}_lfe{k}")
                    for kk in range(k - 3, k + 1):
                        op("tensor", lambda g, kk=kk, lgb=lgb, off=off: g.matmul(
                                lgb.t[0:8, off:off + 128],
                                wgp.t[:, 8 * kk:8 * kk + 8],
                                lf.t[:, 128 * (kk % 4):128 * (kk % 4) + 128],
                                start=(kk == 0), stop=(kk == HC - 1)),
                           reads=[wgp, lf], writes=[lgb],
                            tag=f"R{jj}_lg{kk}")
                        op("tensor", lambda g, kk=kk, sgb=sgb, off=off: g.matmul(
                                sgb.t[0:1, off:off + 128],
                                wsg_sb.t[:, kk:kk + 1],
                                lf.t[:, 128 * (kk % 4):128 * (kk % 4) + 128],
                                start=(kk == 0), stop=(kk == HC - 1)),
                           reads=[wsg_sb, lf], writes=[sgb],
                            tag=f"R{jj}_sg{kk}")

            quant_tile(lnt, 0, H, gain2048, st["rstdrms"], st["rstdrms"].t[:, jj:jj + 1],
                       "sWg", st["alph"], jj, f"R{jj}_q1")
            transpose_q_to_fm(qfm_a, jj, HC, f"R{jj}_q1T")

        stats_specs = [(sWd_d, IC, "sWd", 16), (Wg_d, HC, "Wg", 0),
                       (Wd_d, IC, "Wd", 16)]
        chunk_plan = []
        for srcp, kc, nm, cb in stats_specs:
            chunk_plan += [(srcp, kc, nm, cb, k) for k in range(kc)]
        for jj in range(4):
            pass2_tile(jj)
            for (srcp, kc, nm, cb, k) in chunk_plan[8 * jj:8 * jj + 8]:
                tern_stats_stream(srcp, kc, nm, wfb, cbase=cb, k0=k, k1=k + 1)
            if jj == 0:
                tern_stats_reduce(IC, "sWd", pb[5], cbase=16)
            elif jj == 2:
                tern_stats_reduce(HC, "Wg", pb[5], cbase=0)
            elif jj == 3:
                tern_stats_reduce(IC, "Wd", pb[5], cbase=16)
        op("vector", lambda g: g.tensor_copy(lgf.t[:, 0:512], pb[0].t[0:8, :]),
           reads=[pb[0]], writes=[lgf], tag="lg_ev0")
        op("vector", lambda g: g.tensor_copy(sgr.t[0:1, 0:512], pb[2].t[0:1, :]),
           reads=[pb[2]], writes=[sgr], tag="sg_ev0")
        mm1_toks = mlp_mm(qfm_a, HC, t_sWg, 2 * II, [0, 1, 2, 3], sh_mm1_evac,
                          "shmm1g0", banks=(0, 2, 5, 6), defer_qfm=True)
        for _w in wfb:
            qfm_b.rtoks += _w.rtoks + _w.wtok
        mlp_post_group([0, 1, 2, 3], st["osc"], gain1024, "sWd", qfm_b, "shp0")
        if want_tap("h0"):
            dma("sync", "tpG", tap_aps["h0"], pool4[0].t[:, 0:II],
                reads=[pool4[0]], tag="tp_h0")
        if want_tap("rrE0"):
            dma("sync", "tpH", tap_aps["rrE0"], st["rrE"].t[:, :],
                reads=[st["rrE"]], tag="tp_rrE0")
        if want_tap("y0"):
            dma("sync", "tpF", tap_aps["y0"], pool4[0].t[:, :], reads=[pool4[0]],
                tag="tp_y0")
        for jj in range(4, NT):
            pass2_tile(jj)
        op("vector", lambda g: g.tensor_copy(lgf.t[:, 512:1024], pb[1].t[0:8, :]),
           reads=[pb[1]], writes=[lgf], tag="lg_ev1")
        op("vector", lambda g: g.tensor_copy(sgr.t[0:1, 512:1024], pb[3].t[0:1, :]),
           reads=[pb[3]], writes=[sgr], tag="sg_ev1")
        mm1_toks += mlp_mm(qfm_a, HC, t_sWg, 2 * II, [4, 5, 6, 7], sh_mm1_evac,
                           "shmm1g4", banks=(0, 1, 2, 3), defer_qfm=True)
        qfm_a.rtoks += mm1_toks
        mlp_post_group([4, 5, 6, 7], st["osc"], gain1024, "sWd", qfm_b, "shp4")

        for j in range(NT):
            op("tensor", lambda g, j=j: g.transpose(
                    pb[4].t[:, 8 * j:8 * j + 8], lgf.t[0:8, 128 * j:128 * j + 128],
                    c_idf.t[0:8, 0:8]),
               reads=[lgf, c_idf], writes=[pb[4]],
                tag=f"lgT{j}")
        op("vector", lambda g: g.tensor_copy(ltm.t[:, :], pb[4].t[:, 0:64]),
           reads=[pb[4]], writes=[ltm], tag="ltm_ev")
        for j in range(NT):
            op("tensor", lambda g, j=j: g.transpose(
                    pb[4].t[:, j:j + 1], sgr.t[0:1, 128 * j:128 * j + 128],
                    c_idf.t[0:1, 0:1]),
               reads=[sgr, c_idf], writes=[pb[4]],
                tag=f"sgT{j}")
        op("vector", lambda g: g.tensor_copy(sgv.t[:, :], pb[4].t[:, 0:8]),
           reads=[pb[4]], writes=[sgv], tag="sgv_ev")

        for jj in range(NT):
            rrc = st["rstdr"].t[:, jj:jj + 1]
            op("vector", lambda g, jj=jj, rrc=rrc: g.tensor_scalar(
                    r8["p"].t[:, :], ltm.t[:, 8 * jj:8 * jj + 8], rrc, None, A.mult),
               reads=[ltm, st["rstdr"]], writes=[r8["p"]], tag=f"rt{jj}_l")
            op("vector", lambda g: g.reduce_max(rcol["nmax"].t[:, :], r8["p"].t[:, :],
                                                axis=AX.X, negate=True),
               reads=[r8["p"]], writes=[rcol["nmax"]], tag=f"rt{jj}_nm")
            op("scalar", lambda g: g.activation(r8["eq"].t[:, :], r8["p"].t[:, :], AF.Exp,
                                                bias=rcol["nmax"].t[:, :],
                                                accum_out=rcol["ssum"].t[:, :]),
               reads=[r8["p"], rcol["nmax"]], writes=[r8["eq"], rcol["ssum"]],
               tag=f"rt{jj}_exp")
            op("vector", lambda g: g.reciprocal(rcol["rs"].t[:, :], rcol["ssum"].t[:, :]),
               reads=[rcol["ssum"]], writes=[rcol["rs"]], tag=f"rt{jj}_rs")
            op("vector", lambda g: g.tensor_scalar(r8["p"].t[:, :], r8["eq"].t[:, :],
                                                   rcol["rs"].t[:, :], None, A.mult),
               reads=[r8["eq"], rcol["rs"]], writes=[r8["p"]], tag=f"rt{jj}_p")
            op("vector", lambda g, jj=jj: g.reduce_max(st["m1"].t[:, jj:jj + 1],
                                                       r8["p"].t[:, :], axis=AX.X),
               reads=[r8["p"]], writes=[st["m1"]], tag=f"rt{jj}_m1")
            m1c = st["m1"].t[:, jj:jj + 1]
            op("vector", lambda g, m1c=m1c: g.tensor_scalar(
                    r8["eq"].t[:, :], r8["p"].t[:, :], m1c, None, A.is_ge),
               reads=[r8["p"], st["m1"]], writes=[r8["eq"]], tag=f"rt{jj}_eq")
            op("vector", lambda g: g.tensor_tensor_scan(
                    r8["cs"].t[:, :], r8["eq"].t[:, :], z8.t[:, :], 0.0, A.add, A.add),
               reads=[r8["eq"], z8], writes=[r8["cs"]], tag=f"rt{jj}_cs")
            op("vector", lambda g: g.tensor_scalar(r8["c1"].t[:, :], r8["cs"].t[:, :],
                                                   1.0, None, A.is_le),
               reads=[r8["cs"]], writes=[r8["c1"]], tag=f"rt{jj}_c1")
            op("vector", lambda g, jj=jj: g.tensor_tensor(
                    stf1[jj].t[:, :], r8["eq"].t[:, :], r8["c1"].t[:, :], A.mult),
               reads=[r8["eq"], r8["c1"]], writes=[stf1[jj]], tag=f"rt{jj}_f1")
            op("vector", lambda g, jj=jj: g.scalar_tensor_tensor(
                    r8["p2"].t[:, :], stf1[jj].t[:, :], -1e30, r8["p"].t[:, :],
                    A.mult, A.add),
               reads=[stf1[jj], r8["p"]], writes=[r8["p2"]], tag=f"rt{jj}_p2")
            op("vector", lambda g, jj=jj: g.reduce_max(st["m2"].t[:, jj:jj + 1],
                                                       r8["p2"].t[:, :], axis=AX.X),
               reads=[r8["p2"]], writes=[st["m2"]], tag=f"rt{jj}_m2")
            m2c = st["m2"].t[:, jj:jj + 1]
            op("vector", lambda g, m2c=m2c: g.tensor_scalar(
                    r8["eq2"].t[:, :], r8["p2"].t[:, :], m2c, None, A.is_ge),
               reads=[r8["p2"], st["m2"]], writes=[r8["eq2"]], tag=f"rt{jj}_eq2")
            op("vector", lambda g: g.tensor_tensor_scan(
                    r8["cs2"].t[:, :], r8["eq2"].t[:, :], z8.t[:, :], 0.0, A.add, A.add),
               reads=[r8["eq2"], z8], writes=[r8["cs2"]], tag=f"rt{jj}_cs2")
            op("vector", lambda g: g.tensor_scalar(r8["c2"].t[:, :], r8["cs2"].t[:, :],
                                                   1.0, None, A.is_le),
               reads=[r8["cs2"]], writes=[r8["c2"]], tag=f"rt{jj}_c2")
            op("vector", lambda g, jj=jj: g.tensor_tensor(
                    stf2[jj].t[:, :], r8["eq2"].t[:, :], r8["c2"].t[:, :], A.mult),
               reads=[r8["eq2"], r8["c2"]], writes=[stf2[jj]], tag=f"rt{jj}_f2")
            op("vector", lambda g, jj=jj, m1c=m1c: g.tensor_scalar(
                    r8["tmp"].t[:, :], stf1[jj].t[:, :], m1c, None, A.mult),
               reads=[stf1[jj], st["m1"]], writes=[r8["tmp"]], tag=f"rt{jj}_wm1")
            op("vector", lambda g, jj=jj, m2c=m2c: g.scalar_tensor_tensor(
                    r8["wm"].t[:, :], stf2[jj].t[:, :], m2c, r8["tmp"].t[:, :],
                    A.mult, A.add),
               reads=[stf2[jj], st["m2"], r8["tmp"]], writes=[r8["wm"]], tag=f"rt{jj}_wm")
            op("tensor", lambda g, jj=jj: g.transpose(
                    pb[4].t[0:8, 128 * (jj % 4):128 * (jj % 4) + 128],
                    r8["wm"].t[:, :], c_idf.t[:, :]),
               reads=[r8["wm"], c_idf], writes=[pb[4]],
                tag=f"rt{jj}_wmT")
            if (jj % 4) == 3:
                j0 = jj - 3
                op("vector", lambda g, j0=j0: g.tensor_copy(
                        wm_sb.t[0:8, 128 * j0:128 * j0 + 512], pb[4].t[0:8, 0:512]),
                   reads=[pb[4]], writes=[wm_sb], tag=f"rt{jj}_wmev")
        op("scalar", lambda g: g.activation(st["gate"].t[:, :], sgv.t[:, :], AF.Sigmoid),
           reads=[sgv], writes=[st["gate"]], tag="gates")

        dma("gpsimd", "wm", wm_bounce.t, wm_sb.t[0:8, :], reads=[wm_sb], writes=[wm_bounce],
            tag="wm_st")
        P.op("gpsimd", lambda g: g.collective_compute(
                "AllGather", A.bypass, replica_groups=[list(range(NCORE))],
                ins=[wm_bounce.t.opt()], outs=[wm_all.t.opt()]),
             reads=[wm_bounce], writes=[wm_all], sem="ag1", tag="AG1")

        if want_tap("wm_all"):
            dma("gpsimd", "tp1", tap_aps["wm_all"], wm_all.t, reads=[wm_all], tag="tp_wma")
        if want_tap("gate"):
            dma("sync", "tp2", tap_aps["gate"], st["gate"].t[:, :], reads=[st["gate"]],
                tag="tp_gate")
        if want_tap("alph"):
            dma("sync", "tp3", tap_aps["alph"], st["alph"].t[:, :], reads=[st["alph"]],
                tag="tp_alph")
        if want_tap("q1fm"):
            dma("sync", "tp4", tap_aps["q1fm"], qfm_a.t[:, :], reads=[qfm_a], tag="tp_q1fm")

        tern_write(sWd_d, t_sWd, IC, "sWd", xtile)
        tern_write(Wg_d, t_Wg, HC, "Wg", xtile)
        tern_write(Wd_d, t_Wd, IC, "Wd", xtile)

        x1a_last = {}

        def sh_mm2_evac(jj, ji, n0, pbuf):
            acc = pool4[ji]
            op("scalar", lambda g, jj=jj, n0=n0, pbuf=pbuf, acc=acc: g.activation(
                    acc.t[:, n0:n0 + 512], pbuf.t[:, :], AF.Copy,
                    scale=st["osc"].t[:, jj:jj + 1]),
               reads=[pbuf, st["osc"]], writes=[acc], tag=f"s2e{jj}_{n0}")
            if n0 == 1024 and ji < 2:
                xs = xtile[jj % 2]
                dma("sync", f"xt{jj % 2}", xs.t[:, :],
                    x_own[128 * jj:128 * jj + 128, :],
                    reads=[xs], writes=[xs], tag=f"s2x{jj}")
            if n0 == 1536:
                xs = xtile[jj % 2]
                if ji >= 2:
                    dma("sync", f"xt{jj % 2}", xs.t[:, :],
                        x_own[128 * jj:128 * jj + 128, :],
                        reads=[xs], writes=[xs], tag=f"s2x{jj}")
                op("vector", lambda g, xs=xs, acc=acc: g.tensor_tensor(
                        acc.t[:, :], acc.t[:, :], xs.t[:, :], A.add),
                   reads=[xs, acc], writes=[acc], tag=f"s2a{jj}")
                tk = dma("sync", f"x1a{ji}", x1_dram.t[128 * jj:128 * jj + 128, :],
                         acc.t[:, :], reads=[acc], deps=list(x1_dram.rtoks),
                         tag=f"s2w{jj}")
                x1a_last[ji] = tk
                x1_dram.wtok = [t for t in x1a_last.values()]

        op("vector", lambda g: g.tensor_tensor(st["osc"].t[:, :], st["osc"].t[:, :],
                                               st["gate"].t[:, :], A.mult),
           reads=[st["osc"], st["gate"]], writes=[st["osc"]], tag="oscg")

        for grp in (0, 4):
            jjg = list(range(grp, grp + 4))
            mlp_mm(qfm_b, IC, t_sWd, H, jjg, sh_mm2_evac, f"shmm2g{grp}")

        dma("sync", "wa", wa.t[:, :], wm_all.t, reads=[wm_all], writes=[wa], tag="wa_ld")
        op("vector", lambda g: g.tensor_scalar(mk_sb.t[:, :], wa.t[:, :], 0.0, None,
                                               A.is_gt),
           reads=[wa], writes=[mk_sb], tag="mk")
        op("vector", lambda g: g.memset(wa.t[:, :], 0.0), reads=[wa], writes=[wa],
           tag="waz")
        op("vector", lambda g: g.tensor_tensor_scan(
                ci_sb.t[:, :], mk_sb.t[:, :], wa.t[:, :], 0.0, A.add, A.add),
           reads=[mk_sb, wa], writes=[ci_sb], tag="scan")
        for hh in range(2):
            sl = slice(512 * hh, 512 * hh + 512)
            bank = pb[0] if hh == 0 else pb[1]
            op("tensor", lambda g, sl=sl, bank=bank: g.matmul(
                    bank.t[0:8, :], c_selaU.t[0:64, :], ci_sb.t[:, sl],
                    start=True, stop=False),
               reads=[c_selaU, ci_sb], writes=[bank], tag=f"own1_{hh}")
            op("tensor", lambda g, sl=sl, bank=bank: g.matmul(
                    bank.t[0:8, :], c_selgt.t[0:64, :], mk_sb.t[:, sl],
                    start=False, stop=True),
               reads=[c_selgt, mk_sb], writes=[bank], tag=f"own2_{hh}")
            bank2 = pb[2] if hh == 0 else pb[3]
            op("tensor", lambda g, sl=sl, bank2=bank2: g.matmul(
                    bank2.t[0:8, :], c_selaE.t[0:64, :], ci_sb.t[:, sl],
                    start=True, stop=False),
               reads=[c_selaE, ci_sb], writes=[bank2], tag=f"exp1_{hh}")
            op("tensor", lambda g, sl=sl, bank2=bank2: g.matmul(
                    bank2.t[0:8, :], c_selgE.t[0:64, :], mk_sb.t[:, sl],
                    start=False, stop=True),
               reads=[c_selgE, mk_sb], writes=[bank2], tag=f"exp2_{hh}")
            bank3 = pb[5] if hh == 0 else pb[6]
            op("tensor", lambda g, sl=sl, bank3=bank3: g.matmul(
                    bank3.t[0:8, :], c_selidE.t[0:64, :], mk_sb.t[:, sl],
                    start=True, stop=True),
               reads=[c_selidE, mk_sb], writes=[bank3], tag=f"mke_{hh}")
        op("vector", lambda g: g.tensor_copy(inclo.t[:, 0:512], pb[0].t[0:8, :]),
           reads=[pb[0]], writes=[inclo], tag="io0")
        op("vector", lambda g: g.tensor_copy(inclo.t[:, 512:1024], pb[1].t[0:8, :]),
           reads=[pb[1]], writes=[inclo], tag="io1")
        op("vector", lambda g: g.tensor_copy(incle.t[:, 0:512], pb[2].t[0:8, :]),
           reads=[pb[2]], writes=[incle], tag="ie0")
        op("vector", lambda g: g.tensor_copy(incle.t[:, 512:1024], pb[3].t[0:8, :]),
           reads=[pb[3]], writes=[incle], tag="ie1")
        op("vector", lambda g: g.tensor_copy(mke.t[:, 0:512], pb[5].t[0:8, :]),
           reads=[pb[5]], writes=[mke], tag="me0")
        op("vector", lambda g: g.tensor_copy(mke.t[:, 512:1024], pb[6].t[0:8, :]),
           reads=[pb[6]], writes=[mke], tag="me1")
        op("vector", lambda g: g.tensor_scalar(wa.t[0:8, 0:1024], incle.t[:, :], float(CAP),
                                               None, A.is_le),
           reads=[incle], writes=[wa], tag="vE1")
        op("vector", lambda g: g.tensor_tensor(wa.t[0:8, 0:1024], wa.t[0:8, 0:1024], mke.t[:, :],
                                               A.mult),
           reads=[wa, mke], writes=[wa], tag="vE2")
        op("vector", lambda g: g.tensor_scalar(incle.t[:, :], incle.t[:, :], -1.0, None,
                                               A.add),
           reads=[incle], writes=[incle], tag="posE")
        op("vector", lambda g: g.tensor_tensor(scatf.t[:, :], incle.t[:, :],
                                               wa.t[0:8, 0:1024], A.mult),
           reads=[incle, wa], writes=[scatf], tag="posEm")
        op("vector", lambda g: g.tensor_scalar(wa.t[0:8, 0:1024], wa.t[0:8, 0:1024], -BIGF, BIGF,
                                               A.mult, A.add),
           reads=[wa], writes=[wa], tag="invE")
        op("vector", lambda g: g.tensor_tensor(scatf.t[:, :], scatf.t[:, :],
                                               wa.t[0:8, 0:1024], A.add),
           reads=[scatf, wa], writes=[scatf], tag="scatf")
        for j in range(8):
            op("tensor", lambda g, j=j: g.transpose(
                    pb[4].t[:, 8 * j:8 * j + 8], scatf.t[0:8, 128 * j:128 * j + 128],
                    c_idf.t[0:8, 0:8]),
               reads=[scatf, c_idf], writes=[pb[4]],
                tag=f"scT{j}")
        op("vector", lambda g: g.tensor_copy(scat_tm.t[:, :], pb[4].t[:, 0:64]),
           reads=[pb[4]], writes=[scat_tm], tag="sc_ev")
        op("vector", lambda g: g.tensor_copy(scat_i.t[:, :], scat_tm.t[:, :]),
           reads=[scat_tm], writes=[scat_i], tag="sc_i32")
        for j in range(8):
            op("tensor", lambda g, j=j: g.transpose(
                    pb[4].t[:, 8 * j:8 * j + 8], inclo.t[0:8, 128 * j:128 * j + 128],
                    c_idf.t[0:8, 0:8]),
               reads=[inclo, c_idf], writes=[pb[4]],
                tag=f"ioT{j}")
        op("vector", lambda g: g.tensor_copy(inclo_tm.t[:, :], pb[4].t[:, 0:64]),
           reads=[pb[4]], writes=[inclo_tm], tag="io_ev")

        if want_tap("scat_tm"):
            dma("sync", "tp5", tap_aps["scat_tm"], scat_tm.t[:, :], reads=[scat_tm], tag="tp_sc")
        if want_tap("inclo_tm"):
            dma("sync", "tp6", tap_aps["inclo_tm"], inclo_tm.t[:, :], reads=[inclo_tm],
                tag="tp_io")

        op("vector", lambda g: g.memset(dfill.t[:, :], T), writes=[dfill], tag="dfill")
        dma("gpsimd", "dd", disp_dram.t.rearrange("(p c) one -> p (c one)", p=128),
            dfill.t[:, :], reads=[dfill], writes=[disp_dram], tag="disp_pre")
        pre_tok = disp_dram.wtok
        last_tok = None
        for rr in range(8):
            for j in range(8):
                col = 8 * j + rr
                ic = 8 * rr + j
                last_tok = P.op("gpsimd", lambda g, col=col, ic=ic: g.indirect_dma_start(
                        disp_dram.t,
                        IndirectOffsetOnAxis(ap=scat_i.t[:, col:col + 1], axis=0),
                        c_iota.t[:, ic:ic + 1], None,
                        bounds_check=REGS["cap"], oob_is_err=False),
                     reads=[scat_i, c_iota], deps=pre_tok, dma=True, sem="dd",
                     tag=f"dsp{rr}_{j}")
        disp_dram.wtok = [last_tok]
        for g8 in range(8):
            tk = dma("sync", "dsp", dsp_sb.t[:, g8:g8 + 1],
                     disp_dram.t[128 * g8:128 * g8 + 128, :],
                     reads=[disp_dram], tag=f"dspld{g8}")
            dsp_sb.wtok = [tk]

        if want_tap("disp"):
            dma("sync", "tp11", tap_aps["disp"], dsp_sb.t[:, :], reads=[dsp_sb], tag="tp_dsp")

        if upto == "route":
            return finish()



        if want_tap("x_shared"):
            dma("sync", "tp12", tap_aps["x_shared"], x1_dram.t, reads=[x1_dram], tag="tp_xs")
        if want_tap("q2fm"):
            dma("sync", "tpD", tap_aps["q2fm"], qfm_b.t[:, :], reads=[qfm_b],
                tag="tp_q2fm")
        if want_tap("osc8"):
            dma("sync", "tpE", tap_aps["osc8"], st["osc"].t[:, :],
                reads=[st["osc"]], tag="tp_osc8")
        if upto == "shared":
            return finish()

        replicate_row(gain2048, Ng_r, H, "ng")
        replicate_row(gain1024, Nd_r, II, "nd")

        def ex_mm1_evac(jj, ji, n0, pbuf):
            op("scalar", lambda g, ji=ji, jj=jj, n0=n0, pbuf=pbuf: g.activation(
                    pool4[ji].t[:, n0:n0 + 512], pbuf.t[:, :], AF.Copy,
                    scale=ste["alph"].t[:, jj:jj + 1]),
               reads=[pbuf, ste["alph"]], writes=[pool4[ji]], tag=f"exev{jj}_{n0}")

        eo_last = {}

        def ex_mm2_evac(jj, ji, n0, pbuf):
            acc = pool4[ji]
            op("scalar", lambda g, jj=jj, n0=n0, pbuf=pbuf, acc=acc: g.activation(
                    acc.t[:, n0:n0 + 512], pbuf.t[:, :], AF.Copy,
                    scale=ste["osc"].t[:, jj:jj + 1]),
               reads=[pbuf, ste["osc"]], writes=[acc], tag=f"e2e{jj}_{n0}")
            if n0 == 1536:
                tk = dma("sync", f"eo{ji}", eo_bounce.t[128 * jj:128 * jj + 128, :],
                         acc.t[:, :], reads=[acc], deps=list(eo_bounce.rtoks),
                         tag=f"e2s{jj}")
                eo_last[ji] = tk
                eo_bounce.wtok = [t for t in eo_last.values()]

        for grp in (0, 4):
            jjg = list(range(grp, grp + 4))
            for ji, jj in enumerate(jjg):
                op("vector", lambda g, ji=ji: g.memset(pool4[ji].t[:, :], 0.0),
                   writes=[pool4[ji]], tag=f"ex{grp}_ms{ji}")
                P.op("gpsimd", lambda g, ji=ji, jj=jj: g.indirect_dma_start(
                        pool4[ji].t[:, :], None,
                        x_full, IndirectOffsetOnAxis(ap=dsp_sb.t[:, jj:jj + 1], axis=0),
                        bounds_check=REGS["t"], oob_is_err=False),
                     reads=[dsp_sb], writes=[pool4[ji]], dma=True, sem=f"g{ji}",
                     tag=f"ex{grp}_g{jj}")
            for ji, jj in enumerate(jjg):
                stats_tile(pool4[ji], jj, st["s1x"], st["s2x"], f"exs{jj}")
            batch_ln_math(st["s1x"], st["s2x"], f"exb{grp}", want_q="rms")
            for ji, jj in enumerate(jjg):
                xg = pool4[ji]
                rstdc = st["rstd"].t[:, jj:jj + 1]
                op("scalar", lambda g, xg=xg, rstdc=rstdc, jj=jj: g.activation(
                        lnt.t[:, :], xg.t[:, :], AF.Identity,
                        bias=st["nmr8"].t[:, jj:jj + 1], scale=rstdc),
                   reads=[xg, st["rstd"], st["nmr8"]], writes=[lnt], tag=f"exl{jj}_ln")
                quant_tile(lnt, 0, H, gain2048, st["rstdrms"],
                           st["rstdrms"].t[:, jj:jj + 1], "Wg",
                           ste["alph"], jj, f"exq1_{jj}")
                transpose_q_to_fm(qfm_a, jj, HC, f"exq1T{jj}")
            mlp_mm(qfm_a, HC, t_Wg, 2 * II, jjg, ex_mm1_evac, f"exmm1g{grp}")
            mlp_post_group(jjg, ste["osc"], gain1024, "Wd", qfm_b, f"exp{grp}")
            mlp_mm(qfm_b, IC, t_Wd, H, jjg, ex_mm2_evac, f"exmm2g{grp}")
            half = eo_A if grp == 0 else eo_B
            r0 = 0 if grp == 0 else 512
            P.op("gpsimd", lambda g, half=half, r0=r0: g.collective_compute(
                    "AllGather", A.bypass, replica_groups=[list(range(NCORE))],
                    ins=[eo_bounce.t[r0:r0 + 512, :].opt()], outs=[half.t.opt()]),
                 reads=[eo_bounce], writes=[half], sem=f"ag2{r0}",
                 tag=f"AG2_{r0}")

        for jj in range(NT):
            isl = inclo_tm.t[:, 8 * jj:8 * jj + 8]
            op("vector", lambda g, isl=isl: g.tensor_scalar(
                    r8["va"].t[:, :], isl, float(CAP), None, A.is_le),
               reads=[inclo_tm], writes=[r8["va"]], tag=f"oi{jj}_v")
            op("vector", lambda g, isl=isl: g.tensor_scalar(
                    r8["po"].t[:, :], isl, -1.0, None, A.add),
               reads=[inclo_tm], writes=[r8["po"]], tag=f"oi{jj}_p")
            op("vector", lambda g: g.tensor_tensor(
                    r8["ia"].t[:, :], r8["po"].t[:, :], c_eoff2.t[:, :], A.add),
               reads=[r8["po"], c_eoff2], writes=[r8["ia"]], tag=f"oi{jj}_ia")
            for which, fst, wst in ((1, stf1[jj], "w1"), (2, stf2[jj], "w2")):
                op("vector", lambda g, fst=fst: g.tensor_tensor(
                        r8["v1"].t[:, :], fst.t[:, :], r8["va"].t[:, :], A.mult),
                   reads=[fst, r8["va"]], writes=[r8["v1"]], tag=f"oi{jj}_{which}v")
                op("vector", lambda g: g.tensor_tensor(
                        r8["t1"].t[:, :], r8["ia"].t[:, :], r8["v1"].t[:, :], A.mult),
                   reads=[r8["ia"], r8["v1"]], writes=[r8["t1"]], tag=f"oi{jj}_{which}i")
                op("vector", lambda g: g.reduce_sum(rcol["i1s"].t[:, :], r8["t1"].t[:, :],
                                                    axis=AX.X),
                   reads=[r8["t1"]], writes=[rcol["i1s"]], tag=f"oi{jj}_{which}s")
                op("vector", lambda g: g.reduce_sum(rcol["n1"].t[:, :], r8["v1"].t[:, :],
                                                    axis=AX.X),
                   reads=[r8["v1"]], writes=[rcol["n1"]], tag=f"oi{jj}_{which}n")
                op("vector", lambda g: g.tensor_tensor(
                        r8["t1"].t[:, :], r8["po"].t[:, :], r8["v1"].t[:, :], A.mult),
                   reads=[r8["po"], r8["v1"]], writes=[r8["t1"]], tag=f"oi{jj}_{which}pm")
                op("vector", lambda g: g.reduce_sum(rcol["pm"].t[:, :], r8["t1"].t[:, :],
                                                    axis=AX.X),
                   reads=[r8["t1"]], writes=[rcol["pm"]], tag=f"oi{jj}_{which}ps")
                op("vector", lambda g: g.tensor_scalar(
                        rcol["vb"].t[:, :], rcol["pm"].t[:, :], 512.0, None, A.is_ge),
                   reads=[rcol["pm"]], writes=[rcol["vb"]], tag=f"oi{jj}_{which}hb")
                op("vector", lambda g: g.tensor_tensor(
                        rcol["vbn"].t[:, :], rcol["vb"].t[:, :], rcol["n1"].t[:, :],
                        A.mult),
                   reads=[rcol["vb"], rcol["n1"]], writes=[rcol["vbn"]],
                   tag=f"oi{jj}_{which}hn")
                op("vector", lambda g: g.tensor_tensor(
                        rcol["van"].t[:, :], rcol["n1"].t[:, :], rcol["vbn"].t[:, :],
                        A.subtract),
                   reads=[rcol["n1"], rcol["vbn"]], writes=[rcol["van"]],
                   tag=f"oi{jj}_{which}ha")
                for hnm, vsel, sub in ((f"i{which}A", "van", 0.0),
                                       (f"i{which}B", "vbn", -512.0)):
                    op("vector", lambda g, vsel=vsel: g.tensor_scalar(
                            rcol["big1"].t[:, :], rcol[vsel].t[:, :], -BIGF, BIGF,
                            A.mult, A.add),
                       reads=[rcol[vsel]], writes=[rcol["big1"]],
                       tag=f"oi{jj}_{which}b{hnm}")
                    op("vector", lambda g, sub=sub: g.tensor_scalar(
                            rcol["isb"].t[:, :], rcol["i1s"].t[:, :], sub, None,
                            A.add),
                       reads=[rcol["i1s"]], writes=[rcol["isb"]],
                       tag=f"oi{jj}_{which}o{hnm}")
                    op("vector", lambda g, jj=jj, hnm=hnm: g.tensor_tensor(
                            st[hnm].t[:, jj:jj + 1], rcol["isb"].t[:, :],
                            rcol["big1"].t[:, :], A.add),
                       reads=[rcol["isb"], rcol["big1"]], writes=[st[hnm]],
                       tag=f"oi{jj}_{which}x{hnm}")
                mcol = st["m1"] if which == 1 else st["m2"]
                op("vector", lambda g, jj=jj, mcol=mcol, wst=wst: g.tensor_tensor(
                        st[wst].t[:, jj:jj + 1], mcol.t[:, jj:jj + 1],
                        rcol["n1"].t[:, :], A.mult),
                   reads=[mcol, rcol["n1"]], writes=[st[wst]], tag=f"oi{jj}_{which}w")
        for inm, isrc in (("1A", "i1A"), ("1B", "i1B"), ("2A", "i2A"),
                          ("2B", "i2B")):
            op("vector", lambda g, inm=inm, isrc=isrc: g.tensor_copy(
                    idxh_i[inm].t[:, :], st[isrc].t[:, :]),
               reads=[st[isrc]], writes=[idxh_i[inm]], tag=f"ix{inm}")
        if want_tap("idx1"):
            dma("sync", "tp7", tap_aps["idx1"], st["i1A"].t[:, :], reads=[st["i1A"]],
                tag="tp_i1")
        if want_tap("idx2"):
            dma("sync", "tp8", tap_aps["idx2"], st["i2A"].t[:, :], reads=[st["i2A"]],
                tag="tp_i2")
        if want_tap("w1"):
            dma("sync", "tp9", tap_aps["w1"], st["w1"].t[:, :], reads=[st["w1"]], tag="tp_w1")
        if want_tap("w2"):
            dma("sync", "tp10", tap_aps["w2"], st["w2"].t[:, :], reads=[st["w2"]], tag="tp_w2")



        if want_tap("eo_own"):
            dma("gpsimd", "tp13", tap_aps["eo_own"], eo_bounce.t, reads=[eo_bounce], tag="tp_eo")

        for jj in range(NT):
            xs = xtile[jj % 2]
            r1t = pool4[2 * (jj % 2)]
            r2t = pool4[2 * (jj % 2) + 1]
            op("vector", lambda g, r1t=r1t: g.memset(r1t.t[:, :], 0.0),
               writes=[r1t], tag=f"oc{jj}_m1")
            op("vector", lambda g, r2t=r2t: g.memset(r2t.t[:, :], 0.0),
               writes=[r2t], tag=f"oc{jj}_m2")
            for half, hl in ((eo_A, "A"), (eo_B, "B")):
                P.op("gpsimd", lambda g, jj=jj, r1t=r1t, half=half, hl=hl:
                     g.indirect_dma_start(
                        r1t.t[:, :], None, half.t,
                        IndirectOffsetOnAxis(ap=idxh_i["1" + hl].t[:, jj:jj + 1],
                                             axis=0),
                        bounds_check=REGS["h"], oob_is_err=False),
                     reads=[idxh_i["1" + hl], half], writes=[r1t], dma=True,
                     sem=f"g{2 * (jj % 2)}", tag=f"oc{jj}_g1{hl}")
                P.op("gpsimd", lambda g, jj=jj, r2t=r2t, half=half, hl=hl:
                     g.indirect_dma_start(
                        r2t.t[:, :], None, half.t,
                        IndirectOffsetOnAxis(ap=idxh_i["2" + hl].t[:, jj:jj + 1],
                                             axis=0),
                        bounds_check=REGS["h"], oob_is_err=False),
                     reads=[idxh_i["2" + hl], half], writes=[r2t], dma=True,
                     sem=f"g{2 * (jj % 2) + 1}", tag=f"oc{jj}_g2{hl}")
            dma("sync", f"xt{jj % 2}", xs.t[:, :], x1_dram.t[128 * jj:128 * jj + 128, :],
                reads=[x1_dram, xs], writes=[xs], tag=f"oc{jj}_xl")
            op("vector", lambda g, jj=jj, r1t=r1t: g.tensor_scalar(
                    zq.t[:, :], r1t.t[:, :], st["w1"].t[:, jj:jj + 1], None,
                    A.mult),
               reads=[r1t, st["w1"]], writes=[zq], tag=f"oc{jj}_t1")
            op("vector", lambda g, jj=jj, r2t=r2t: g.scalar_tensor_tensor(
                    c0q.t[:, :], r2t.t[:, :], st["w2"].t[:, jj:jj + 1],
                    zq.t[:, :], A.mult, A.add),
               reads=[r2t, st["w2"], zq], writes=[c0q], tag=f"oc{jj}_t2")
            op("vector", lambda g, jj=jj, xs=xs: g.tensor_tensor(
                    xs.t[:, :], xs.t[:, :], c0q.t[:, :], A.add),
               reads=[c0q, xs], writes=[xs], tag=f"oc{jj}_x1")
            stats_tile(xs, jj, st["s1x"], st["s2x"], f"L2s{jj}")
            tk = dma("sync", "x1b", x1_dram.t[128 * jj:128 * jj + 128, :], xs.t[:, :],
                     reads=[xs], deps=list(x1_dram.rtoks), tag=f"oc{jj}_xw")
            x1_dram.wtok = [tk]

        if want_tap("x1"):
            dma("sync", "tp14", tap_aps["x1"], x1_dram.t, reads=[x1_dram], tag="tp_x1")
        if upto == "x1":
            return finish()

        batch_ln_math(st["s1x"], st["s2x"], "L2b", want_q=False)
        for jj in range(NT):
            xa = xtile[jj % 2]
            dma("sync", f"xt{jj % 2}", xa.t[:, :], x1_dram.t[128 * jj:128 * jj + 128, :],
                reads=[x1_dram, xa], writes=[xa], tag=f"L2{jj}_xl")
            rstdc = st["rstd"].t[:, jj:jj + 1]
            op("scalar", lambda g, xa=xa, rstdc=rstdc, jj=jj: g.activation(
                    qtm.t[:, :], xa.t[:, :], AF.Identity,
                    bias=st["nmr8"].t[:, jj:jj + 1], scale=rstdc),
               reads=[xa, st["rstd"], st["nmr8"]], writes=[qtm], tag=f"L2{jj}_c")
            transpose_q_to_fm(qfm_a, jj, HC, f"L2{jj}_T")

        if want_tap("ln2fm"):
            dma("sync", "tpA", tap_aps["ln2fm"], qfm_a.t[:, :], reads=[qfm_a],
                tag="tp_ln2fm")

        moe_deps = P.all_toks()
        moe_stack.close()

        h1 = Buf(nc.alloc_sbuf_tensor("h1", [128, 32 * 1024], BF16), "h1")
        h2 = Buf(nc.alloc_sbuf_tensor("h2", [128, 32 * 1024], BF16), "h2")
        wf32 = [sb(f"wf32_{i}", (128, 384), F32) for i in range(5)]
        wbf = [sb(f"wbf_{i}", (128, 384), BF16) for i in range(5)]
        x1s = [sb(f"x1s_{i}", (128, 512), F32) for i in range(2)]
        outsb = [sb(f"outsb_{i}", (128, 512), F32) for i in range(2)]
        w3f = [sb(f"w3f_{i}", (128, 512), F32) for i in range(4)]
        w3b = [sb(f"w3b_{i}", (128, 512), BF16) for i in range(4)]
        b3row = sb("b3row", (1, H), F32)
        for b in (h1, h2, *wf32, *wbf, x1s[0], x1s[1],
                  outsb[0], outsb[1], *w3f, *w3b, b3row):
            b.wtok = list(moe_deps)
        dma("sync", "b3", b3row.t[:, :], b3r, reads=[b3row], writes=[b3row], tag="b3_ld")

        def ffn_layer(src_fm, kchunks, wt_ap, n_out, bias_t, dst_fm, tag):
            nch = n_out // 128
            groups = [list(range(i, min(i + 3, nch))) for i in range(0, nch, 3)]
            for gi, grp in enumerate(groups):
                gw = 128 * len(grp)
                n0 = 128 * grp[0]
                for k in range(kchunks):
                    wfb = wf32[k % 5]
                    wbb = wbf[k % 5]
                    dma("sync" if k % 2 == 0 else "scalar", f"wf{k % 5}",
                        wfb.t[:, 0:gw],
                        wt_ap[128 * k:128 * k + 128, n0:n0 + gw],
                        reads=[wfb], writes=[wfb], tag=f"{tag}_wl{gi}_{k}")
                    op("vector", lambda g, wfb=wfb, wbb=wbb, gw=gw: g.tensor_copy(
                            wbb.t[:, 0:gw], wfb.t[:, 0:gw]),
                       reads=[wfb], writes=[wbb], tag=f"{tag}_wc{gi}_{k}")
                    for ni in range(len(grp)):
                        for th in range(2):
                            bank = pb[2 * ni + th]
                            op("tensor", lambda g, k=k, ni=ni, th=th, wbb=wbb, bank=bank,
                               src_fm=src_fm, kchunks=kchunks: g.matmul(
                                    bank.t[:, :], wbb.t[:, 128 * ni:128 * ni + 128],
                                    src_fm.t[:, 1024 * k + 512 * th:
                                             1024 * k + 512 * th + 512],
                                    start=(k == 0), stop=(k == kchunks - 1)),
                               reads=[wbb, src_fm], writes=[bank],
                               
                               tag=f"{tag}_mm{gi}_{k}_{ni}_{th}")
                for ni, nn in enumerate(grp):
                    for th in range(2):
                        bank = pb[2 * ni + th]
                        dst = dst_fm.t[:, 1024 * nn + 512 * th:1024 * nn + 512 * th + 512]
                        op("scalar", lambda g, bank=bank, dst=dst, nn=nn,
                           bias_t=bias_t: g.activation(
                                dst, bank.t[:, :], AF.Relu, bias=bias_t.t[:, nn:nn + 1]),
                           reads=[bank, bias_t], writes=[dst_fm],
                           tag=f"{tag}_ev{gi}_{ni}_{th}")

        ffn_layer(qfm_a, HC, W1T, D1, b1t, h1, "f1")
        if want_tap("h1"):
            dma("sync", "tpB", tap_aps["h1"], h1.t[:, :], reads=[h1], tag="tp_h1")
        ffn_layer(h1, 32, W2T, D1, b2t, h2, "f2")
        if want_tap("h2"):
            dma("sync", "tpC", tap_aps["h2"], h2.t[:, :], reads=[h2], tag="tp_h2")

        for grp in (0, 4):
            jjg = list(range(grp, grp + 4))
            for nb in range(4):
                for k in range(32):
                    w3fb = w3f[k % 4]
                    w3bb = w3b[k % 4]
                    dma("sync" if k % 2 == 0 else "scalar", f"w3{k % 4}", w3fb.t[:, :],
                        W3T[128 * k:128 * k + 128, 512 * nb:512 * nb + 512],
                        reads=[w3fb], writes=[w3fb], tag=f"f3w{grp}_{nb}_{k}")
                    op("vector", lambda g, w3fb=w3fb, w3bb=w3bb: g.tensor_copy(
                            w3bb.t[:, :], w3fb.t[:, :]),
                       reads=[w3fb], writes=[w3bb], tag=f"f3c{grp}_{nb}_{k}")
                    for ji, jj in enumerate(jjg):
                        bank = pb[ji]
                        if k == 0:
                            op("tensor", lambda g, nb=nb, bank=bank: g.matmul(
                                    bank.t[:, :], c_onesr.t[0:1, :],
                                    b3row.t[0:1, 512 * nb:512 * nb + 512],
                                    start=True, stop=False),
                               reads=[c_onesr, b3row], writes=[bank],
                               tag=f"f3b{grp}_{nb}_{jj}")
                        op("tensor", lambda g, k=k, jj=jj, bank=bank, w3bb=w3bb: g.matmul(
                                bank.t[:, :],
                                h2.t[:, 1024 * k + 128 * jj:1024 * k + 128 * jj + 128],
                                w3bb.t[:, :],
                                start=False, stop=(k == 31)),
                           reads=[h2, w3bb], writes=[bank],
                           tag=f"f3m{grp}_{nb}_{k}_{jj}")
                for ji, jj in enumerate(jjg):
                    bank = pb[ji]
                    xs = x1s[ji % 2]
                    ob = outsb[ji % 2]
                    dma("sync", f"xr{ji % 2}", xs.t[:, :],
                        x1_dram.t[128 * jj:128 * jj + 128, 512 * nb:512 * nb + 512],
                        reads=[x1_dram, xs], writes=[xs], tag=f"f3x{grp}_{nb}_{jj}")
                    op("vector", lambda g, bank=bank, xs=xs, ob=ob: g.tensor_tensor(
                            ob.t[:, :], bank.t[:, :], xs.t[:, :], A.add),
                       reads=[bank, xs], writes=[ob], tag=f"f3o{grp}_{nb}_{jj}")
                    dma("sync", f"out{ji % 2}",
                        out_own[128 * jj:128 * jj + 128, 512 * nb:512 * nb + 512],
                        ob.t[:, :], reads=[ob], tag=f"f3s{grp}_{nb}_{jj}")

        return finish()



def _consts(core):
    r = core
    ident_bf = np.eye(128, dtype=ml_dtypes.bfloat16)
    ident_f = np.eye(128, dtype=np.float32)
    ones_col = np.ones((128, 1), np.float32)
    ones_row = np.ones((1, 128), np.float32)
    rows = np.arange(64)
    selall = (rows[:, None] % 8 == np.arange(8)[None, :]).astype(np.float32)
    selgt_own = -((rows[:, None] % 8 == np.arange(8)[None, :]) &
                  (rows[:, None] // 8 > r)).astype(np.float32)
    selallE = np.repeat((rows % 8 == r).astype(np.float32)[:, None], 8, axis=1)
    selgtnegE = -(((rows[:, None] % 8) == r) &
                  (rows[:, None] // 8 > np.arange(8)[None, :])).astype(np.float32)
    selidE = (rows[:, None] == 8 * np.arange(8)[None, :] + r).astype(np.float32)
    eoff = np.repeat((np.arange(8) * CAP).astype(np.float32)[None, :], 128, axis=0)
    iota = np.zeros((128, 64), np.int32)
    for rr in range(8):
        for j in range(8):
            iota[:, 8 * rr + j] = 8 * (128 * j + np.arange(128)) + rr
    return dict(ident_bf=ident_bf, ident_f=ident_f, ones_col=ones_col,
                ones_row=ones_row, selall=selall, selgt_own=selgt_own,
                selallE=selallE, selgtnegE=selgtnegE, selidE=selidE,
                eoff_rep=eoff, iota_rm=iota)


def make_in_maps(inputs):
    x = np.ascontiguousarray(np.asarray(inputs["L0_attn_x"], dtype=np.float32)[0])
    W1T = np.ascontiguousarray(np.asarray(inputs["L0_ffn_W1"], np.float32).T)
    W2T = np.ascontiguousarray(np.asarray(inputs["L0_ffn_W2"], np.float32).T)
    W3T = np.ascontiguousarray(np.asarray(inputs["L0_ffn_W3"], np.float32).T)
    b1m = np.ascontiguousarray(np.asarray(inputs["L0_ffn_b1"], np.float32).reshape(32, 128))
    b2m = np.ascontiguousarray(np.asarray(inputs["L0_ffn_b2"], np.float32).reshape(32, 128))
    b3r = np.ascontiguousarray(np.asarray(inputs["L0_ffn_b3"], np.float32).reshape(1, H))
    sWg = np.ascontiguousarray(np.asarray(inputs["L0_attn_sWg"], np.float32))
    sWd = np.ascontiguousarray(np.asarray(inputs["L0_attn_sWd"], np.float32))
    Wg = np.asarray(inputs["L0_attn_Wg"], np.float32)
    Wd = np.asarray(inputs["L0_attn_Wd"], np.float32)
    in_maps = []
    for r in range(NCORE):
        cm = _consts(r)
        m = dict(
            x_own=np.ascontiguousarray(x[r::8, :]),
            x_full=x,
            W1T=W1T, W2T=W2T, W3T=W3T, b1m=b1m, b2m=b2m, b3r=b3r,
            sWg=sWg, sWd=sWd,
            Wg_e=np.ascontiguousarray(Wg[r]),
            Wd_e=np.ascontiguousarray(Wd[r]),
            sNg_r=np.ascontiguousarray(np.asarray(inputs["L0_attn_sNg"], np.float32).reshape(1, H)),
            sNd_r=np.ascontiguousarray(np.asarray(inputs["L0_attn_sNd"], np.float32).reshape(1, II)),
            Ng_r=np.ascontiguousarray(np.asarray(inputs["L0_attn_Ng"], np.float32)[r].reshape(1, H)),
            Nd_r=np.ascontiguousarray(np.asarray(inputs["L0_attn_Nd"], np.float32)[r].reshape(1, II)),
            gnw_r=np.ascontiguousarray(np.asarray(inputs["L0_attn_gate_norm_w"], np.float32).reshape(1, H)),
            Wgate=np.ascontiguousarray(np.asarray(inputs["L0_attn_W_gate"], np.float32)),
            Wsg=np.ascontiguousarray(np.asarray(inputs["L0_attn_W_shared_gate"], np.float32)),
            **cm,
        )
        in_maps.append(m)
    return in_maps


def kernel(**inputs):
    nc = build_program()
    in_maps = make_in_maps(inputs)
    res = run_bass_kernel_spmd(nc, in_maps, core_ids=list(range(NCORE)))
    out = np.empty((T, H), np.float32)
    for r in range(NCORE):
        out[r::8, :] = res.results[r]["out_own"]
    return out.reshape(1, T, H)


# version 23 (final)
# speedup vs baseline: 1.0615x; 1.0182x over previous
import contextlib

import numpy as np
import ml_dtypes
import concourse.bass as bass
import concourse.mybir as mybir
from concourse.bass import IndirectOffsetOnAxis
from concourse.bass_utils import run_bass_kernel_spmd

F32 = mybir.dt.float32
BF16 = mybir.dt.bfloat16
I32 = mybir.dt.int32
A = mybir.AluOpType
AF = mybir.ActivationFunctionType
AX = mybir.AxisListType

NCORE = 8
T = 8192
TPC = 1024
NT = 8
H = 2048
HC = 16
II = 1024
IC = 8
E = 8
CAP = 1024
D1 = 4096
BIGF = 100000.0
LADDER = [128.0, 64.0, 32.0, 16.0, 8.0, 4.0, 2.0, 1.0]

ENGS = ("sync", "scalar", "vector", "tensor", "gpsimd")


class Buf:
    __slots__ = ("t", "wtok", "rtoks", "name")

    def __init__(self, t, name=""):
        self.t = t
        self.name = name
        self.wtok = []
        self.rtoks = []


class Plan:
    def __init__(self):
        self.streams = {e: [] for e in ENGS}
        self.counts = {e: 0 for e in ENGS}
        self.marks = {e: {} for e in ENGS}
        self.sem_names = set(ENGS)

    def op(self, eng, fn, reads=(), writes=(), deps=(), dma=False, sem=None, tag=""):
        want = []
        for b in reads:
            want += b.wtok
        for b in writes:
            want += b.wtok
            want += b.rtoks
        want += list(deps)
        need = {}
        for (s, v) in want:
            if eng == "tensor" and s == "tensor":
                continue
            if self.marks[eng].get(s, 0) >= v:
                continue
            if need.get(s, 0) < v:
                need[s] = v
        waits = sorted(need.items())
        for s, v in waits:
            self.marks[eng][s] = v
        amt = 16 if dma else 1
        cnt_key = sem if sem is not None else eng
        if cnt_key not in self.counts:
            self.counts[cnt_key] = 0
            self.sem_names.add(cnt_key)
        self.counts[cnt_key] += amt
        tok = (cnt_key, self.counts[cnt_key])
        self.streams[eng].append((waits, fn, amt, cnt_key, tag))
        for b in writes:
            b.wtok = [tok]
            b.rtoks = []
        for b in reads:
            b.rtoks.append(tok)
        return tok

    def all_toks(self):
        return [(k, v) for k, v in self.counts.items() if v > 0]

    def emit(self, block, sems, pre_emit=None):
        def mk(eng_name):
            def body(eng):
                if pre_emit and eng_name in pre_emit:
                    pre_emit[eng_name](eng)
                for waits, fn, amt, cnt_key, tag in self.streams[eng_name]:
                    for s, v in waits:
                        eng.wait_ge(sems[s], v)
                    ins = fn(eng)
                    ins.then_inc(sems[cnt_key], amt)
            return body
        for e in ENGS:
            getattr(block, e)(mk(e))


def build_program(upto="all", taps=()):
    nc = bass.Bass("TRN2", target_bir_lowering=False)
    P = Plan()

    def din(name, shape, dt=F32):
        return nc.dram_tensor(name, list(shape), dt, kind="ExternalInput").ap()

    x_own = din("x_own", (TPC, H))
    x_full = din("x_full", (T, H))
    W1T = din("W1T", (H, D1))
    W2T = din("W2T", (D1, D1))
    W3T = din("W3T", (D1, H))
    b1m = din("b1m", (32, 128))
    b2m = din("b2m", (32, 128))
    b3r = din("b3r", (1, H))
    sWg_d = din("sWg", (H, 2 * II))
    sWd_d = din("sWd", (II, H))
    Wg_d = din("Wg_e", (H, 2 * II))
    Wd_d = din("Wd_e", (II, H))
    sNg_r = din("sNg_r", (1, H))
    sNd_r = din("sNd_r", (1, II))
    Ng_r = din("Ng_r", (1, H))
    Nd_r = din("Nd_r", (1, II))
    gnw_r = din("gnw_r", (1, H))
    Wgate = din("Wgate", (H, E))
    Wsg = din("Wsg", (H, 1))
    ident_bf = din("ident_bf", (128, 128), BF16)
    ident_f = din("ident_f", (128, 128), F32)
    ones_col = din("ones_col", (128, 1))
    ones_row = din("ones_row", (1, 128))
    selall_d = din("selall", (64, 8))
    selgt_own = din("selgt_own", (64, 8))
    selallE_d = din("selallE", (64, 8))
    selgtnegE_d = din("selgtnegE", (64, 8))
    selidE_d = din("selidE", (64, 8))
    eoff_d = din("eoff_rep", (128, 8))
    iota_d = din("iota_rm", (128, 64), I32)

    out_own = nc.dram_tensor("out_own", [TPC, H], F32, kind="ExternalOutput").ap()

    tap_aps = {}
    for nm, shape, dt in taps:
        tap_aps[nm] = nc.dram_tensor("tap_" + nm, list(shape), dt, kind="ExternalOutput").ap()

    def want_tap(nm):
        return nm in tap_aps

    def dbuf(name, shape, dt):
        return Buf(nc.dram_tensor(name, list(shape), dt).ap(), name)

    t_sWg = dbuf("t_sWg", (H, 2 * II), BF16)
    t_sWd = dbuf("t_sWd", (II, H), BF16)
    t_Wg = dbuf("t_Wg", (H, 2 * II), BF16)
    t_Wd = dbuf("t_Wd", (II, H), BF16)
    wm_bounce = dbuf("wm_bounce", (8, 1024), F32)
    wm_all = dbuf("wm_all", (64, 1024), F32)
    eo_bounce = dbuf("eo_bounce", (TPC, H), F32)
    eo_A = dbuf("eo_A", (T // 2, H), F32)
    eo_B = dbuf("eo_B", (T // 2, H), F32)
    x1_dram = dbuf("x1_dram", (TPC, H), F32)
    disp_dram = dbuf("disp_dram", (CAP, 1), I32)

    stack = contextlib.ExitStack()
    moe_stack = contextlib.ExitStack()

    def sb(name, shape, dt=F32):
        return Buf(nc.alloc_sbuf_tensor(name, list(shape), dt), name)

    def msb(name, shape, dt=F32):
        return Buf(moe_stack.enter_context(nc.sbuf_tensor(name, list(shape), dt)), name)

    def psum(name, shape, dt=F32):
        return Buf(stack.enter_context(nc.psum_tensor(name, list(shape), dt)), name)

    sems = {}

    def dma(eng, sem, dst_ap, src_ap, reads=(), writes=(), deps=(), tag=""):
        return P.op(eng, lambda g, d=dst_ap, s=src_ap: g.dma_start(out=d, in_=s),
                    reads=reads, writes=writes, deps=deps, dma=True, sem=sem, tag=tag)

    def op(eng, fn, reads=(), writes=(), deps=(), tag=""):
        return P.op(eng, fn, reads=reads, writes=writes, deps=deps, tag=tag)

    REGS = {}

    def _pre_gpsimd(eng):
        rc = eng.alloc_register("bc_cap")
        rt = eng.alloc_register("bc_t")
        rh = eng.alloc_register("bc_h")
        eng.reg_mov(rc, CAP - 1)
        eng.reg_mov(rt, T - 1)
        eng.reg_mov(rh, T // 2 - 1)
        REGS["cap"] = rc
        REGS["t"] = rt
        REGS["h"] = rh

    def finish():
        op("sync", lambda g: g.nop(), deps=P.all_toks(), tag="fin")
        make_sems()
        P.emit(stack.enter_context(nc.Block()), sems, pre_emit={"gpsimd": _pre_gpsimd})
        return nc

    def make_sems():
        for nm in sorted(P.sem_names):
            if nm not in sems:
                sems[nm] = stack.enter_context(nc.semaphore("sem_" + nm))

    with stack:
        pass_placeholder = None

        qfm_a = sb("qfm_a", (128, HC * 1024), BF16)
        c_idbf = sb("c_idbf", (128, 128), BF16)
        c_idf = sb("c_idf", (128, 128), F32)
        c_ones = sb("c_ones", (128, 1), F32)
        c_onesr = sb("c_onesr", (1, 128), F32)
        c_eoff = sb("c_eoff", (128, 8), F32)
        c_iota = sb("c_iota", (128, 64), I32)
        c_selaU = sb("c_selaU", (64, 8), F32)
        c_selgt = sb("c_selgt", (64, 8), F32)
        c_selaE = sb("c_selaE", (64, 8), F32)
        c_selgE = sb("c_selgE", (64, 8), F32)
        c_selidE = sb("c_selidE", (64, 8), F32)
        wgp = sb("wgp", (128, HC * 8), F32)
        wsg_sb = sb("wsg_sb", (128, HC), F32)
        gnw_c = sb("gnw_c", (128, HC), F32)
        b1t = sb("b1t", (128, 32), F32)
        b2t = sb("b2t", (128, 32), F32)
        row_scr = sb("row_scr", (1, 128), F32)
        st = {nm: sb("st_" + nm, (128, 8), F32) for nm in
              ("mu", "rstd", "var", "rmsq", "rstdrms", "rstdr", "alph", "gate",
               "m1", "m2", "w1", "w2", "i1A", "i1B", "i2A", "i2B", "osc", "t18",
               "s1x", "s2x", "nmr8", "rmsE", "rrE", "tb1", "tb2")}
        ste = {nm: sb("ste_" + nm, (128, 8), F32) for nm in ("alph", "osc")}
        stf1 = [sb(f"stf1_{j}", (128, 8), F32) for j in range(NT)]
        stf2 = [sb(f"stf2_{j}", (128, 8), F32) for j in range(NT)]
        idxh_i = {nm: sb(f"idxh_{nm}", (128, 8), I32)
                  for nm in ("1A", "1B", "2A", "2B")}
        c_eoff2 = sb("c_eoff2", (128, 8), F32)
        z8 = sb("z8", (128, 8), F32)
        trep = {nm: sb("trep_" + nm, (128, 2), F32) for nm in ("sWg", "sWd", "Wg", "Wd")}
        tmc = {nm: sb("tmc_" + nm, (128, 1), F32) for nm in ("sWg", "sWd", "Wg", "Wd")}
        rcol = {nm: sb("rc_" + nm, (128, 1), F32) for nm in
                ("s2", "atmp", "m", "mm", "rc", "s127", "csc", "nmax", "ssum",
                 "rs", "n1", "big1", "i1s", "pm", "vb", "vbn", "van", "isb")}

        pb = [psum(f"pb{i}", (128, 512), F32) for i in range(7)]
        pQ = psum("pQ", (128, 1024), BF16)

        const_bufs = (c_idbf, c_idf, c_ones, c_onesr, c_eoff, c_iota, c_selaU,
                      c_selgt, c_selaE, c_selgE, c_selidE)
        for cbuf, src in zip(const_bufs, (ident_bf, ident_f, ones_col, ones_row,
                                          eoff_d, iota_d, selall_d, selgt_own,
                                          selallE_d, selgtnegE_d, selidE_d)):
            dma("sync", "c", cbuf.t[:, :], src, writes=[cbuf])
        for k in range(HC):
            ctk = dma("sync", "c", wsg_sb.t[:, k:k + 1], Wsg[128 * k:128 * k + 128, :],
                      writes=[wsg_sb])
        for cbuf in const_bufs + (wsg_sb,):
            cbuf.wtok = [ctk]
        op("vector", lambda g: g.memset(z8.t[:, :], 0.0), writes=[z8])
        op("vector", lambda g: g.tensor_scalar(c_eoff2.t[:, :], c_eoff.t[:, :],
                                               0.5, None, A.mult),
           reads=[c_eoff], writes=[c_eoff2], tag="eoff2")
        op("vector", lambda g: g.memset(st["rmsE"].t[:, :], 1.0),
           writes=[st["rmsE"]], tag="ms_rmsE")
        epsb = {}
        for enm, ev in (("e5", 1e-5), ("e6", 1e-6), ("e8", 1e-8),
                        ("invII", 1.0 / 1024.0)):
            epsb[enm] = sb("eps_" + enm, (128, 1), F32)
            op("vector", lambda g, enm=enm, ev=ev: g.memset(
                    epsb[enm].t[:, :], ev), writes=[epsb[enm]], tag=f"ms_{enm}")

        qfm_b = msb("qfm_b", (128, IC * 1024), BF16)
        xtile = [msb(f"xtile{i}", (128, H), F32) for i in range(2)]
        pool4 = [msb(f"pool4_{i}", (128, H), F32) for i in range(4)]
        gain2048 = msb("gain2048", (128, H), F32)
        gain1024 = msb("gain1024", (128, II), F32)
        zq = msb("zq", (128, 2048), F32)
        c0q = msb("c0q", (128, 2048), F32)
        ddq = msb("ddq", (128, 2048), F32)
        qtm = msb("qtm", (128, 2048), BF16)
        lnt = msb("lnt", (128, H), F32)
        wtile = [msb(f"wtile{i}", (128, 512), BF16) for i in range(8)]
        twt = [msb(f"twt{i}", (128, 2048), BF16) for i in range(2)]
        tern_rs = msb("tern_rs", (128, 32), F32)
        lgf = msb("lgf", (8, 1024), F32)
        sgr = msb("sgr", (1, 1024), F32)
        ltm = msb("ltm", (128, 64), F32)
        sgv = msb("sgv", (128, 8), F32)
        r8 = {nm: msb("r8_" + nm, (128, 8), F32) for nm in
              ("p", "eq", "cs", "c1", "p2", "eq2", "cs2", "c2", "tmp", "wm",
               "va", "po", "ia", "v1", "t1")}
        lf = msb("lf", (128, 512), F32)
        wa = msb("wa", (64, 1024), F32)
        mk_sb = msb("mk_sb", (64, 1024), F32)
        ci_sb = msb("ci_sb", (64, 1024), F32)
        inclo = msb("inclo", (8, 1024), F32)
        incle = msb("incle", (8, 1024), F32)
        mke = msb("mke", (8, 1024), F32)
        inclo_tm = msb("inclo_tm", (128, 64), F32)
        scat_tm = msb("scat_tm", (128, 64), F32)
        scat_i = msb("scat_i", (128, 64), I32)
        dsp_sb = msb("dsp_sb", (128, 8), I32)
        dfill = msb("dfill", (128, 8), I32)
        ysc = msb("ysc", (128, 512), F32)
        ysc2 = msb("ysc2", (128, 512), F32)
        scatf = mke
        wm_sb = lgf
        tern_p, tern_n = c0q, ddq

        for hh in range(2):
            dma("sync", "sgr", sgr.t[0:1, :], gnw_r[0:1, 1024 * hh:1024 * hh + 1024],
                writes=[sgr], tag=f"gnw_ld{hh}")
            for kk in range(8):
                k = 8 * hh + kk
                op("tensor", lambda g, k=k, kk=kk: g.transpose(
                        pb[6].t[:, k:k + 1], sgr.t[0:1, 128 * kk:128 * kk + 128],
                        c_idf.t[0:1, 0:1]),
                   reads=[sgr, c_idf], writes=[pb[6]],
                    tag=f"gnwT{k}")
        op("scalar", lambda g: g.activation(gnw_c.t[:, :], pb[6].t[:, 0:HC], AF.Copy),
           reads=[pb[6]], writes=[gnw_c], tag="gnw_ev")
        for k in range(HC):
            dma("sync", "lf", lf.t[:, 8 * k % 512:8 * k % 512 + 8],
                Wgate[128 * k:128 * k + 128, :], writes=[lf], tag=f"wg_ld{k}")
            op("vector", lambda g, k=k: g.tensor_scalar(
                    wgp.t[:, 8 * k:8 * k + 8], lf.t[:, 8 * k % 512:8 * k % 512 + 8],
                    gnw_c.t[:, k:k + 1], None, A.mult),
               reads=[lf, gnw_c], writes=[wgp], tag=f"wgp{k}")

        dma("sync", "lf", lf.t[0:32, 0:128], b1m, reads=[lf], writes=[lf], tag="b1_ld")
        op("tensor", lambda g: g.transpose(pb[6].t[:, 32:64], lf.t[0:32, 0:128],
                                           c_idf.t[0:32, 0:32]),
           reads=[lf, c_idf], writes=[pb[6]], tag="b1T")
        op("scalar", lambda g: g.activation(b1t.t[:, :], pb[6].t[:, 32:64], AF.Copy),
           reads=[pb[6]], writes=[b1t], tag="b1ev")
        dma("sync", "lf", lf.t[0:32, 0:128], b2m, reads=[lf], writes=[lf], tag="b2_ld")
        op("tensor", lambda g: g.transpose(pb[6].t[:, 64:96], lf.t[0:32, 0:128],
                                           c_idf.t[0:32, 0:32]),
           reads=[lf, c_idf], writes=[pb[6]], tag="b2T")
        op("scalar", lambda g: g.activation(b2t.t[:, :], pb[6].t[:, 64:96], AF.Copy),
           reads=[pb[6]], writes=[b2t], tag="b2ev")

        def replicate_row(dst, src_ap, width, tag):
            for hh in range((width + 1023) // 1024):
                w1 = min(1024, width - 1024 * hh)
                dma("sync", "sgr", sgr.t[0:1, 0:w1], src_ap[0:1, 1024 * hh:1024 * hh + w1],
                    reads=[sgr], writes=[sgr], tag=f"rr_{tag}{hh}")
                for qq in range((w1 + 511) // 512):
                    w2 = min(512, w1 - 512 * qq)
                    op("tensor", lambda g, qq=qq, w2=w2: g.matmul(
                            pb[5].t[:, 0:w2], c_onesr.t[0:1, :],
                            sgr.t[0:1, 512 * qq:512 * qq + w2], start=True, stop=True),
                       reads=[c_onesr, sgr], writes=[pb[5]], tag=f"rm_{tag}{hh}_{qq}")
                    op("scalar", lambda g, hh=hh, qq=qq, w2=w2: g.activation(
                            dst.t[:, 1024 * hh + 512 * qq:1024 * hh + 512 * qq + w2],
                            pb[5].t[:, 0:w2], AF.Copy),
                       reads=[pb[5]], writes=[dst], tag=f"re_{tag}{hh}_{qq}")

        replicate_row(gain2048, sNg_r, H, "sng")
        replicate_row(gain1024, sNd_r, II, "snd")

        qbf = qfm_b.t[:, :].bitcast(F32)
        wfb = [Buf(qbf[:, 0:2048], "wfb0"), Buf(qbf[:, 2048:4096], "wfb1")]

        def tern_stats_stream(src_ap, kchunks, name, slots, cbase=0, k0=0,
                              k1=None):
            for k in range(k0, kchunks if k1 is None else k1):
                wf = slots[k % 2]
                dma("gpsimd", f"swf_{wf.name}", wf.t[:, :],
                    src_ap[128 * k:128 * k + 128, :],
                    reads=[wf], writes=[wf], tag=f"ts1_{name}{k}")
                op("scalar", lambda g, wf=wf, k=k, cbase=cbase: g.activation(
                        pool4[0].t[:, :], wf.t[:, :], AF.Abs,
                        accum_out=tern_rs.t[:, cbase + k:cbase + k + 1]),
                   reads=[wf], writes=[pool4[0], tern_rs], tag=f"ta_{name}{k}")

        def tern_stats_reduce(kchunks, name, bank, cbase=0):
            ncc = kchunks
            op("tensor", lambda g, ncc=ncc, bank=bank, cbase=cbase: g.matmul(
                    bank.t[0:1, 0:ncc], c_ones.t[:, :],
                    tern_rs.t[:, cbase:cbase + ncc],
                    start=True, stop=True),
               reads=[c_ones, tern_rs], writes=[bank], tag=f"tsu_{name}")
            op("vector", lambda g, ncc=ncc, bank=bank: g.tensor_copy(
                    row_scr.t[0:1, 32:32 + ncc], bank.t[0:1, 0:ncc]),
               reads=[bank], writes=[row_scr], tag=f"tc_{name}")
            op("vector", lambda g, ncc=ncc: g.reduce_sum(
                    row_scr.t[0:1, 127:128], row_scr.t[0:1, 32:32 + ncc], axis=AX.X),
               reads=[row_scr], writes=[row_scr], tag=f"tr_{name}")
            inv_n = 1.0 / (kchunks * 128 * 2048)
            op("vector", lambda g, inv_n=inv_n: g.tensor_scalar(
                    row_scr.t[0:1, 126:127], row_scr.t[0:1, 127:128], inv_n, 1e-5,
                    A.mult, A.max),
               reads=[row_scr], writes=[row_scr], tag=f"tm_{name}")
            op("vector", lambda g: g.reciprocal(row_scr.t[0:1, 125:126],
                                                row_scr.t[0:1, 126:127]),
               reads=[row_scr], writes=[row_scr], tag=f"tre_{name}")
            op("vector", lambda g: g.tensor_scalar(row_scr.t[0:1, 124:125],
                                                   row_scr.t[0:1, 125:126], -1.0, None,
                                                   A.mult),
               reads=[row_scr], writes=[row_scr], tag=f"tn_{name}")
            op("tensor", lambda g, bank=bank: g.matmul(
                    bank.t[:, 16:19], c_onesr.t[0:1, :],
                    row_scr.t[0:1, 124:127], start=True, stop=True),
               reads=[c_onesr, row_scr], writes=[bank], tag=f"trp_{name}")
            op("scalar", lambda g, bank=bank: g.activation(
                    trep[name].t[:, 0:1], bank.t[:, 17:18], AF.Copy),
               reads=[bank], writes=[trep[name]], tag=f"tv1_{name}")
            op("scalar", lambda g, bank=bank: g.activation(
                    trep[name].t[:, 1:2], bank.t[:, 16:17], AF.Copy),
               reads=[bank], writes=[trep[name]], tag=f"tv2_{name}")
            op("scalar", lambda g, bank=bank: g.activation(
                    tmc[name].t[:, :], bank.t[:, 18:19], AF.Copy),
               reads=[bank], writes=[tmc[name]], tag=f"tv3_{name}")

        def tern_write(src_ap, dst_dram, kchunks, name, slots):
            ts_last = {}
            for k in range(kchunks):
                wf = slots[k % 2]
                tw = twt[k % 2]
                dma("scalar", f"wf_{wf.name}", wf.t[:, :],
                    src_ap[128 * k:128 * k + 128, :],
                    reads=[wf], writes=[wf], tag=f"tw_{name}{k}")
                op("vector", lambda g, wf=wf: g.tensor_scalar(
                        tern_p.t[:, :], wf.t[:, :], trep[name].t[:, 0:1],
                        0.5, A.mult, A.is_gt),
                   reads=[wf, trep[name]], writes=[tern_p], tag=f"tp_{name}{k}")
                op("vector", lambda g, wf=wf: g.tensor_scalar(
                        tern_n.t[:, :], wf.t[:, :], trep[name].t[:, 1:2],
                        0.5, A.mult, A.is_gt),
                   reads=[wf, trep[name]], writes=[tern_n], tag=f"tq_{name}{k}")
                op("vector", lambda g, tw=tw: g.tensor_tensor(
                        tw.t[:, :], tern_p.t[:, :], tern_n.t[:, :], A.subtract),
                   reads=[tern_p, tern_n], writes=[tw], tag=f"tt_{name}{k}")
                tk = dma("sync", f"ts{k % 2}_" + name,
                         dst_dram.t[128 * k:128 * k + 128, :], tw.t[:, :],
                         reads=[tw], deps=list(dst_dram.rtoks), tag=f"tso_{name}{k}")
                ts_last[k % 2] = tk
                dst_dram.wtok = [t for t in ts_last.values()]

        def quant_ladder(zh, c0h, ch, dh, qh, csc_buf, eng, width, tag):
            op(eng, lambda g, width=width: g.tensor_scalar(
                    c0h.t[:, 0:width], zh.t[:, 0:width], csc_buf.t[:, :], 128.5,
                    A.mult, A.add),
               reads=[zh, csc_buf], writes=[c0h], tag=tag + "_c0")
            cur = c0h
            for si, tk in enumerate(LADDER):
                op(eng, lambda g, tk=tk, cur=cur, width=width: g.tensor_scalar(
                        dh.t[:, 0:width], cur.t[:, 0:width], tk, -tk, A.is_ge, A.mult),
                   reads=[cur], writes=[dh], tag=tag + f"_d{si}")
                op(eng, lambda g, cur=cur, width=width: g.tensor_tensor(
                        ch.t[:, 0:width], cur.t[:, 0:width], dh.t[:, 0:width], A.add),
                   reads=[cur, dh], writes=[ch], tag=tag + f"_a{si}")
                cur = ch
            if eng == "vector":
                op(eng, lambda g, width=width: g.scalar_tensor_tensor(
                        qh.t[:, 0:width], c0h.t[:, 0:width], -128.0, ch.t[:, 0:width],
                        A.add, A.subtract),
                   reads=[c0h, ch], writes=[qh], tag=tag + "_q")
            else:
                op(eng, lambda g, width=width: g.tensor_scalar(
                        dh.t[:, 0:width], c0h.t[:, 0:width], -128.0, None, A.add),
                   reads=[c0h], writes=[dh], tag=tag + "_q1")
                op(eng, lambda g, width=width: g.tensor_tensor(
                        qh.t[:, 0:width], dh.t[:, 0:width], ch.t[:, 0:width],
                        A.subtract),
                   reads=[dh, ch], writes=[qh], tag=tag + "_q2")

        def quant_tile(hsrc, off0, off1, gain_buf, rrms_buf, rrms_ap, mc_name,
                       alph_st, jj, tag):
            width = off1 - off0
            op("vector", lambda g, off0=off0, off1=off1, width=width: g.tensor_tensor(
                    zq.t[:, 0:width], hsrc.t[:, off0:off1],
                    gain_buf.t[:, 0:width], A.mult),
               reads=[hsrc, gain_buf], writes=[zq], tag=f"{tag}_z")
            op("vector", lambda g, width=width: g.reduce_max(
                    rcol["m"].t[:, :], zq.t[:, 0:width], axis=AX.X,
                    apply_absolute_value=True),
               reads=[zq], writes=[rcol["m"]], tag=f"{tag}_m")
            op("vector", lambda g, rrms_ap=rrms_ap: g.tensor_scalar(
                    rcol["mm"].t[:, :], rcol["m"].t[:, :], rrms_ap, 1e-5, A.mult, A.max),
               reads=[rcol["m"], rrms_buf], writes=[rcol["mm"]], tag=f"{tag}_mm")
            op("vector", lambda g: g.reciprocal(rcol["rc"].t[:, :], rcol["mm"].t[:, :]),
               reads=[rcol["mm"]], writes=[rcol["rc"]], tag=f"{tag}_rc")
            op("vector", lambda g: g.tensor_scalar(rcol["s127"].t[:, :], rcol["rc"].t[:, :],
                                                   127.0, None, A.mult),
               reads=[rcol["rc"]], writes=[rcol["s127"]], tag=f"{tag}_s")
            op("vector", lambda g, rrms_ap=rrms_ap: g.tensor_scalar(
                    rcol["csc"].t[:, :], rcol["s127"].t[:, :], rrms_ap, None, A.mult),
               reads=[rcol["s127"], rrms_buf], writes=[rcol["csc"]], tag=f"{tag}_csc")
            op("vector", lambda g: g.tensor_scalar(rcol["atmp"].t[:, :], rcol["mm"].t[:, :],
                                                   1.0 / 127.0, None, A.mult),
               reads=[rcol["mm"]], writes=[rcol["atmp"]], tag=f"{tag}_i")
            op("vector", lambda g, jj=jj, alph_st=alph_st: g.tensor_tensor(
                    alph_st.t[:, jj:jj + 1], rcol["atmp"].t[:, :], tmc[mc_name].t[:, :],
                    A.mult),
               reads=[rcol["atmp"], tmc[mc_name]], writes=[alph_st], tag=f"{tag}_al")
            quant_ladder(zq, c0q, zq, ddq, qtm, rcol["csc"], "vector", width, tag + "_L")

        def transpose_q_to_fm(dst_fm, jj, kchunks, tag):
            for k in range(kchunks):
                sl8 = 128 * (k % 8)
                op("tensor", lambda g, k=k, sl8=sl8: g.transpose(
                        pQ.t[:, sl8:sl8 + 128], qtm.t[:, 128 * k:128 * k + 128],
                        c_idbf.t[:, :]),
                   reads=[qtm, c_idbf], writes=[pQ],
                   tag=f"{tag}_T{k}")
                if (k % 8) == 7 or k == kchunks - 1:
                    k0 = (k // 8) * 8
                    nblk = k - k0 + 1
                    dst = dst_fm.t[:, :].rearrange("p (a b) -> p a b", b=1024)[
                        :, k0:k0 + nblk, 128 * jj:128 * jj + 128]
                    op("scalar", lambda g, dst=dst, nblk=nblk: g.activation(
                            dst, pQ.t[:, 0:128 * nblk], AF.Copy),
                       reads=[pQ], writes=[dst_fm], tag=f"{tag}_ev{k0}")


        def batch_rsqrt(src_buf, eps_buf, dst_buf, tag, scale_buf=None):
            def fn(g):
                kw = {}
                if scale_buf is not None:
                    kw["scale"] = scale_buf.t[:, :]
                return g.activation(st["tb1"].t[:, :], src_buf.t[:, :], AF.Sqrt,
                                    bias=eps_buf.t[:, :], **kw)
            rd = [src_buf, eps_buf] + ([scale_buf] if scale_buf is not None else [])
            op("scalar", fn, reads=rd, writes=[st["tb1"]], tag=tag + "_sq")
            op("vector", lambda g: g.reciprocal(dst_buf.t[:, :], st["tb1"].t[:, :]),
               reads=[st["tb1"]], writes=[dst_buf], tag=tag + "_rc")

        def stats_tile(xa, jj, s1_st, s2_st, tag):
            op("scalar", lambda g, xa=xa, jj=jj, s1_st=s1_st: g.activation(
                    twt[0].t[:, :], xa.t[:, :], AF.Copy,
                    accum_out=s1_st.t[:, jj:jj + 1]),
               reads=[xa], writes=[twt[0], s1_st], tag=f"{tag}_s1")
            op("scalar", lambda g, xa=xa, jj=jj, s2_st=s2_st: g.activation(
                    twt[1].t[:, :], xa.t[:, :], AF.Square,
                    accum_out=s2_st.t[:, jj:jj + 1]),
               reads=[xa], writes=[twt[1], s2_st], tag=f"{tag}_s2")

        def batch_ln_math(s1_st, s2_st, tag, want_q=True):
            op("vector", lambda g, s1_st=s1_st: g.tensor_scalar(
                    st["mu"].t[:, :], s1_st.t[:, :], 1.0 / H, None, A.mult),
               reads=[s1_st], writes=[st["mu"]], tag=tag + "_mu")
            op("vector", lambda g, s2_st=s2_st: g.tensor_scalar(
                    st["t18"].t[:, :], s2_st.t[:, :], 1.0 / H, None, A.mult),
               reads=[s2_st], writes=[st["t18"]], tag=tag + "_t1")
            op("vector", lambda g: g.tensor_tensor(
                    st["tb1"].t[:, :], st["mu"].t[:, :], st["mu"].t[:, :], A.mult),
               reads=[st["mu"]], writes=[st["tb1"]], tag=tag + "_m2")
            op("vector", lambda g: g.tensor_tensor(
                    st["var"].t[:, :], st["t18"].t[:, :], st["tb1"].t[:, :],
                    A.subtract),
               reads=[st["t18"], st["tb1"]], writes=[st["var"]], tag=tag + "_var")
            batch_rsqrt(st["var"], epsb["e5"], st["rstd"], tag + "_rsq")
            if want_q:
                op("vector", lambda g: g.tensor_tensor(
                        st["tb1"].t[:, :], st["var"].t[:, :], st["rstd"].t[:, :],
                        A.mult),
                   reads=[st["var"], st["rstd"]], writes=[st["tb1"]], tag=tag + "_q1")
                op("vector", lambda g: g.tensor_tensor(
                        st["rmsq"].t[:, :], st["tb1"].t[:, :], st["rstd"].t[:, :],
                        A.mult),
                   reads=[st["tb1"], st["rstd"]], writes=[st["rmsq"]], tag=tag + "_q2")
                if want_q is True:
                    batch_rsqrt(st["rmsq"], epsb["e6"], st["rstdr"], tag + "_rsr")
                batch_rsqrt(st["rmsq"], epsb["e8"], st["rstdrms"], tag + "_rsm")
            op("vector", lambda g: g.tensor_tensor(
                    st["tb1"].t[:, :], st["mu"].t[:, :], st["rstd"].t[:, :], A.mult),
               reads=[st["mu"], st["rstd"]], writes=[st["tb1"]], tag=tag + "_n1")
            op("vector", lambda g: g.tensor_scalar(
                    st["nmr8"].t[:, :], st["tb1"].t[:, :], -1.0, None, A.mult),
               reads=[st["tb1"]], writes=[st["nmr8"]], tag=tag + "_n2")

        def mlp_post_a(jj, ji, tag):
            yb = pool4[ji]
            op("scalar", lambda g, yb=yb: g.activation(lnt.t[:, 0:II], yb.t[:, 0:II],
                                                       AF.Silu),
               reads=[yb], writes=[lnt], tag=f"{tag}_silu")
            op("vector", lambda g, yb=yb: g.tensor_tensor(
                    yb.t[:, 0:II], lnt.t[:, 0:II], yb.t[:, II:2 * II], A.mult),
               reads=[lnt, yb], writes=[yb], tag=f"{tag}_h")
            op("scalar", lambda g, yb=yb, jj=jj: g.activation(
                    ddq.t[:, 0:II], yb.t[:, 0:II], AF.Square,
                    accum_out=st["rmsE"].t[:, jj:jj + 1]),
               reads=[yb], writes=[ddq, st["rmsE"]], tag=f"{tag}_h2")

        def mlp_post_b(jj, ji, osc_st, gain_i, mc_d, qdst, tag):
            quant_tile(pool4[ji], 0, II, gain_i, st["rrE"],
                       st["rrE"].t[:, jj:jj + 1], mc_d, osc_st, jj, tag + "_q2")
            transpose_q_to_fm(qdst, jj, IC, f"{tag}_qT")

        def mlp_post_group(jjg, osc_st, gain_i, mc_d, qdst, tag):
            for ji, jj in enumerate(jjg):
                mlp_post_a(jj, ji, f"{tag}a{jj}")
            batch_rsqrt(st["rmsE"], epsb["e8"], st["rrE"], f"{tag}_rr",
                        scale_buf=epsb["invII"])
            for ji, jj in enumerate(jjg):
                mlp_post_b(jj, ji, osc_st, gain_i, mc_d, qdst, f"{tag}b{jj}")




        def mlp_mm(qfm, kchunks, wt_dram, n_out, jjgrp, evac_cb, tag,
                   banks=(0, 1, 2, 3), defer_qfm=False):
            toks = []
            nblocks = n_out // 512
            for nb in range(nblocks):
                for k in range(kchunks):
                    wt = wtile[(nb * kchunks + k) % 8]
                    dma("sync", f"wt{(nb * kchunks + k) % 8}", wt.t[:, :],
                        wt_dram.t[128 * k:128 * k + 128, 512 * nb:512 * nb + 512],
                        reads=[wt_dram, wt], writes=[wt], tag=f"{tag}_w{nb}_{k}")
                    for ji, jj in enumerate(jjgrp):
                        bank = pb[banks[ji]]
                        fn = lambda g, k=k, jj=jj, wt=wt, bank=bank, qfm=qfm, \
                            kchunks=kchunks: g.matmul(
                                bank.t[:, :],
                                qfm.t[:, 1024 * k + 128 * jj:1024 * k + 128 * jj + 128],
                                wt.t[:, :],
                                start=(k == 0), stop=(k == kchunks - 1))
                        if defer_qfm:
                            t_ = op("tensor", fn, reads=[wt], writes=[bank],
                                    deps=list(qfm.wtok), tag=f"{tag}_mm{nb}_{k}_{jj}")
                            toks.append(t_)
                        else:
                            op("tensor", fn, reads=[qfm, wt], writes=[bank],
                               tag=f"{tag}_mm{nb}_{k}_{jj}")
                for ji, jj in enumerate(jjgrp):
                    evac_cb(jj, ji, 512 * nb, pb[banks[ji]])
            return toks

        def sh_mm1_evac(jj, ji, n0, pbuf):
            op("scalar", lambda g, jj=jj, ji=ji, n0=n0, pbuf=pbuf: g.activation(
                    pool4[ji].t[:, n0:n0 + 512], pbuf.t[:, :], AF.Copy,
                    scale=st["alph"].t[:, jj:jj + 1]),
               reads=[pbuf, st["alph"]], writes=[pool4[ji]], tag=f"shev{jj}_{n0}")

        for jj in range(NT):
            xa = xtile[jj % 2]
            dma("sync", f"xt{jj % 2}", xa.t[:, :], x_own[128 * jj:128 * jj + 128, :],
                reads=[xa], writes=[xa], tag=f"xsload{jj}")
            stats_tile(xa, jj, st["s1x"], st["s2x"], f"Rs{jj}")
            tern_stats_stream(sWg_d, HC, "sWg", wfb, cbase=0, k0=2 * jj,
                              k1=2 * jj + 2)
        batch_ln_math(st["s1x"], st["s2x"], "Rb")
        tern_stats_reduce(HC, "sWg", pb[5], cbase=0)
        tern_write(sWg_d, t_sWg, HC, "sWg", wfb)

        def pass2_tile(jj):
            xa = xtile[jj % 2]
            dma("sync", f"xt{jj % 2}", xa.t[:, :], x_own[128 * jj:128 * jj + 128, :],
                reads=[xa], writes=[xa], tag=f"xload{jj}")
            rstdc = st["rstd"].t[:, jj:jj + 1]
            op("scalar", lambda g, xa=xa, rstdc=rstdc, jj=jj: g.activation(
                    lnt.t[:, :], xa.t[:, :], AF.Identity,
                    bias=st["nmr8"].t[:, jj:jj + 1], scale=rstdc),
               reads=[xa, st["rstd"], st["nmr8"]], writes=[lnt], tag=f"R{jj}_ln1")

            lgb = pb[0] if jj < 4 else pb[1]
            sgb = pb[2] if jj < 4 else pb[3]
            off = 128 * (jj % 4)
            for k in range(HC):
                sl4 = 128 * (k % 4)
                op("tensor", lambda g, k=k, sl4=sl4: g.transpose(
                        pb[4].t[:, sl4:sl4 + 128], lnt.t[:, 128 * k:128 * k + 128],
                        c_idf.t[:, :]),
                   reads=[lnt, c_idf], writes=[pb[4]],
                    tag=f"R{jj}_T{k}")
                if (k % 4) == 3:
                    op("scalar", lambda g: g.activation(lf.t[:, :], pb[4].t[:, :], AF.Copy),
                       reads=[pb[4]], writes=[lf], tag=f"R{jj}_lfe{k}")
                    for kk in range(k - 3, k + 1):
                        op("tensor", lambda g, kk=kk, lgb=lgb, off=off: g.matmul(
                                lgb.t[0:8, off:off + 128],
                                wgp.t[:, 8 * kk:8 * kk + 8],
                                lf.t[:, 128 * (kk % 4):128 * (kk % 4) + 128],
                                start=(kk == 0), stop=(kk == HC - 1)),
                           reads=[wgp, lf], writes=[lgb],
                            tag=f"R{jj}_lg{kk}")
                        op("tensor", lambda g, kk=kk, sgb=sgb, off=off: g.matmul(
                                sgb.t[0:1, off:off + 128],
                                wsg_sb.t[:, kk:kk + 1],
                                lf.t[:, 128 * (kk % 4):128 * (kk % 4) + 128],
                                start=(kk == 0), stop=(kk == HC - 1)),
                           reads=[wsg_sb, lf], writes=[sgb],
                            tag=f"R{jj}_sg{kk}")

            quant_tile(lnt, 0, H, gain2048, st["rstdrms"], st["rstdrms"].t[:, jj:jj + 1],
                       "sWg", st["alph"], jj, f"R{jj}_q1")
            transpose_q_to_fm(qfm_a, jj, HC, f"R{jj}_q1T")

        stats_specs = [(sWd_d, IC, "sWd", 16), (Wg_d, HC, "Wg", 0),
                       (Wd_d, IC, "Wd", 16)]
        chunk_plan = []
        for srcp, kc, nm, cb in stats_specs:
            chunk_plan += [(srcp, kc, nm, cb, k) for k in range(kc)]
        for jj in range(4):
            pass2_tile(jj)
            for (srcp, kc, nm, cb, k) in chunk_plan[8 * jj:8 * jj + 8]:
                tern_stats_stream(srcp, kc, nm, wfb, cbase=cb, k0=k, k1=k + 1)
            if jj == 0:
                tern_stats_reduce(IC, "sWd", pb[5], cbase=16)
            elif jj == 2:
                tern_stats_reduce(HC, "Wg", pb[5], cbase=0)
            elif jj == 3:
                tern_stats_reduce(IC, "Wd", pb[5], cbase=16)
        op("vector", lambda g: g.tensor_copy(lgf.t[:, 0:512], pb[0].t[0:8, :]),
           reads=[pb[0]], writes=[lgf], tag="lg_ev0")
        op("vector", lambda g: g.tensor_copy(sgr.t[0:1, 0:512], pb[2].t[0:1, :]),
           reads=[pb[2]], writes=[sgr], tag="sg_ev0")
        mm1_toks = mlp_mm(qfm_a, HC, t_sWg, 2 * II, [0, 1, 2, 3], sh_mm1_evac,
                          "shmm1g0", banks=(0, 2, 5, 6), defer_qfm=True)
        for _w in wfb:
            qfm_b.rtoks += _w.rtoks + _w.wtok
        mlp_post_group([0, 1, 2, 3], st["osc"], gain1024, "sWd", qfm_b, "shp0")
        if want_tap("h0"):
            dma("sync", "tpG", tap_aps["h0"], pool4[0].t[:, 0:II],
                reads=[pool4[0]], tag="tp_h0")
        if want_tap("rrE0"):
            dma("sync", "tpH", tap_aps["rrE0"], st["rrE"].t[:, :],
                reads=[st["rrE"]], tag="tp_rrE0")
        if want_tap("y0"):
            dma("sync", "tpF", tap_aps["y0"], pool4[0].t[:, :], reads=[pool4[0]],
                tag="tp_y0")
        for jj in range(4, NT):
            pass2_tile(jj)
        op("vector", lambda g: g.tensor_copy(lgf.t[:, 512:1024], pb[1].t[0:8, :]),
           reads=[pb[1]], writes=[lgf], tag="lg_ev1")
        op("vector", lambda g: g.tensor_copy(sgr.t[0:1, 512:1024], pb[3].t[0:1, :]),
           reads=[pb[3]], writes=[sgr], tag="sg_ev1")
        mm1_toks += mlp_mm(qfm_a, HC, t_sWg, 2 * II, [4, 5, 6, 7], sh_mm1_evac,
                           "shmm1g4", banks=(0, 1, 2, 3), defer_qfm=True)
        qfm_a.rtoks += mm1_toks
        mlp_post_group([4, 5, 6, 7], st["osc"], gain1024, "sWd", qfm_b, "shp4")

        for j in range(NT):
            op("tensor", lambda g, j=j: g.transpose(
                    pb[4].t[:, 8 * j:8 * j + 8], lgf.t[0:8, 128 * j:128 * j + 128],
                    c_idf.t[0:8, 0:8]),
               reads=[lgf, c_idf], writes=[pb[4]],
                tag=f"lgT{j}")
        op("vector", lambda g: g.tensor_copy(ltm.t[:, :], pb[4].t[:, 0:64]),
           reads=[pb[4]], writes=[ltm], tag="ltm_ev")
        for j in range(NT):
            op("tensor", lambda g, j=j: g.transpose(
                    pb[4].t[:, j:j + 1], sgr.t[0:1, 128 * j:128 * j + 128],
                    c_idf.t[0:1, 0:1]),
               reads=[sgr, c_idf], writes=[pb[4]],
                tag=f"sgT{j}")
        op("vector", lambda g: g.tensor_copy(sgv.t[:, :], pb[4].t[:, 0:8]),
           reads=[pb[4]], writes=[sgv], tag="sgv_ev")

        for jj in range(NT):
            rrc = st["rstdr"].t[:, jj:jj + 1]
            op("vector", lambda g, jj=jj, rrc=rrc: g.tensor_scalar(
                    r8["p"].t[:, :], ltm.t[:, 8 * jj:8 * jj + 8], rrc, None, A.mult),
               reads=[ltm, st["rstdr"]], writes=[r8["p"]], tag=f"rt{jj}_l")
            op("vector", lambda g: g.reduce_max(rcol["nmax"].t[:, :], r8["p"].t[:, :],
                                                axis=AX.X, negate=True),
               reads=[r8["p"]], writes=[rcol["nmax"]], tag=f"rt{jj}_nm")
            op("scalar", lambda g: g.activation(r8["eq"].t[:, :], r8["p"].t[:, :], AF.Exp,
                                                bias=rcol["nmax"].t[:, :],
                                                accum_out=rcol["ssum"].t[:, :]),
               reads=[r8["p"], rcol["nmax"]], writes=[r8["eq"], rcol["ssum"]],
               tag=f"rt{jj}_exp")
            op("vector", lambda g: g.reciprocal(rcol["rs"].t[:, :], rcol["ssum"].t[:, :]),
               reads=[rcol["ssum"]], writes=[rcol["rs"]], tag=f"rt{jj}_rs")
            op("vector", lambda g: g.tensor_scalar(r8["p"].t[:, :], r8["eq"].t[:, :],
                                                   rcol["rs"].t[:, :], None, A.mult),
               reads=[r8["eq"], rcol["rs"]], writes=[r8["p"]], tag=f"rt{jj}_p")
            op("vector", lambda g, jj=jj: g.reduce_max(st["m1"].t[:, jj:jj + 1],
                                                       r8["p"].t[:, :], axis=AX.X),
               reads=[r8["p"]], writes=[st["m1"]], tag=f"rt{jj}_m1")
            m1c = st["m1"].t[:, jj:jj + 1]
            op("vector", lambda g, m1c=m1c: g.tensor_scalar(
                    r8["eq"].t[:, :], r8["p"].t[:, :], m1c, None, A.is_ge),
               reads=[r8["p"], st["m1"]], writes=[r8["eq"]], tag=f"rt{jj}_eq")
            op("vector", lambda g: g.tensor_tensor_scan(
                    r8["cs"].t[:, :], r8["eq"].t[:, :], z8.t[:, :], 0.0, A.add, A.add),
               reads=[r8["eq"], z8], writes=[r8["cs"]], tag=f"rt{jj}_cs")
            op("vector", lambda g: g.tensor_scalar(r8["c1"].t[:, :], r8["cs"].t[:, :],
                                                   1.0, None, A.is_le),
               reads=[r8["cs"]], writes=[r8["c1"]], tag=f"rt{jj}_c1")
            op("vector", lambda g, jj=jj: g.tensor_tensor(
                    stf1[jj].t[:, :], r8["eq"].t[:, :], r8["c1"].t[:, :], A.mult),
               reads=[r8["eq"], r8["c1"]], writes=[stf1[jj]], tag=f"rt{jj}_f1")
            op("vector", lambda g, jj=jj: g.scalar_tensor_tensor(
                    r8["p2"].t[:, :], stf1[jj].t[:, :], -1e30, r8["p"].t[:, :],
                    A.mult, A.add),
               reads=[stf1[jj], r8["p"]], writes=[r8["p2"]], tag=f"rt{jj}_p2")
            op("vector", lambda g, jj=jj: g.reduce_max(st["m2"].t[:, jj:jj + 1],
                                                       r8["p2"].t[:, :], axis=AX.X),
               reads=[r8["p2"]], writes=[st["m2"]], tag=f"rt{jj}_m2")
            m2c = st["m2"].t[:, jj:jj + 1]
            op("vector", lambda g, m2c=m2c: g.tensor_scalar(
                    r8["eq2"].t[:, :], r8["p2"].t[:, :], m2c, None, A.is_ge),
               reads=[r8["p2"], st["m2"]], writes=[r8["eq2"]], tag=f"rt{jj}_eq2")
            op("vector", lambda g: g.tensor_tensor_scan(
                    r8["cs2"].t[:, :], r8["eq2"].t[:, :], z8.t[:, :], 0.0, A.add, A.add),
               reads=[r8["eq2"], z8], writes=[r8["cs2"]], tag=f"rt{jj}_cs2")
            op("vector", lambda g: g.tensor_scalar(r8["c2"].t[:, :], r8["cs2"].t[:, :],
                                                   1.0, None, A.is_le),
               reads=[r8["cs2"]], writes=[r8["c2"]], tag=f"rt{jj}_c2")
            op("vector", lambda g, jj=jj: g.tensor_tensor(
                    stf2[jj].t[:, :], r8["eq2"].t[:, :], r8["c2"].t[:, :], A.mult),
               reads=[r8["eq2"], r8["c2"]], writes=[stf2[jj]], tag=f"rt{jj}_f2")
            op("vector", lambda g, jj=jj, m1c=m1c: g.tensor_scalar(
                    r8["tmp"].t[:, :], stf1[jj].t[:, :], m1c, None, A.mult),
               reads=[stf1[jj], st["m1"]], writes=[r8["tmp"]], tag=f"rt{jj}_wm1")
            op("vector", lambda g, jj=jj, m2c=m2c: g.scalar_tensor_tensor(
                    r8["wm"].t[:, :], stf2[jj].t[:, :], m2c, r8["tmp"].t[:, :],
                    A.mult, A.add),
               reads=[stf2[jj], st["m2"], r8["tmp"]], writes=[r8["wm"]], tag=f"rt{jj}_wm")
            op("tensor", lambda g, jj=jj: g.transpose(
                    pb[4].t[0:8, 128 * (jj % 4):128 * (jj % 4) + 128],
                    r8["wm"].t[:, :], c_idf.t[:, :]),
               reads=[r8["wm"], c_idf], writes=[pb[4]],
                tag=f"rt{jj}_wmT")
            if (jj % 4) == 3:
                j0 = jj - 3
                op("vector", lambda g, j0=j0: g.tensor_copy(
                        wm_sb.t[0:8, 128 * j0:128 * j0 + 512], pb[4].t[0:8, 0:512]),
                   reads=[pb[4]], writes=[wm_sb], tag=f"rt{jj}_wmev")
        op("scalar", lambda g: g.activation(st["gate"].t[:, :], sgv.t[:, :], AF.Sigmoid),
           reads=[sgv], writes=[st["gate"]], tag="gates")

        dma("gpsimd", "wm", wm_bounce.t, wm_sb.t[0:8, :], reads=[wm_sb], writes=[wm_bounce],
            tag="wm_st")
        P.op("gpsimd", lambda g: g.collective_compute(
                "AllGather", A.bypass, replica_groups=[list(range(NCORE))],
                ins=[wm_bounce.t.opt()], outs=[wm_all.t.opt()]),
             reads=[wm_bounce], writes=[wm_all], sem="ag1", tag="AG1")

        if want_tap("wm_all"):
            dma("gpsimd", "tp1", tap_aps["wm_all"], wm_all.t, reads=[wm_all], tag="tp_wma")
        if want_tap("gate"):
            dma("sync", "tp2", tap_aps["gate"], st["gate"].t[:, :], reads=[st["gate"]],
                tag="tp_gate")
        if want_tap("alph"):
            dma("sync", "tp3", tap_aps["alph"], st["alph"].t[:, :], reads=[st["alph"]],
                tag="tp_alph")
        if want_tap("q1fm"):
            dma("sync", "tp4", tap_aps["q1fm"], qfm_a.t[:, :], reads=[qfm_a], tag="tp_q1fm")

        tern_write(sWd_d, t_sWd, IC, "sWd", xtile)
        tern_write(Wg_d, t_Wg, HC, "Wg", xtile)
        tern_write(Wd_d, t_Wd, IC, "Wd", xtile)

        x1a_last = {}

        def sh_mm2_evac(jj, ji, n0, pbuf):
            acc = pool4[ji]
            op("scalar", lambda g, jj=jj, n0=n0, pbuf=pbuf, acc=acc: g.activation(
                    acc.t[:, n0:n0 + 512], pbuf.t[:, :], AF.Copy,
                    scale=st["osc"].t[:, jj:jj + 1]),
               reads=[pbuf, st["osc"]], writes=[acc], tag=f"s2e{jj}_{n0}")
            if n0 == 1024 and ji < 2:
                xs = xtile[jj % 2]
                dma("sync", f"xt{jj % 2}", xs.t[:, :],
                    x_own[128 * jj:128 * jj + 128, :],
                    reads=[xs], writes=[xs], tag=f"s2x{jj}")
            if n0 == 1536:
                xs = xtile[jj % 2]
                if ji >= 2:
                    dma("sync", f"xt{jj % 2}", xs.t[:, :],
                        x_own[128 * jj:128 * jj + 128, :],
                        reads=[xs], writes=[xs], tag=f"s2x{jj}")
                op("vector", lambda g, xs=xs, acc=acc: g.tensor_tensor(
                        acc.t[:, :], acc.t[:, :], xs.t[:, :], A.add),
                   reads=[xs, acc], writes=[acc], tag=f"s2a{jj}")
                tk = dma("sync", f"x1a{ji}", x1_dram.t[128 * jj:128 * jj + 128, :],
                         acc.t[:, :], reads=[acc], deps=list(x1_dram.rtoks),
                         tag=f"s2w{jj}")
                x1a_last[ji] = tk
                x1_dram.wtok = [t for t in x1a_last.values()]

        op("vector", lambda g: g.tensor_tensor(st["osc"].t[:, :], st["osc"].t[:, :],
                                               st["gate"].t[:, :], A.mult),
           reads=[st["osc"], st["gate"]], writes=[st["osc"]], tag="oscg")

        for grp in (0, 4):
            jjg = list(range(grp, grp + 4))
            mlp_mm(qfm_b, IC, t_sWd, H, jjg, sh_mm2_evac, f"shmm2g{grp}")

        dma("sync", "wa", wa.t[:, :], wm_all.t, reads=[wm_all], writes=[wa], tag="wa_ld")
        op("vector", lambda g: g.tensor_scalar(mk_sb.t[:, :], wa.t[:, :], 0.0, None,
                                               A.is_gt),
           reads=[wa], writes=[mk_sb], tag="mk")
        op("vector", lambda g: g.memset(wa.t[:, :], 0.0), reads=[wa], writes=[wa],
           tag="waz")
        op("vector", lambda g: g.tensor_tensor_scan(
                ci_sb.t[:, :], mk_sb.t[:, :], wa.t[:, :], 0.0, A.add, A.add),
           reads=[mk_sb, wa], writes=[ci_sb], tag="scan")
        for hh in range(2):
            sl = slice(512 * hh, 512 * hh + 512)
            bank = pb[0] if hh == 0 else pb[1]
            op("tensor", lambda g, sl=sl, bank=bank: g.matmul(
                    bank.t[0:8, :], c_selaU.t[0:64, :], ci_sb.t[:, sl],
                    start=True, stop=False),
               reads=[c_selaU, ci_sb], writes=[bank], tag=f"own1_{hh}")
            op("tensor", lambda g, sl=sl, bank=bank: g.matmul(
                    bank.t[0:8, :], c_selgt.t[0:64, :], mk_sb.t[:, sl],
                    start=False, stop=True),
               reads=[c_selgt, mk_sb], writes=[bank], tag=f"own2_{hh}")
            bank2 = pb[2] if hh == 0 else pb[3]
            op("tensor", lambda g, sl=sl, bank2=bank2: g.matmul(
                    bank2.t[0:8, :], c_selaE.t[0:64, :], ci_sb.t[:, sl],
                    start=True, stop=False),
               reads=[c_selaE, ci_sb], writes=[bank2], tag=f"exp1_{hh}")
            op("tensor", lambda g, sl=sl, bank2=bank2: g.matmul(
                    bank2.t[0:8, :], c_selgE.t[0:64, :], mk_sb.t[:, sl],
                    start=False, stop=True),
               reads=[c_selgE, mk_sb], writes=[bank2], tag=f"exp2_{hh}")
            bank3 = pb[5] if hh == 0 else pb[6]
            op("tensor", lambda g, sl=sl, bank3=bank3: g.matmul(
                    bank3.t[0:8, :], c_selidE.t[0:64, :], mk_sb.t[:, sl],
                    start=True, stop=True),
               reads=[c_selidE, mk_sb], writes=[bank3], tag=f"mke_{hh}")
        op("vector", lambda g: g.tensor_copy(inclo.t[:, 0:512], pb[0].t[0:8, :]),
           reads=[pb[0]], writes=[inclo], tag="io0")
        op("vector", lambda g: g.tensor_copy(inclo.t[:, 512:1024], pb[1].t[0:8, :]),
           reads=[pb[1]], writes=[inclo], tag="io1")
        op("vector", lambda g: g.tensor_copy(incle.t[:, 0:512], pb[2].t[0:8, :]),
           reads=[pb[2]], writes=[incle], tag="ie0")
        op("vector", lambda g: g.tensor_copy(incle.t[:, 512:1024], pb[3].t[0:8, :]),
           reads=[pb[3]], writes=[incle], tag="ie1")
        op("vector", lambda g: g.tensor_copy(mke.t[:, 0:512], pb[5].t[0:8, :]),
           reads=[pb[5]], writes=[mke], tag="me0")
        op("vector", lambda g: g.tensor_copy(mke.t[:, 512:1024], pb[6].t[0:8, :]),
           reads=[pb[6]], writes=[mke], tag="me1")
        op("vector", lambda g: g.tensor_scalar(wa.t[0:8, 0:1024], incle.t[:, :], float(CAP),
                                               None, A.is_le),
           reads=[incle], writes=[wa], tag="vE1")
        op("vector", lambda g: g.tensor_tensor(wa.t[0:8, 0:1024], wa.t[0:8, 0:1024], mke.t[:, :],
                                               A.mult),
           reads=[wa, mke], writes=[wa], tag="vE2")
        op("vector", lambda g: g.tensor_scalar(incle.t[:, :], incle.t[:, :], -1.0, None,
                                               A.add),
           reads=[incle], writes=[incle], tag="posE")
        op("vector", lambda g: g.tensor_tensor(scatf.t[:, :], incle.t[:, :],
                                               wa.t[0:8, 0:1024], A.mult),
           reads=[incle, wa], writes=[scatf], tag="posEm")
        op("vector", lambda g: g.tensor_scalar(wa.t[0:8, 0:1024], wa.t[0:8, 0:1024], -BIGF, BIGF,
                                               A.mult, A.add),
           reads=[wa], writes=[wa], tag="invE")
        op("vector", lambda g: g.tensor_tensor(scatf.t[:, :], scatf.t[:, :],
                                               wa.t[0:8, 0:1024], A.add),
           reads=[scatf, wa], writes=[scatf], tag="scatf")
        for j in range(8):
            op("tensor", lambda g, j=j: g.transpose(
                    pb[4].t[:, 8 * j:8 * j + 8], scatf.t[0:8, 128 * j:128 * j + 128],
                    c_idf.t[0:8, 0:8]),
               reads=[scatf, c_idf], writes=[pb[4]],
                tag=f"scT{j}")
        op("vector", lambda g: g.tensor_copy(scat_tm.t[:, :], pb[4].t[:, 0:64]),
           reads=[pb[4]], writes=[scat_tm], tag="sc_ev")
        op("vector", lambda g: g.tensor_copy(scat_i.t[:, :], scat_tm.t[:, :]),
           reads=[scat_tm], writes=[scat_i], tag="sc_i32")
        for j in range(8):
            op("tensor", lambda g, j=j: g.transpose(
                    pb[4].t[:, 8 * j:8 * j + 8], inclo.t[0:8, 128 * j:128 * j + 128],
                    c_idf.t[0:8, 0:8]),
               reads=[inclo, c_idf], writes=[pb[4]],
                tag=f"ioT{j}")
        op("vector", lambda g: g.tensor_copy(inclo_tm.t[:, :], pb[4].t[:, 0:64]),
           reads=[pb[4]], writes=[inclo_tm], tag="io_ev")

        if want_tap("scat_tm"):
            dma("sync", "tp5", tap_aps["scat_tm"], scat_tm.t[:, :], reads=[scat_tm], tag="tp_sc")
        if want_tap("inclo_tm"):
            dma("sync", "tp6", tap_aps["inclo_tm"], inclo_tm.t[:, :], reads=[inclo_tm],
                tag="tp_io")

        op("vector", lambda g: g.memset(dfill.t[:, :], T), writes=[dfill], tag="dfill")
        dma("gpsimd", "dd", disp_dram.t.rearrange("(p c) one -> p (c one)", p=128),
            dfill.t[:, :], reads=[dfill], writes=[disp_dram], tag="disp_pre")
        pre_tok = disp_dram.wtok
        last_tok = None
        for rr in range(8):
            for j in range(8):
                col = 8 * j + rr
                ic = 8 * rr + j
                last_tok = P.op("gpsimd", lambda g, col=col, ic=ic: g.indirect_dma_start(
                        disp_dram.t,
                        IndirectOffsetOnAxis(ap=scat_i.t[:, col:col + 1], axis=0),
                        c_iota.t[:, ic:ic + 1], None,
                        bounds_check=REGS["cap"], oob_is_err=False),
                     reads=[scat_i, c_iota], deps=pre_tok, dma=True, sem="dd",
                     tag=f"dsp{rr}_{j}")
        disp_dram.wtok = [last_tok]
        for g8 in range(8):
            tk = dma("sync", "dsp", dsp_sb.t[:, g8:g8 + 1],
                     disp_dram.t[128 * g8:128 * g8 + 128, :],
                     reads=[disp_dram], tag=f"dspld{g8}")
            dsp_sb.wtok = [tk]

        if want_tap("disp"):
            dma("sync", "tp11", tap_aps["disp"], dsp_sb.t[:, :], reads=[dsp_sb], tag="tp_dsp")

        if upto == "route":
            return finish()



        if want_tap("x_shared"):
            dma("sync", "tp12", tap_aps["x_shared"], x1_dram.t, reads=[x1_dram], tag="tp_xs")
        if want_tap("q2fm"):
            dma("sync", "tpD", tap_aps["q2fm"], qfm_b.t[:, :], reads=[qfm_b],
                tag="tp_q2fm")
        if want_tap("osc8"):
            dma("sync", "tpE", tap_aps["osc8"], st["osc"].t[:, :],
                reads=[st["osc"]], tag="tp_osc8")
        if upto == "shared":
            return finish()

        replicate_row(gain2048, Ng_r, H, "ng")
        replicate_row(gain1024, Nd_r, II, "nd")

        def ex_mm1_evac(jj, ji, n0, pbuf):
            op("scalar", lambda g, ji=ji, jj=jj, n0=n0, pbuf=pbuf: g.activation(
                    pool4[ji].t[:, n0:n0 + 512], pbuf.t[:, :], AF.Copy,
                    scale=ste["alph"].t[:, jj:jj + 1]),
               reads=[pbuf, ste["alph"]], writes=[pool4[ji]], tag=f"exev{jj}_{n0}")

        eo_last = {}

        def ex_mm2_evac(jj, ji, n0, pbuf):
            acc = pool4[ji]
            op("scalar", lambda g, jj=jj, n0=n0, pbuf=pbuf, acc=acc: g.activation(
                    acc.t[:, n0:n0 + 512], pbuf.t[:, :], AF.Copy,
                    scale=ste["osc"].t[:, jj:jj + 1]),
               reads=[pbuf, ste["osc"]], writes=[acc], tag=f"e2e{jj}_{n0}")
            if n0 == 1536:
                tk = dma("sync", f"eo{ji}", eo_bounce.t[128 * jj:128 * jj + 128, :],
                         acc.t[:, :], reads=[acc], deps=list(eo_bounce.rtoks),
                         tag=f"e2s{jj}")
                eo_last[ji] = tk
                eo_bounce.wtok = [t for t in eo_last.values()]

        for grp in (0, 4):
            jjg = list(range(grp, grp + 4))
            for ji, jj in enumerate(jjg):
                op("vector", lambda g, ji=ji: g.memset(pool4[ji].t[:, :], 0.0),
                   writes=[pool4[ji]], tag=f"ex{grp}_ms{ji}")
                P.op("gpsimd", lambda g, ji=ji, jj=jj: g.indirect_dma_start(
                        pool4[ji].t[:, :], None,
                        x_full, IndirectOffsetOnAxis(ap=dsp_sb.t[:, jj:jj + 1], axis=0),
                        bounds_check=REGS["t"], oob_is_err=False),
                     reads=[dsp_sb], writes=[pool4[ji]], dma=True, sem=f"g{ji}",
                     tag=f"ex{grp}_g{jj}")
            for ji, jj in enumerate(jjg):
                stats_tile(pool4[ji], jj, st["s1x"], st["s2x"], f"exs{jj}")
            batch_ln_math(st["s1x"], st["s2x"], f"exb{grp}", want_q="rms")
            for ji, jj in enumerate(jjg):
                xg = pool4[ji]
                rstdc = st["rstd"].t[:, jj:jj + 1]
                op("scalar", lambda g, xg=xg, rstdc=rstdc, jj=jj: g.activation(
                        lnt.t[:, :], xg.t[:, :], AF.Identity,
                        bias=st["nmr8"].t[:, jj:jj + 1], scale=rstdc),
                   reads=[xg, st["rstd"], st["nmr8"]], writes=[lnt], tag=f"exl{jj}_ln")
                quant_tile(lnt, 0, H, gain2048, st["rstdrms"],
                           st["rstdrms"].t[:, jj:jj + 1], "Wg",
                           ste["alph"], jj, f"exq1_{jj}")
                transpose_q_to_fm(qfm_a, jj, HC, f"exq1T{jj}")
            mlp_mm(qfm_a, HC, t_Wg, 2 * II, jjg, ex_mm1_evac, f"exmm1g{grp}")
            mlp_post_group(jjg, ste["osc"], gain1024, "Wd", qfm_b, f"exp{grp}")
            mlp_mm(qfm_b, IC, t_Wd, H, jjg, ex_mm2_evac, f"exmm2g{grp}")
            half = eo_A if grp == 0 else eo_B
            r0 = 0 if grp == 0 else 512
            P.op("gpsimd", lambda g, half=half, r0=r0: g.collective_compute(
                    "AllGather", A.bypass, replica_groups=[list(range(NCORE))],
                    ins=[eo_bounce.t[r0:r0 + 512, :].opt()], outs=[half.t.opt()]),
                 reads=[eo_bounce], writes=[half], sem=f"ag2{r0}",
                 tag=f"AG2_{r0}")

        for jj in range(NT):
            isl = inclo_tm.t[:, 8 * jj:8 * jj + 8]
            op("vector", lambda g, isl=isl: g.tensor_scalar(
                    r8["va"].t[:, :], isl, float(CAP), None, A.is_le),
               reads=[inclo_tm], writes=[r8["va"]], tag=f"oi{jj}_v")
            op("vector", lambda g, isl=isl: g.tensor_scalar(
                    r8["po"].t[:, :], isl, -1.0, None, A.add),
               reads=[inclo_tm], writes=[r8["po"]], tag=f"oi{jj}_p")
            op("vector", lambda g: g.tensor_tensor(
                    r8["ia"].t[:, :], r8["po"].t[:, :], c_eoff2.t[:, :], A.add),
               reads=[r8["po"], c_eoff2], writes=[r8["ia"]], tag=f"oi{jj}_ia")
            for which, fst, wst in ((1, stf1[jj], "w1"), (2, stf2[jj], "w2")):
                op("vector", lambda g, fst=fst: g.tensor_tensor(
                        r8["v1"].t[:, :], fst.t[:, :], r8["va"].t[:, :], A.mult),
                   reads=[fst, r8["va"]], writes=[r8["v1"]], tag=f"oi{jj}_{which}v")
                op("vector", lambda g: g.tensor_tensor(
                        r8["t1"].t[:, :], r8["ia"].t[:, :], r8["v1"].t[:, :], A.mult),
                   reads=[r8["ia"], r8["v1"]], writes=[r8["t1"]], tag=f"oi{jj}_{which}i")
                op("vector", lambda g: g.reduce_sum(rcol["i1s"].t[:, :], r8["t1"].t[:, :],
                                                    axis=AX.X),
                   reads=[r8["t1"]], writes=[rcol["i1s"]], tag=f"oi{jj}_{which}s")
                op("vector", lambda g: g.reduce_sum(rcol["n1"].t[:, :], r8["v1"].t[:, :],
                                                    axis=AX.X),
                   reads=[r8["v1"]], writes=[rcol["n1"]], tag=f"oi{jj}_{which}n")
                op("vector", lambda g: g.tensor_tensor(
                        r8["t1"].t[:, :], r8["po"].t[:, :], r8["v1"].t[:, :], A.mult),
                   reads=[r8["po"], r8["v1"]], writes=[r8["t1"]], tag=f"oi{jj}_{which}pm")
                op("vector", lambda g: g.reduce_sum(rcol["pm"].t[:, :], r8["t1"].t[:, :],
                                                    axis=AX.X),
                   reads=[r8["t1"]], writes=[rcol["pm"]], tag=f"oi{jj}_{which}ps")
                op("vector", lambda g: g.tensor_scalar(
                        rcol["vb"].t[:, :], rcol["pm"].t[:, :], 512.0, None, A.is_ge),
                   reads=[rcol["pm"]], writes=[rcol["vb"]], tag=f"oi{jj}_{which}hb")
                op("vector", lambda g: g.tensor_tensor(
                        rcol["vbn"].t[:, :], rcol["vb"].t[:, :], rcol["n1"].t[:, :],
                        A.mult),
                   reads=[rcol["vb"], rcol["n1"]], writes=[rcol["vbn"]],
                   tag=f"oi{jj}_{which}hn")
                op("vector", lambda g: g.tensor_tensor(
                        rcol["van"].t[:, :], rcol["n1"].t[:, :], rcol["vbn"].t[:, :],
                        A.subtract),
                   reads=[rcol["n1"], rcol["vbn"]], writes=[rcol["van"]],
                   tag=f"oi{jj}_{which}ha")
                for hnm, vsel, sub in ((f"i{which}A", "van", 0.0),
                                       (f"i{which}B", "vbn", -512.0)):
                    op("vector", lambda g, vsel=vsel: g.tensor_scalar(
                            rcol["big1"].t[:, :], rcol[vsel].t[:, :], -BIGF, BIGF,
                            A.mult, A.add),
                       reads=[rcol[vsel]], writes=[rcol["big1"]],
                       tag=f"oi{jj}_{which}b{hnm}")
                    op("vector", lambda g, sub=sub: g.tensor_scalar(
                            rcol["isb"].t[:, :], rcol["i1s"].t[:, :], sub, None,
                            A.add),
                       reads=[rcol["i1s"]], writes=[rcol["isb"]],
                       tag=f"oi{jj}_{which}o{hnm}")
                    op("vector", lambda g, jj=jj, hnm=hnm: g.tensor_tensor(
                            st[hnm].t[:, jj:jj + 1], rcol["isb"].t[:, :],
                            rcol["big1"].t[:, :], A.add),
                       reads=[rcol["isb"], rcol["big1"]], writes=[st[hnm]],
                       tag=f"oi{jj}_{which}x{hnm}")
                mcol = st["m1"] if which == 1 else st["m2"]
                op("vector", lambda g, jj=jj, mcol=mcol, wst=wst: g.tensor_tensor(
                        st[wst].t[:, jj:jj + 1], mcol.t[:, jj:jj + 1],
                        rcol["n1"].t[:, :], A.mult),
                   reads=[mcol, rcol["n1"]], writes=[st[wst]], tag=f"oi{jj}_{which}w")
        for inm, isrc in (("1A", "i1A"), ("1B", "i1B"), ("2A", "i2A"),
                          ("2B", "i2B")):
            op("vector", lambda g, inm=inm, isrc=isrc: g.tensor_copy(
                    idxh_i[inm].t[:, :], st[isrc].t[:, :]),
               reads=[st[isrc]], writes=[idxh_i[inm]], tag=f"ix{inm}")
        if want_tap("idx1"):
            dma("sync", "tp7", tap_aps["idx1"], st["i1A"].t[:, :], reads=[st["i1A"]],
                tag="tp_i1")
        if want_tap("idx2"):
            dma("sync", "tp8", tap_aps["idx2"], st["i2A"].t[:, :], reads=[st["i2A"]],
                tag="tp_i2")
        if want_tap("w1"):
            dma("sync", "tp9", tap_aps["w1"], st["w1"].t[:, :], reads=[st["w1"]], tag="tp_w1")
        if want_tap("w2"):
            dma("sync", "tp10", tap_aps["w2"], st["w2"].t[:, :], reads=[st["w2"]], tag="tp_w2")



        if want_tap("eo_own"):
            dma("gpsimd", "tp13", tap_aps["eo_own"], eo_bounce.t, reads=[eo_bounce], tag="tp_eo")

        for jj in range(NT):
            xs = xtile[jj % 2]
            r1t = pool4[2 * (jj % 2)]
            r2t = pool4[2 * (jj % 2) + 1]
            op("vector", lambda g, r1t=r1t: g.memset(r1t.t[:, :], 0.0),
               writes=[r1t], tag=f"oc{jj}_m1")
            op("vector", lambda g, r2t=r2t: g.memset(r2t.t[:, :], 0.0),
               writes=[r2t], tag=f"oc{jj}_m2")
            for half, hl in ((eo_A, "A"), (eo_B, "B")):
                P.op("gpsimd", lambda g, jj=jj, r1t=r1t, half=half, hl=hl:
                     g.indirect_dma_start(
                        r1t.t[:, :], None, half.t,
                        IndirectOffsetOnAxis(ap=idxh_i["1" + hl].t[:, jj:jj + 1],
                                             axis=0),
                        bounds_check=REGS["h"], oob_is_err=False),
                     reads=[idxh_i["1" + hl], half], writes=[r1t], dma=True,
                     sem=f"g{2 * (jj % 2)}", tag=f"oc{jj}_g1{hl}")
                P.op("gpsimd", lambda g, jj=jj, r2t=r2t, half=half, hl=hl:
                     g.indirect_dma_start(
                        r2t.t[:, :], None, half.t,
                        IndirectOffsetOnAxis(ap=idxh_i["2" + hl].t[:, jj:jj + 1],
                                             axis=0),
                        bounds_check=REGS["h"], oob_is_err=False),
                     reads=[idxh_i["2" + hl], half], writes=[r2t], dma=True,
                     sem=f"g{2 * (jj % 2) + 1}", tag=f"oc{jj}_g2{hl}")
            dma("sync", f"xt{jj % 2}", xs.t[:, :], x1_dram.t[128 * jj:128 * jj + 128, :],
                reads=[x1_dram, xs], writes=[xs], tag=f"oc{jj}_xl")
            op("vector", lambda g, jj=jj, r1t=r1t: g.tensor_scalar(
                    zq.t[:, :], r1t.t[:, :], st["w1"].t[:, jj:jj + 1], None,
                    A.mult),
               reads=[r1t, st["w1"]], writes=[zq], tag=f"oc{jj}_t1")
            op("vector", lambda g, jj=jj, r2t=r2t: g.scalar_tensor_tensor(
                    c0q.t[:, :], r2t.t[:, :], st["w2"].t[:, jj:jj + 1],
                    zq.t[:, :], A.mult, A.add),
               reads=[r2t, st["w2"], zq], writes=[c0q], tag=f"oc{jj}_t2")
            op("vector", lambda g, jj=jj, xs=xs: g.tensor_tensor(
                    xs.t[:, :], xs.t[:, :], c0q.t[:, :], A.add),
               reads=[c0q, xs], writes=[xs], tag=f"oc{jj}_x1")
            stats_tile(xs, jj, st["s1x"], st["s2x"], f"L2s{jj}")
            tk = dma("sync", "x1b", x1_dram.t[128 * jj:128 * jj + 128, :], xs.t[:, :],
                     reads=[xs], deps=list(x1_dram.rtoks), tag=f"oc{jj}_xw")
            x1_dram.wtok = [tk]

        if want_tap("x1"):
            dma("sync", "tp14", tap_aps["x1"], x1_dram.t, reads=[x1_dram], tag="tp_x1")
        if upto == "x1":
            return finish()

        batch_ln_math(st["s1x"], st["s2x"], "L2b", want_q=False)
        for jj in range(NT):
            xa = xtile[jj % 2]
            dma("sync", f"xt{jj % 2}", xa.t[:, :], x1_dram.t[128 * jj:128 * jj + 128, :],
                reads=[x1_dram, xa], writes=[xa], tag=f"L2{jj}_xl")
            rstdc = st["rstd"].t[:, jj:jj + 1]
            op("scalar", lambda g, xa=xa, rstdc=rstdc, jj=jj: g.activation(
                    qtm.t[:, :], xa.t[:, :], AF.Identity,
                    bias=st["nmr8"].t[:, jj:jj + 1], scale=rstdc),
               reads=[xa, st["rstd"], st["nmr8"]], writes=[qtm], tag=f"L2{jj}_c")
            transpose_q_to_fm(qfm_a, jj, HC, f"L2{jj}_T")

        if want_tap("ln2fm"):
            dma("sync", "tpA", tap_aps["ln2fm"], qfm_a.t[:, :], reads=[qfm_a],
                tag="tp_ln2fm")

        moe_deps = P.all_toks()
        moe_stack.close()

        h1 = Buf(nc.alloc_sbuf_tensor("h1", [128, 32 * 1024], BF16), "h1")
        h2 = Buf(nc.alloc_sbuf_tensor("h2", [128, 32 * 1024], BF16), "h2")
        wf32 = [sb(f"wf32_{i}", (128, 384), F32) for i in range(5)]
        wbf = [sb(f"wbf_{i}", (128, 384), BF16) for i in range(5)]
        x1s = [sb(f"x1s_{i}", (128, 512), F32) for i in range(2)]
        outsb = [sb(f"outsb_{i}", (128, 512), F32) for i in range(2)]
        w3f = [sb(f"w3f_{i}", (128, 512), F32) for i in range(4)]
        w3b = [sb(f"w3b_{i}", (128, 512), BF16) for i in range(4)]
        b3row = sb("b3row", (1, H), F32)
        for b in (h1, h2, *wf32, *wbf, x1s[0], x1s[1],
                  outsb[0], outsb[1], *w3f, *w3b, b3row):
            b.wtok = list(moe_deps)
        dma("sync", "b3", b3row.t[:, :], b3r, reads=[b3row], writes=[b3row], tag="b3_ld")

        def ffn_layer(src_fm, kchunks, wt_ap, n_out, bias_t, dst_fm, tag):
            nch = n_out // 128
            groups = [list(range(i, min(i + 3, nch))) for i in range(0, nch, 3)]
            for gi, grp in enumerate(groups):
                gw = 128 * len(grp)
                n0 = 128 * grp[0]
                for k in range(kchunks):
                    wfb = wf32[k % 5]
                    wbb = wbf[k % 5]
                    dma("sync" if k % 2 == 0 else "scalar", f"wf{k % 5}",
                        wfb.t[:, 0:gw],
                        wt_ap[128 * k:128 * k + 128, n0:n0 + gw],
                        reads=[wfb], writes=[wfb], tag=f"{tag}_wl{gi}_{k}")
                    op("vector", lambda g, wfb=wfb, wbb=wbb, gw=gw: g.tensor_copy(
                            wbb.t[:, 0:gw], wfb.t[:, 0:gw]),
                       reads=[wfb], writes=[wbb], tag=f"{tag}_wc{gi}_{k}")
                    for ni in range(len(grp)):
                        for th in range(2):
                            bank = pb[2 * ni + th]
                            op("tensor", lambda g, k=k, ni=ni, th=th, wbb=wbb, bank=bank,
                               src_fm=src_fm, kchunks=kchunks: g.matmul(
                                    bank.t[:, :], wbb.t[:, 128 * ni:128 * ni + 128],
                                    src_fm.t[:, 1024 * k + 512 * th:
                                             1024 * k + 512 * th + 512],
                                    start=(k == 0), stop=(k == kchunks - 1)),
                               reads=[wbb, src_fm], writes=[bank],
                               
                               tag=f"{tag}_mm{gi}_{k}_{ni}_{th}")
                for ni, nn in enumerate(grp):
                    for th in range(2):
                        bank = pb[2 * ni + th]
                        dst = dst_fm.t[:, 1024 * nn + 512 * th:1024 * nn + 512 * th + 512]
                        op("scalar", lambda g, bank=bank, dst=dst, nn=nn,
                           bias_t=bias_t: g.activation(
                                dst, bank.t[:, :], AF.Relu, bias=bias_t.t[:, nn:nn + 1]),
                           reads=[bank, bias_t], writes=[dst_fm],
                           tag=f"{tag}_ev{gi}_{ni}_{th}")

        ffn_layer(qfm_a, HC, W1T, D1, b1t, h1, "f1")
        if want_tap("h1"):
            dma("sync", "tpB", tap_aps["h1"], h1.t[:, :], reads=[h1], tag="tp_h1")
        ffn_layer(h1, 32, W2T, D1, b2t, h2, "f2")
        if want_tap("h2"):
            dma("sync", "tpC", tap_aps["h2"], h2.t[:, :], reads=[h2], tag="tp_h2")

        pQf = pQ.t[:, :].bitcast(F32)
        for nb in range(4):
            for k in range(32):
                w3fb = w3f[k % 4]
                w3bb = w3b[k % 4]
                dma("sync" if k % 2 == 0 else "scalar", f"w3{k % 4}", w3fb.t[:, :],
                    W3T[128 * k:128 * k + 128, 512 * nb:512 * nb + 512],
                    reads=[w3fb], writes=[w3fb], tag=f"f3w{nb}_{k}")
                op("vector", lambda g, w3fb=w3fb, w3bb=w3bb: g.tensor_copy(
                        w3bb.t[:, :], w3fb.t[:, :]),
                   reads=[w3fb], writes=[w3bb], tag=f"f3c{nb}_{k}")
                for jj in range(8):
                    bank_ap = pb[jj].t[:, :] if jj < 7 else pQf
                    bank_buf = pb[jj] if jj < 7 else pQ
                    if k == 0:
                        op("tensor", lambda g, nb=nb, bank_ap=bank_ap: g.matmul(
                                bank_ap, c_onesr.t[0:1, :],
                                b3row.t[0:1, 512 * nb:512 * nb + 512],
                                start=True, stop=False),
                           reads=[c_onesr, b3row], writes=[bank_buf],
                           tag=f"f3b{nb}_{jj}")
                    op("tensor", lambda g, k=k, jj=jj, bank_ap=bank_ap,
                       w3bb=w3bb: g.matmul(
                            bank_ap,
                            h2.t[:, 1024 * k + 128 * jj:1024 * k + 128 * jj + 128],
                            w3bb.t[:, :],
                            start=False, stop=(k == 31)),
                       reads=[h2, w3bb], writes=[bank_buf],
                       tag=f"f3m{nb}_{k}_{jj}")
            for jj in range(8):
                bank_ap = pb[jj].t[:, :] if jj < 7 else pQf
                bank_buf = pb[jj] if jj < 7 else pQ
                xs = x1s[jj % 2]
                ob = outsb[jj % 2]
                dma("sync", f"xr{jj % 2}", xs.t[:, :],
                    x1_dram.t[128 * jj:128 * jj + 128, 512 * nb:512 * nb + 512],
                    reads=[x1_dram, xs], writes=[xs], tag=f"f3x{nb}_{jj}")
                op("vector", lambda g, bank_ap=bank_ap, xs=xs, ob=ob: g.tensor_tensor(
                        ob.t[:, :], bank_ap, xs.t[:, :], A.add),
                   reads=[bank_buf, xs], writes=[ob], tag=f"f3o{nb}_{jj}")
                dma("sync", f"out{jj % 2}",
                    out_own[128 * jj:128 * jj + 128, 512 * nb:512 * nb + 512],
                    ob.t[:, :], reads=[ob], tag=f"f3s{nb}_{jj}")

        return finish()



def _consts(core):
    r = core
    ident_bf = np.eye(128, dtype=ml_dtypes.bfloat16)
    ident_f = np.eye(128, dtype=np.float32)
    ones_col = np.ones((128, 1), np.float32)
    ones_row = np.ones((1, 128), np.float32)
    rows = np.arange(64)
    selall = (rows[:, None] % 8 == np.arange(8)[None, :]).astype(np.float32)
    selgt_own = -((rows[:, None] % 8 == np.arange(8)[None, :]) &
                  (rows[:, None] // 8 > r)).astype(np.float32)
    selallE = np.repeat((rows % 8 == r).astype(np.float32)[:, None], 8, axis=1)
    selgtnegE = -(((rows[:, None] % 8) == r) &
                  (rows[:, None] // 8 > np.arange(8)[None, :])).astype(np.float32)
    selidE = (rows[:, None] == 8 * np.arange(8)[None, :] + r).astype(np.float32)
    eoff = np.repeat((np.arange(8) * CAP).astype(np.float32)[None, :], 128, axis=0)
    iota = np.zeros((128, 64), np.int32)
    for rr in range(8):
        for j in range(8):
            iota[:, 8 * rr + j] = 8 * (128 * j + np.arange(128)) + rr
    return dict(ident_bf=ident_bf, ident_f=ident_f, ones_col=ones_col,
                ones_row=ones_row, selall=selall, selgt_own=selgt_own,
                selallE=selallE, selgtnegE=selgtnegE, selidE=selidE,
                eoff_rep=eoff, iota_rm=iota)


def make_in_maps(inputs):
    x = np.ascontiguousarray(np.asarray(inputs["L0_attn_x"], dtype=np.float32)[0])
    W1T = np.ascontiguousarray(np.asarray(inputs["L0_ffn_W1"], np.float32).T)
    W2T = np.ascontiguousarray(np.asarray(inputs["L0_ffn_W2"], np.float32).T)
    W3T = np.ascontiguousarray(np.asarray(inputs["L0_ffn_W3"], np.float32).T)
    b1m = np.ascontiguousarray(np.asarray(inputs["L0_ffn_b1"], np.float32).reshape(32, 128))
    b2m = np.ascontiguousarray(np.asarray(inputs["L0_ffn_b2"], np.float32).reshape(32, 128))
    b3r = np.ascontiguousarray(np.asarray(inputs["L0_ffn_b3"], np.float32).reshape(1, H))
    sWg = np.ascontiguousarray(np.asarray(inputs["L0_attn_sWg"], np.float32))
    sWd = np.ascontiguousarray(np.asarray(inputs["L0_attn_sWd"], np.float32))
    Wg = np.asarray(inputs["L0_attn_Wg"], np.float32)
    Wd = np.asarray(inputs["L0_attn_Wd"], np.float32)
    in_maps = []
    for r in range(NCORE):
        cm = _consts(r)
        m = dict(
            x_own=np.ascontiguousarray(x[r::8, :]),
            x_full=x,
            W1T=W1T, W2T=W2T, W3T=W3T, b1m=b1m, b2m=b2m, b3r=b3r,
            sWg=sWg, sWd=sWd,
            Wg_e=np.ascontiguousarray(Wg[r]),
            Wd_e=np.ascontiguousarray(Wd[r]),
            sNg_r=np.ascontiguousarray(np.asarray(inputs["L0_attn_sNg"], np.float32).reshape(1, H)),
            sNd_r=np.ascontiguousarray(np.asarray(inputs["L0_attn_sNd"], np.float32).reshape(1, II)),
            Ng_r=np.ascontiguousarray(np.asarray(inputs["L0_attn_Ng"], np.float32)[r].reshape(1, H)),
            Nd_r=np.ascontiguousarray(np.asarray(inputs["L0_attn_Nd"], np.float32)[r].reshape(1, II)),
            gnw_r=np.ascontiguousarray(np.asarray(inputs["L0_attn_gate_norm_w"], np.float32).reshape(1, H)),
            Wgate=np.ascontiguousarray(np.asarray(inputs["L0_attn_W_gate"], np.float32)),
            Wsg=np.ascontiguousarray(np.asarray(inputs["L0_attn_W_shared_gate"], np.float32)),
            **cm,
        )
        in_maps.append(m)
    return in_maps


def kernel(**inputs):
    nc = build_program()
    in_maps = make_in_maps(inputs)
    res = run_bass_kernel_spmd(nc, in_maps, core_ids=list(range(NCORE)))
    out = np.empty((T, H), np.float32)
    for r in range(NCORE):
        out[r::8, :] = res.results[r]["out_own"]
    return out.reshape(1, T, H)
